# version 10
# baseline (speedup 1.0000x reference)
import contextlib

import numpy as np
import ml_dtypes

import concourse.bass as bass
import concourse.mybir as mybir
from concourse.bass_utils import run_bass_kernel_spmd

F32 = mybir.dt.float32
BF16 = mybir.dt.bfloat16
FP8 = mybir.dt.float8e4
I32 = mybir.dt.int32
AF = mybir.ActivationFunctionType
ALU = mybir.AluOpType
nf32 = np.float32

T, D, H, DH, SBK, II = 8192, 2048, 16, 128, 512, 6144
NCORES = 8
TOK = 1024
NCH = D // 128
ICH = II // 128
CEXP = -24.0
QSCALE = float(1.0 / np.sqrt(DH))

DEBUG = []
STOP = None

ENGS = ("sy", "gp", "te", "ve", "se")


class Sched:

    def __init__(self):
        self.ops = {e: [] for e in ENGS}
        import collections
        self.cnt = collections.defaultdict(int)

    def op(self, eng, fn, waits=(), inc=None, inc_amt=1):
        waits = [w for w in waits if w is not None]
        self.ops[eng].append((tuple(waits), (inc, inc_amt), fn))
        if inc is not None:
            self.cnt[inc] += inc_amt
            return self.cnt[inc]
        return None

    def opsw(self, eng, fn, waits=(), n_incs=1):
        waits = [w for w in waits if w is not None]
        base = self.cnt[eng]
        self.cnt[eng] += n_incs
        self.ops[eng].append((tuple(waits), (None, 0),
                              (lambda fn=fn, base=base: lambda h: fn(h, base))()))
        return self.cnt[eng]

    def opn(self, eng, fn, waits=(), bump=None, amt=0):
        waits = [w for w in waits if w is not None]
        self.ops[eng].append((tuple(waits), (None, 0), fn))
        self.cnt[bump] += amt
        return self.cnt[bump]


def _finish(nc, es, sch, sems, dbg, **kw):
    W_ = kw["W_"]
    W_("sy", lambda h: None, waits=[("sy", sch.cnt["sy"])])
    W_("gp", lambda h: None, waits=[("gp", sch.cnt["gp"])])
    block = es.enter_context(nc.Block())

    def emit(eng_name, h):
        for waits, (inc, amt), fn in sch.ops[eng_name]:
            for s, v in waits:
                h.wait_ge(sems[s], v)
            inst = fn(h)
            if inc is not None:
                assert inst is not None
                inst.then_inc(sems[inc], amt)

    @block.sync
    def _(h):
        emit("sy", h)

    @block.gpsimd
    def _(h):
        emit("gp", h)

    @block.tensor
    def _(h):
        emit("te", h)

    @block.vector
    def _(h):
        emit("ve", h)

    @block.scalar
    def _(h):
        emit("se", h)
    return nc


def ap_of(t, offset, dims):
    return bass.AP(t, offset, [list(d) for d in dims])


def dynreg(h):
    if not hasattr(h, "_dynreg"):
        h._dynreg = h.alloc_register("dynoff")
    return h._dynreg


def build():
    nc = bass.Bass("TRN2", target_bir_lowering=False, detect_race_conditions=False)

    x_own = nc.dram_tensor("x_own", [TOK, D], F32, kind="ExternalInput")
    Wq = nc.dram_tensor("Wq", [NCH, 128, NCH, 128], F32, kind="ExternalInput")
    Wk = nc.dram_tensor("Wk", [NCH, 128, NCH, 128], F32, kind="ExternalInput")
    Wv = nc.dram_tensor("Wv", [D, D], F32, kind="ExternalInput")
    Wo = nc.dram_tensor("Wo", [NCH, 128, NCH, 128], F32, kind="ExternalInput")
    gw = nc.dram_tensor("gw", [ICH, 128, NCH, 128], F32, kind="ExternalInput")
    uw = nc.dram_tensor("uw", [ICH, 128, NCH, 128], F32, kind="ExternalInput")
    dw = nc.dram_tensor("dw", [NCH, 128, ICH, 128], F32, kind="ExternalInput")
    bvecs_d = nc.dram_tensor("bvecs", [128, 48], F32, kind="ExternalInput")
    bvo_d = nc.dram_tensor("bvo", [128, 2], F32, kind="ExternalInput")
    scales_d = nc.dram_tensor("scales", [128, 112], F32, kind="ExternalInput")
    cb_d = nc.dram_tensor("constsb", [128, 2304], BF16, kind="ExternalInput")
    cf_d = nc.dram_tensor("constsf", [128, 128], F32, kind="ExternalInput")
    offs_d = nc.dram_tensor("offs", [1, 24], I32, kind="ExternalInput")
    out_own = nc.dram_tensor("out_own", [TOK, D], F32, kind="ExternalOutput")

    kvb = nc.dram_tensor("kvb", [2, D, TOK], BF16)
    kvg = nc.dram_tensor("kvg", [2 * NCORES, D, TOK], BF16)
    afb = nc.dram_tensor("afb", [256, T], BF16)
    afg = nc.dram_tensor("afg", [D, T], BF16)

    dbg = {}
    for name, shp, dt in [("ws3", [128, 6144], FP8),
                          ("fuse", [128, ICH * SBK], BF16),
                          ("gtf", [128, 512], F32),
                          ("xnT", [128, NCH * TOK], BF16),
                          ("qtT", [128, NCH * TOK], BF16),
                          ("kvg", [2 * NCORES, D, TOK], BF16),
                          ("ffnA", [128, NCH * SBK], F32),
                          ("afg", [D, T], BF16)]:
        if name in DEBUG:
            dbg[name] = nc.dram_tensor("dbg_" + name, shp, dt, kind="ExternalOutput")

    es = contextlib.ExitStack()
    with es:
        S = lambda n, shp, d: es.enter_context(nc.sbuf_tensor(n, shp, d))
        P = lambda n, shp, d: es.enter_context(nc.psum_tensor(n, shp, d))

        xnT = S("xnT", [128, NCH * TOK], BF16)
        qtT = S("qtT", [128, NCH * TOK], BF16)
        ffnA = S("ffnA", [128, NCH * SBK], F32)
        BIG = S("BIG", [128, 65536], FP8)
        ws = [S(f"ws{i}", [128, 6144], FP8) for i in range(4)]
        cb = S("cb", [128, 2304], BF16)
        cf = S("cf", [128, 128], F32)
        bvecs = S("bvecs_s", [128, 48], F32)
        bvo = S("bvo_s", [128, 2], F32)
        scales = S("scales_s", [128, 112], F32)
        stats = S("stats", [128, 24], F32)
        ofs = S("ofs", [1, 24], I32)
        mv = S("mv", [128, 2], F32)
        sd = S("sd", [128, 1], F32)
        rinv = S("rinv", [128, 1], F32)
        qslot = [S(f"qslot{i}", [128, 512], BF16) for i in range(2)]
        ptile = [S(f"ptile{i}", [128, 512], BF16) for i in range(4)]
        lrep = S("lrep", [128, 512], F32)
        linv = S("linv", [128, 512], F32)
        otf = lrep
        ostg = [S(f"ostg{i}", [128, 512], BF16) for i in range(2)]
        kstg = ostg
        gtf = [S("gtf0", [128, 512], F32)] * 2
        gsil = [S("gsil0", [128, 512], F32)] * 2
        dbg_psu = None

        pb = [P(f"pb{i}", [128, 512], F32) for i in range(8)]
        pTv = pb[7].ap().bitcast(BF16)

        BIGb = BIG.ap().bitcast(BF16)
        BIGf = BIG.ap().bitcast(F32)
        ffnB = qtT.ap().bitcast(F32)
        identb = cb.ap()[:, 0:128]
        onesb = cb.ap()[:, 128:256]
        maskb = cb.ap()[:, 256:2304]

        sem_names = list(ENGS) + ["cc", "cs", "xa", "xb", "kst0", "kst1", "vs8",
                                  "ost0", "ost1", "fx0", "fx1", "fst0", "fst1",
                                  "ws0", "ws1", "ws2", "ws3", "wv", "kv0", "kv1",
                                  "ofs", "opm", "dbg"]
        sems = {e: es.enter_context(nc.semaphore(f"sem_{e}")) for e in sem_names}

        sch = Sched()
        W_ = sch.op

        W_("sy", lambda h: h.dma_start(out=cb[:], in_=cb_d.ap()), inc="cs", inc_amt=16)
        W_("sy", lambda h: h.dma_start(out=cf[:], in_=cf_d.ap()), inc="cs", inc_amt=16)
        W_("sy", lambda h: h.dma_start(out=bvecs[:], in_=bvecs_d.ap()), inc="cs", inc_amt=16)
        W_("sy", lambda h: h.dma_start(out=bvo[:], in_=bvo_d.ap()), inc="cs", inc_amt=16)
        ck_consts = W_("sy", lambda h: h.dma_start(out=scales[:], in_=scales_d.ap()),
                       inc="cs", inc_amt=16)

        xt_ap = lambda b: BIGf[:, b * 2048:(b + 1) * 2048]
        xnb_ap = lambda b: BIGb[:, 8192 + b * 2048: 8192 + (b + 1) * 2048]

        ck_xnb = [0] * 8
        ck_tr_tile = [0] * 8
        last_evac = 0
        for t8 in range(8):
            b = t8 % 2
            xsem = "xa" if b == 0 else "xb"
            waits = [("ve", ck_xnb[t8 - 2])] if t8 >= 2 else [("cs", ck_consts)]
            ck_xl = W_("sy", (lambda t8=t8, b=b: lambda h: h.dma_start(
                out=xt_ap(b), in_=x_own.ap()[t8 * 128:(t8 + 1) * 128, :]))(),
                waits=waits, inc=xsem, inc_amt=16)

            def ve_ln(h, base, b=b):
                r = None
                for i in range(4):
                    r = h.bn_stats(stats[:, 6 * i:6 * i + 6], xt_ap(b)[:, 512 * i:512 * i + 512])
                r.then_inc(sems["ve"], 1)
                h.wait_ge(sems["ve"], base + 1)
                h.bn_aggr(mv[:], stats[:]).then_inc(sems["ve"], 1)
                h.wait_ge(sems["ve"], base + 2)
                h.tensor_scalar_add(sd[:], mv[:, 1:2], 1e-5).then_inc(sems["ve"], 1)
            ck_sd = sch.opsw("ve", ve_ln, waits=[(xsem, ck_xl)], n_incs=3)
            ck_sq = W_("se", lambda h: h.activation(out=sd[:], in_=sd[:], func=AF.Sqrt),
                       waits=[("ve", ck_sd)], inc="se")

            def ve_norm(h, base, b=b):
                h.reciprocal(rinv[:], sd[:]).then_inc(sems["ve"], 1)
                h.wait_ge(sems["ve"], base + 1)
                h.tensor_scalar(out=xnb_ap(b), in0=xt_ap(b),
                                scalar1=mv[:, 0:1], scalar2=rinv[:],
                                op0=ALU.subtract, op1=ALU.mult).then_inc(sems["ve"], 1)
            w_n = [("se", ck_sq)]
            if t8 >= 2:
                w_n.append(("te", ck_tr_tile[t8 - 2]))
            ck_xnb[t8] = sch.opsw("ve", ve_norm, waits=w_n, n_incs=2)

            for half in range(2):
                waits = [("ve", ck_xnb[t8])]
                if t8 == 0 and half == 0:
                    waits.append(("cs", ck_consts))
                if last_evac:
                    waits.append(("se", last_evac))

                def te_tr(h, b=b, half=half):
                    r = None
                    for c8 in range(8):
                        ch = half * 8 + c8
                        r = h.transpose(pTv[:, c8 * 128:(c8 + 1) * 128],
                                        xnb_ap(b)[:, ch * 128:(ch + 1) * 128], identb)
                    return r
                ck_tr = W_("te", te_tr, waits=waits, inc="te")
                ck_tr_tile[t8] = ck_tr

                def se_evac(h, t8=t8, half=half):
                    dst = ap_of(xnT, (half * 8) * TOK + t8 * 128,
                                [[NCH * TOK, 128], [TOK, 8], [1, 128]])
                    return h.copy(out=dst,
                                  in_=pTv[:, 0:1024].rearrange("p (a b) -> p a b", a=8))
                last_evac = W_("se", se_evac, waits=[("te", ck_tr)], inc="se")

        ck_xnT = last_evac
        if STOP == "A":
            return _finish(nc, es, sch, sems, dbg, xnT=xnT, qtT=qtT, ffnA=ffnA,
                           kvg=kvg, afg=afg, W_=W_, ck={"xnT": ck_xnT})

        wsb = lambda i: ws[i].ap().bitcast(BF16)

        def w_attn_load(h, Wt, ch, wi):
            src = ap_of(Wt, ch * 128 * D, [[D, 128], [1, D]])
            h.dma_start(out=wsb(wi)[:, 0:2048], in_=src).then_inc(sems[f"ws{wi}"], 16)
            return None

        ck_g = {}
        ck_ev = {}
        gidx = 0
        ck_w = {}
        for pi, (Wt, bcol, kind) in enumerate([(Wq, 0, "q"), (Wk, 16, "k")]):
            for ch in range(NCH):
                ci = pi * NCH + ch
                wi = ci % 2
                waits = [("te", ck_g[ci - 2])] if ci >= 2 else []
                ck_w[ci] = sch.opn("gp", (lambda Wt=Wt, ch=ch, wi=wi:
                                          lambda h: w_attn_load(h, Wt, ch, wi))(),
                                   waits=waits, bump=f"ws{wi}", amt=16)
                for half2 in range(2):
                    par = gidx % 2
                    waits = [(f"ws{wi}", ck_w[ci])]
                    if gidx == 0:
                        waits.append(("se", ck_xnT))
                        waits.append(("cs", ck_consts))
                    if gidx >= 2:
                        waits.append(("ve", ck_ev[gidx - 2]))

                    def te_proj(h, wi=wi, half2=half2, par=par):
                        r = None
                        for dch in range(NCH):
                            r = h.matmul(
                                pb[par][:], wsb(wi)[:, dch * 128:(dch + 1) * 128],
                                xnT[:, dch * TOK + half2 * 512: dch * TOK + half2 * 512 + 512],
                                start=(dch == 0), stop=(dch == NCH - 1))
                        return r
                    ck_g[ci] = W_("te", te_proj, waits=waits, inc="te")

                    if kind == "q":
                        def ve_evq(h, ch=ch, half2=half2, par=par):
                            return h.tensor_scalar(
                                out=qtT[:, ch * TOK + half2 * 512: ch * TOK + half2 * 512 + 512],
                                in0=pb[par][:], scalar1=bvecs[:, ch:ch + 1],
                                scalar2=None, op0=ALU.add)
                        ck_ev[gidx] = W_("ve", ve_evq, waits=[("te", ck_g[ci])], inc="ve")
                    else:
                        kb = gidx % 2
                        waits_e = [("te", ck_g[ci])]
                        if (gidx - 2, "kst") in ck_ev:
                            waits_e.append((f"kst{kb}", ck_ev[(gidx - 2, "kst")]))

                        def ve_evk(h, ch=ch, par=par, kb=kb):
                            return h.tensor_scalar(
                                out=kstg[kb][:], in0=pb[par][:],
                                scalar1=bvecs[:, 16 + ch:16 + ch + 1],
                                scalar2=None, op0=ALU.add)
                        ck_ev[gidx] = W_("ve", ve_evk, waits=waits_e, inc="ve")

                        def sy_kst(h, ch=ch, half2=half2, kb=kb):
                            dst = ap_of(kvb, ch * 128 * TOK + half2 * 512,
                                        [[TOK, 128], [1, 512]])
                            return h.dma_start(out=dst, in_=kstg[kb][:])
                        ck_ev[(gidx, "kst")] = W_("sy", sy_kst,
                                                  waits=[("ve", ck_ev[gidx])],
                                                  inc=f"kst{kb}", inc_amt=16)
                    gidx += 1
        ck_kst_fin = {0: sch.cnt["kst0"], 1: sch.cnt["kst1"]}
        if "qtT" in dbg:
            W_("sy", lambda h: h.dma_start(out=dbg["qtT"].ap(), in_=qtT[:]),
               waits=[("ve", ck_ev[31])], inc="dbg", inc_amt=16)

        wv_ap = lambda dch: BIGb[:, dch * 1024:(dch + 1) * 1024]
        vfull_ap = lambda t8: BIGb[:, 16384 + t8 * 2048: 16384 + (t8 + 1) * 2048]

        ck_wv = {}
        ck_v_h0_last = 0
        vgi = 0
        for vhalf in range(2):
            for dch in range(NCH):
                waits = []
                if vhalf == 0 and dch == 0:
                    waits.append(("se", ck_xnT))
                if vhalf == 1:
                    waits.append(("te", ck_v_h0_last))

                def gp_wv(h, dch=dch, vhalf=vhalf):
                    src = ap_of(Wv, dch * 128 * D + vhalf * 1024,
                                [[D, 128], [1, 1024]])
                    return h.dma_start(out=wv_ap(dch), in_=src)
                ck_wv[(vhalf, dch)] = W_("gp", gp_wv, waits=waits, inc="wv", inc_amt=16)
            for t8 in range(8):
                for g2 in range(2):
                    par = vgi % 2
                    waits = [("wv", ck_wv[(vhalf, NCH - 1)])]
                    if vgi >= 2:
                        waits.append(("ve", ck_ev[("v", vgi - 2)]))
                    else:
                        waits.append(("ve", ck_ev[gidx - 1]))

                    def te_v(h, t8=t8, g2=g2, par=par):
                        r = None
                        for dch in range(NCH):
                            r = h.matmul(
                                pb[par][:],
                                xnT[:, dch * TOK + t8 * 128: dch * TOK + t8 * 128 + 128],
                                wv_ap(dch)[:, g2 * 512:(g2 + 1) * 512],
                                start=(dch == 0), stop=(dch == NCH - 1))
                        return r
                    ck_te_v = W_("te", te_v, waits=waits, inc="te")
                    if vhalf == 0:
                        ck_v_h0_last = ck_te_v

                    def ve_evv(h, t8=t8, g2=g2, vhalf=vhalf, par=par):
                        return h.tensor_scalar(
                            out=vfull_ap(t8)[:, vhalf * 1024 + g2 * 512:
                                             vhalf * 1024 + g2 * 512 + 512],
                            in0=pb[par][:], scalar1=1.0, scalar2=None, op0=ALU.mult)
                    ck_ev[("v", vgi)] = W_("ve", ve_evv, waits=[("te", ck_te_v)], inc="ve")
                    vgi += 1

        ck_vst = 0
        for t8 in range(8):
            def sy_vst(h, t8=t8):
                dst = ap_of(kvb, D * TOK + t8 * 128 * D, [[D, 128], [1, D]])
                return h.dma_start(out=dst, in_=vfull_ap(t8))
            ck_vst = W_("sy", sy_vst, waits=[("ve", ck_ev[("v", 16 + t8 * 2 + 1)])],
                        inc="vs8", inc_amt=16)

        ck_ag1 = W_("gp", lambda h: h.collective_compute(
            "AllGather", ALU.bypass, replica_groups=[list(range(NCORES))],
            ins=[kvb.ap().opt()], outs=[kvg.ap().opt()]),
            waits=[("vs8", ck_vst), ("kst0", ck_kst_fin[0]), ("kst1", ck_kst_fin[1])],
            inc="cc")

        if STOP == "B":
            if "qtT" in dbg:
                pass
            if "kvg" in dbg:
                W_("gp", lambda h: h.dma_start(out=dbg["kvg"].ap(), in_=kvg.ap()),
                   waits=[("cc", ck_ag1)], inc="gp", inc_amt=16)
            return _finish(nc, es, sch, sems, dbg, xnT=xnT, qtT=qtT, ffnA=ffnA,
                           kvg=kvg, afg=afg, W_=W_, ck={})

        fuse_ap = lambda ich: BIGb[:, ich * 512:(ich + 1) * 512]

        def build_ffn(hb, out_slice, first_waits, tag):
            ck_wg = {}
            last_down_te = 0
            for ich in range(ICH):
                for gu, Wt in ((0, gw), (1, uw)):
                    wi = (ich % 2) * 2 + gu
                    waits = list(first_waits) if (ich == 0 and gu == 0) else []
                    if ich >= 2:
                        waits.append(("ve", ck_ev[(tag, "fuse", ich - 2)]))

                    def gp_w(h, Wt=Wt, ich=ich, wi=wi):
                        src = ap_of(Wt, ich * 128 * D, [[D, 128], [1, D]])
                        h.dma_start(out=ws[wi].ap()[:, 0:2048],
                                    in_=src).then_inc(sems[f"ws{wi}"], 16)
                        return None
                    ck_wg[(ich, gu)] = sch.opn("gp", gp_w, waits=waits, bump=f"ws{wi}", amt=16)

                parG, parU = (ich % 2), 2 + (ich % 2)
                wiG, wiU = (ich % 2) * 2, (ich % 2) * 2 + 1
                waits = [(f"ws{wiG}", ck_wg[(ich, 0)]), (f"ws{wiU}", ck_wg[(ich, 1)])]
                if ich >= 2:
                    waits.append(("ve", ck_ev[(tag, "gtf", ich - 2)]))
                    waits.append(("ve", ck_ev[(tag, "fuse", ich - 2)]))

                def te_gu(h, ich=ich, hb=hb, parG=parG, parU=parU):
                    wiG, wiU = (ich % 2) * 2, (ich % 2) * 2 + 1
                    r = None
                    for dch in range(NCH):
                        h.matmul(pb[parG][:], ws[wiG].ap()[:, dch * 128:(dch + 1) * 128],
                                 xnT[:, dch * TOK + hb * 512: dch * TOK + hb * 512 + 512],
                                 start=(dch == 0), stop=(dch == NCH - 1))
                    for dch in range(NCH):
                        r = h.matmul(pb[parU][:], ws[wiU].ap()[:, dch * 128:(dch + 1) * 128],
                                     xnT[:, dch * TOK + hb * 512: dch * TOK + hb * 512 + 512],
                                     start=(dch == 0), stop=(dch == NCH - 1))
                    return r
                ck_teg = W_("te", te_gu, waits=waits, inc="te")



                gb = ich % 2

                def ve_g(h, ich=ich, parG=parG, gb=gb):
                    return h.tensor_scalar(out=gtf[gb][:], in0=pb[parG][:],
                                           scalar1=scales[:, ich:ich + 1], scalar2=None,
                                           op0=ALU.mult)
                wv_g = [("te", ck_teg)]
                if ich >= 1:
                    wv_g.append(("se", ck_ev[(tag, "sil", ich - 1)]))
                ck_ev[(tag, "gtf", ich)] = W_("ve", ve_g, waits=wv_g, inc="ve")

                ws_s = [("ve", ck_ev[(tag, "gtf", ich)])]
                if ich >= 1:
                    ws_s.append(("ve", ck_ev[(tag, "fuse", ich - 1)]))
                ck_sil = W_("se", (lambda gb=gb: lambda h: h.activation(
                    out=gsil[gb][:], in_=gtf[gb][:], func=AF.Silu))(),
                    waits=ws_s, inc="se")
                ck_ev[(tag, "sil", ich)] = ck_sil

                import os as _os
                _fv = _os.environ.get("FUSEVAR", "")

                def ve_fuse(h, ich=ich, parU=parU, gb=gb, _fv=_fv):
                    if _fv == "psu":
                        return h.tensor_scalar(out=fuse_ap(ich), in0=pb[parU][:],
                                               scalar1=scales[:, 48 + ich:48 + ich + 1],
                                               scalar2=None, op0=ALU.mult)
                    if _fv == "gsil":
                        return h.tensor_scalar(out=fuse_ap(ich), in0=gsil[gb][:],
                                               scalar1=1.0, scalar2=None, op0=ALU.mult)
                    return h.scalar_tensor_tensor(out=fuse_ap(ich), in0=pb[parU][:],
                                                  scalar=scales[:, 48 + ich:48 + ich + 1],
                                                  in1=gsil[gb][:], op0=ALU.mult, op1=ALU.mult)
                ck_ev[(tag, "fuse", ich)] = W_("ve", ve_fuse, waits=[("se", ck_sil)], inc="ve")

            for dm in range(NCH):
                wi = dm % 2
                waits = [("ve", ck_ev[(tag, "fuse", ICH - 1)])] if dm < 2 else []
                if dm >= 2:
                    waits.append(("te", ck_ev[(tag, "dte", dm - 2)]))

                def gp_wd(h, dm=dm, wi=wi):
                    src = ap_of(dw, dm * 128 * II, [[II, 128], [1, II]])
                    h.dma_start(out=ws[wi].ap()[:, 0:6144],
                                in_=src).then_inc(sems[f"ws{wi}"], 16)
                    return None
                ck_wd = sch.opn("gp", gp_wd, waits=waits, bump=f"ws{wi}", amt=16)

                par = dm % 2
                waits = [(f"ws{wi}", ck_wd)]
                if dm >= 2:
                    waits.append(("ve", ck_ev[(tag, "down", dm - 2)]))

                def te_d(h, dm=dm, par=par, wi=wi):
                    r = None
                    for ick in range(ICH):
                        r = h.matmul(pb[par][:], ws[wi].ap()[:, ick * 128:(ick + 1) * 128],
                                     fuse_ap(ick), start=(ick == 0), stop=(ick == ICH - 1))
                    return r
                ck_ev[(tag, "dte", dm)] = W_("te", te_d, waits=waits, inc="te")

                def ve_d(h, dm=dm, par=par):
                    return h.tensor_scalar(out=out_slice(dm), in0=pb[par][:],
                                           scalar1=scales[:, 96 + dm:96 + dm + 1],
                                           scalar2=None, op0=ALU.mult)
                ck_ev[(tag, "down", dm)] = W_("ve", ve_d,
                                              waits=[("te", ck_ev[(tag, "dte", dm)])],
                                              inc="ve")
            return ck_ev[(tag, "down", NCH - 1)]

        ck_ffnA = build_ffn(0, lambda dm: ffnA[:, dm * 512:(dm + 1) * 512],
                            [("vs8", ck_vst), ("ve", ck_ev[("v", 31)])], "fA")
        if "ffnA" in dbg:
            W_("sy", lambda h: h.dma_start(out=dbg["ffnA"].ap(), in_=ffnA[:]),
               waits=[("ve", ck_ffnA)], inc="dbg", inc_amt=16)
        ck_dbg_fuse = None
        if "fuse" in dbg:
            ck_dbg_fuse = W_("sy", lambda h: h.dma_start(out=dbg["fuse"].ap(),
                                                         in_=BIGb[:, 0:ICH * 512]),
                             waits=[("ve", ck_ev[("fA", "fuse", ICH - 1)])], inc="dbg", inc_amt=16)


        if STOP == "FA":
            return _finish(nc, es, sch, sems, dbg, xnT=xnT, qtT=qtT, ffnA=ffnA,
                           kvg=kvg, afg=afg, W_=W_, ck={})

        kh_ap = lambda hl: BIGb[:, hl * 16384: hl * 16384 + 8192]
        vh_ap = lambda hl: BIGb[:, hl * 16384 + 8192: hl * 16384 + 16384]

        ck_kv = {}
        ck_ofs = W_("gp", lambda h: h.dma_start(out=ofs[:], in_=offs_d.ap()),
                    inc="ofs", inc_amt=16)
        for hl in range(2):
            waits = [("cc", ck_ag1), ("ofs", ck_ofs)]
            if hl == 0:
                waits.append(("ve", ck_ffnA))
                if "fuse" in dbg:
                    waits.append(("dbg", ck_dbg_fuse))

            def gp_kv(h, hl=hl):
                rk = dynreg(h)
                h.reg_load(rk, ofs[0:1, hl:hl + 1])
                src = bass.AP(kvg, rk,
                              [[TOK, 128], [2 * D * TOK, NCORES], [1, TOK]])
                h.dma_start(out=kh_ap(hl), in_=src).then_inc(sems[f"kv{hl}"], 16)
                for r in range(NCORES):
                    h.reg_load(rk, ofs[0:1, 4 + hl * 8 + r:5 + hl * 8 + r])
                    src = bass.AP(kvg, rk, [[D, 128], [128 * D, 8], [1, 128]])
                    h.dma_start(out=vh_ap(hl)[:, r * 1024:(r + 1) * 1024],
                                in_=src).then_inc(sems[f"kv{hl}"], 16)
                return None
            ck_kv[hl] = sch.opn("gp", gp_kv, waits=waits, bump=f"kv{hl}", amt=16 * 9)

        psL = {0: pb[6], 1: pb[7]}
        ck_qs = {}
        ck_exp = {}
        ck_lmm = {}
        ck_oev = {}
        last_s_te = {}
        for C in range(NCH):
            nst = 4 * (C + 1)
            for hh in range(2):
                waits = [(f"kv{hh}", ck_kv[hh])]
                if C == 0:
                    waits.append(("ve", ck_ffnA))
                else:
                    waits.append(("te", last_s_te[hh]))

                def ve_qs(h, hh=hh, C=C):
                    srcap = ap_of(qtT, hh * 512 + 32 * C,
                                  [[NCH * TOK, 128], [1, 32], [TOK, 16]])
                    return h.tensor_copy(qslot[hh][:], srcap)
                ck_qs[hh] = W_("ve", ve_qs, waits=waits, inc="ve")

            for st in range(nst):
                for hh in range(2):
                    par_s = st % 2
                    pt_b = 2 * hh + st % 2
                    diag = st >= 4 * C

                    waits = [("ve", ck_qs[hh])]
                    if st >= 2:
                        waits.append(("se", ck_exp[(hh, st - 2)]))
                    elif C >= 1:
                        waits.append(("se", ck_exp[(hh, "last")]))
                    elif st < 2:
                        waits.append(("ve", ck_ffnA))

                    def te_s(h, hh=hh, st=st, par_s=par_s):
                        return h.matmul(pb[2 * hh + par_s][:],
                                        kh_ap(hh)[:, st * 128:(st + 1) * 128],
                                        qslot[hh][:], start=True, stop=True)
                    ck_s = W_("te", te_s, waits=waits, inc="te")
                    if st == nst - 1:
                        last_s_te[hh] = ck_s

                    w_e = [("te", ck_s)]
                    if st >= 2:
                        w_e.append(("te", ck_lmm[(hh, st - 2)]))
                    elif C >= 1:
                        w_e.append(("te", ck_lmm[(hh, "last")]))

                    def se_exp(h, hh=hh, pt_b=pt_b, par_s=par_s):
                        return h.activation(out=ptile[pt_b][:], in_=pb[2 * hh + par_s][:],
                                            func=AF.Exp, bias=nf32(CEXP),
                                            scale=nf32(QSCALE))
                    ck_exp[(hh, st)] = W_("se", se_exp, waits=w_e, inc="se")

                    if diag:
                        def ve_mask(h, st=st, C=C, pt_b=pt_b):
                            mi = st - 4 * C
                            return h.tensor_tensor(out=ptile[pt_b][:], in0=ptile[pt_b][:],
                                                   in1=maskb[:, mi * 512:(mi + 1) * 512],
                                                   op=ALU.mult)
                        ck_m = W_("ve", ve_mask, waits=[("se", ck_exp[(hh, st)])], inc="ve")
                        av_waits = [("ve", ck_m)]
                    else:
                        av_waits = [("se", ck_exp[(hh, st)])]
                    if st == 0 and C >= 1:
                        av_waits.append(("ve", ck_oev[hh]))

                    def te_av(h, hh=hh, st=st, pt_b=pt_b,
                              first=(st == 0), last=(st == nst - 1)):
                        h.matmul(pb[4 + hh][:], vh_ap(hh)[:, st * 128:(st + 1) * 128],
                                 ptile[pt_b][:], start=first, stop=last)
                        return h.matmul(psL[hh][:], onesb, ptile[pt_b][:],
                                        start=first, stop=last)
                    ck_lmm[(hh, st)] = W_("te", te_av, waits=av_waits, inc="te")
            ck_exp[(0, "last")] = ck_exp[(0, nst - 1)]
            ck_exp[(1, "last")] = ck_exp[(1, nst - 1)]
            ck_lmm[(0, "last")] = ck_lmm[(0, nst - 1)]
            ck_lmm[(1, "last")] = ck_lmm[(1, nst - 1)]

            for hh in range(2):
                def ve_oev(h, base, hh=hh):
                    h.tensor_scalar(out=lrep[:], in0=psL[hh][:], scalar1=1.0,
                                    scalar2=None, op0=ALU.mult).then_inc(sems["ve"], 1)
                    h.wait_ge(sems["ve"], base + 1)
                    h.reciprocal(linv[:], lrep[:]).then_inc(sems["ve"], 1)
                    h.wait_ge(sems["ve"], base + 2)
                    h.scalar_tensor_tensor(out=otf[:], in0=pb[4 + hh][:], scalar=1.0,
                                           in1=linv[:], op0=ALU.mult, op1=ALU.mult
                                           ).then_inc(sems["ve"], 1)
                    h.wait_ge(sems["ve"], base + 3)
                    h.tensor_scalar(out=ostg[hh][:], in0=otf[:],
                                    scalar1=bvo[:, hh:hh + 1], scalar2=None,
                                    op0=ALU.add).then_inc(sems["ve"], 1)
                w_o = [("te", ck_lmm[(hh, "last")])]
                if C >= 1:
                    w_o.append((f"ost{hh}", ck_oev[(hh, "st")]))
                ck_oev[hh] = sch.opsw("ve", ve_oev, waits=w_o, n_incs=4)

                def sy_ost(h, hh=hh, C=C):
                    dst = ap_of(afb, hh * 128 * T + C * 512, [[T, 128], [1, 512]])
                    return h.dma_start(out=dst, in_=ostg[hh][:])
                ck_oev[(hh, "st")] = W_("sy", sy_ost, waits=[("ve", ck_oev[hh])],
                                        inc=f"ost{hh}", inc_amt=16)

        last_exp = max(ck_exp[(0, "last")], ck_exp[(1, "last")])
        ost_fin = (sch.cnt["ost0"], sch.cnt["ost1"])

        ck_ffnB = build_ffn(1, lambda dm: ffnB[:, dm * 512:(dm + 1) * 512],
                            [("ve", ck_oev[1]), ("ve", ck_oev[0]),
                             ("te", max(ck_lmm[(0, "last")], ck_lmm[(1, "last")])),
                             ("se", last_exp)], "fB")

        ck_ag2 = W_("gp", lambda h: h.collective_compute(
            "AllGather", ALU.bypass, replica_groups=[list(range(NCORES))],
            ins=[afb.ap().opt()], outs=[afg.ap().opt()]),
            waits=[("ost0", ost_fin[0]), ("ost1", ost_fin[1])], inc="cc")

        if STOP == "FB":
            return _finish(nc, es, sch, sems, dbg, xnT=xnT, qtT=qtT, ffnA=ffnA,
                           kvg=kvg, afg=afg, W_=W_, ck={})

        opm_ap = lambda hd: BIGb[:, hd * 1024:(hd + 1) * 1024]

        def gp_opm(h):
            ro = dynreg(h)
            h.reg_load(ro, ofs[0:1, 20:21])
            return h.dma_start(out=BIGb[:, 0:16384],
                               in_=bass.AP(afg, ro,
                                           [[T, 128], [128 * T, NCH], [1, TOK]]))
        ck_opm = W_("gp", gp_opm,
                    waits=[("cc", ck_ag2), ("ve", ck_ffnB)], inc="opm", inc_amt=16)

        ogi = 0
        ck_og = {}
        ck_oge = {}
        for dm in range(NCH):
            wi = dm % 2
            waits = [("ve", ck_ffnB)] if dm < 2 else [("te", ck_og[dm - 2])]
            ck_wo = sch.opn("gp", (lambda dm=dm, wi=wi:
                                   lambda h: w_attn_load(h, Wo, dm, wi))(),
                            waits=waits, bump=f"ws{wi}", amt=16)
            for half2 in range(2):
                par = ogi % 2
                waits = [(f"ws{wi}", ck_wo), ("opm", ck_opm)]
                if ogi >= 2:
                    waits.append(("ve", ck_oge[ogi - 2]))

                def te_o(h, half2=half2, par=par, wi=wi):
                    r = None
                    for hd in range(NCH):
                        r = h.matmul(pb[par][:], wsb(wi)[:, hd * 128:(hd + 1) * 128],
                                     opm_ap(hd)[:, half2 * 512:(half2 + 1) * 512],
                                     start=(hd == 0), stop=(hd == NCH - 1))
                    return r
                ck_og[dm] = W_("te", te_o, waits=waits, inc="te")

                def ve_op(h, dm=dm, half2=half2, par=par):
                    dst = (ffnA[:, dm * 512:(dm + 1) * 512] if half2 == 0
                           else ffnB[:, dm * 512:(dm + 1) * 512])
                    return h.scalar_tensor_tensor(out=dst, in0=pb[par][:],
                                                  scalar=bvecs[:, 32 + dm:32 + dm + 1],
                                                  in1=dst, op0=ALU.add, op1=ALU.add)
                ck_oge[ogi] = W_("ve", ve_op, waits=[("te", ck_og[dm])], inc="ve")
                ogi += 1
        ck_oproj = ck_oge[ogi - 1]

        fxt_ap = lambda b: BIGf[:, 8192 + b * 2048: 8192 + (b + 1) * 2048]
        fos_ap = lambda b: BIGf[:, 12288 + b * 2048: 12288 + (b + 1) * 2048]

        last_fadd = {0: 0, 1: 0}
        ck_fst = {}
        fti = 0
        for half2 in range(2):
            src_ffn = (lambda ch, t4, half2=half2:
                       (ffnA if half2 == 0 else ffnB)[:, ch * 512 + t4 * 128:
                                                      ch * 512 + t4 * 128 + 128])
            for t4 in range(4):
                b = fti % 2
                waits = [("ve", ck_ffnB)]
                if fti >= 2:
                    waits.append(("ve", ck_fst[("e", fti - 2)]))
                    waits.append((f"fst{b}", ck_fst[fti - 2]))

                def sy_fx(h, half2=half2, t4=t4, b=b):
                    r0 = half2 * 512 + t4 * 128
                    return h.dma_start(out=fxt_ap(b), in_=x_own.ap()[r0:r0 + 128, :])
                ck_fx = W_("sy", sy_fx, waits=waits, inc=f"fx{b}", inc_amt=16)

                for grp in range(4):
                    par = grp % 2
                    waits = [("ve", ck_oproj)]
                    if last_fadd[par]:
                        waits.append(("ve", last_fadd[par]))

                    def te_ftr(h, src_ffn=src_ffn, t4=t4, grp=grp, par=par):
                        r = None
                        for k in range(4):
                            r = h.transpose(pb[par][:, k * 128:(k + 1) * 128],
                                            src_ffn(grp * 4 + k, t4), cf[:])
                        return r
                    ck_tr = W_("te", te_ftr, waits=waits, inc="te")

                    def ve_fadd(h, b=b, grp=grp, par=par):
                        return h.tensor_tensor(out=fos_ap(b)[:, grp * 512:(grp + 1) * 512],
                                               in0=pb[par][:],
                                               in1=fxt_ap(b)[:, grp * 512:(grp + 1) * 512],
                                               op=ALU.add)
                    last_fadd[par] = W_("ve", ve_fadd,
                                        waits=[("te", ck_tr), (f"fx{b}", ck_fx)], inc="ve")
                ck_fst[("e", fti)] = last_fadd[3 % 2]

                def sy_fst(h, half2=half2, t4=t4, b=b):
                    r0 = half2 * 512 + t4 * 128
                    return h.dma_start(out=out_own.ap()[r0:r0 + 128, :], in_=fos_ap(b))
                ck_fst[fti] = W_("sy", sy_fst,
                                 waits=[("ve", last_fadd[0]), ("ve", last_fadd[1])],
                                 inc=f"fst{b}", inc_amt=16)
                fti += 1

        if "xnT" in dbg:
            W_("sy", lambda h: h.dma_start(out=dbg["xnT"].ap(), in_=xnT[:]),
               waits=[("se", ck_xnT)], inc="dbg", inc_amt=16)
        if "kvg" in dbg:
            W_("gp", lambda h: h.dma_start(out=dbg["kvg"].ap(), in_=kvg.ap()),
               waits=[("cc", ck_ag1)], inc="dbg", inc_amt=16)
        if "afg" in dbg:
            W_("gp", lambda h: h.dma_start(out=dbg["afg"].ap(), in_=afg.ap()),
               waits=[("cc", ck_ag2)], inc="dbg", inc_amt=16)

        W_("sy", lambda h: None, waits=[("fst0", sch.cnt["fst0"]),
                                        ("fst1", sch.cnt["fst1"]),
                                        ("sy", sch.cnt["sy"]), ("dbg", sch.cnt["dbg"])])
        W_("gp", lambda h: None, waits=[("gp", sch.cnt["gp"])])

        block = es.enter_context(nc.Block())

        def emit(eng_name, h):
            for waits, (inc, amt), fn in sch.ops[eng_name]:
                for s, v in waits:
                    h.wait_ge(sems[s], v)
                inst = fn(h)
                if inc is not None:
                    assert inst is not None, f"{eng_name} op with inc has no instruction"
                    inst.then_inc(sems[inc], amt)

        @block.sync
        def _(h):
            emit("sy", h)

        @block.gpsimd
        def _(h):
            emit("gp", h)

        @block.tensor
        def _(h):
            emit("te", h)

        @block.vector
        def _(h):
            emit("ve", h)

        @block.scalar
        def _(h):
            emit("se", h)

    return nc


_BF = ml_dtypes.bfloat16


def _consts():
    ident = np.eye(128, dtype=_BF)
    ones = np.ones((128, 128), dtype=_BF)
    masks = np.zeros((128, 4 * 512), dtype=_BF)
    for d in range(4):
        m = (np.arange(512)[None, :] >= (128 * d + np.arange(128))[:, None])
        masks[:, d * 512:(d + 1) * 512] = m.astype(_BF)
    cb = np.concatenate([ident, ones, masks], axis=1)
    cf = np.eye(128, dtype=np.float32)
    return np.ascontiguousarray(cb), np.ascontiguousarray(cf)


def kernel(L0_attn_x, L0_attn_Wq, L0_attn_bq, L0_attn_Wk, L0_attn_bk, L0_attn_Wv,
           L0_attn_bv, L0_attn_Wo, L0_attn_bo, L0_ffn_gate_w, L0_ffn_up_w,
           L0_ffn_down_w, L0_ffn_gate_scale, L0_ffn_up_scale, L0_ffn_down_scale):
    x = np.ascontiguousarray(np.asarray(L0_attn_x, dtype=np.float32).reshape(T, D))
    cbv, cfv = _consts()
    bvecs = np.ascontiguousarray(np.concatenate([
        np.asarray(L0_attn_bq, np.float32).reshape(NCH, 128).T,
        np.asarray(L0_attn_bk, np.float32).reshape(NCH, 128).T,
        np.asarray(L0_attn_bo, np.float32).reshape(NCH, 128).T], axis=1))
    bv_all = np.asarray(L0_attn_bv, np.float32).reshape(NCH, 128).T
    scales = np.ascontiguousarray(np.concatenate([
        np.asarray(L0_ffn_gate_scale, np.float32).reshape(ICH, 128).T,
        np.asarray(L0_ffn_up_scale, np.float32).reshape(ICH, 128).T,
        np.asarray(L0_ffn_down_scale, np.float32).reshape(NCH, 128).T], axis=1))

    def chunked(w, nin, nout):
        a = np.asarray(w, np.float32).reshape(nin, 128, nout, 128)
        return np.ascontiguousarray(a.transpose(2, 1, 0, 3))

    shared = {
        "Wq": chunked(L0_attn_Wq, NCH, NCH),
        "Wk": chunked(L0_attn_Wk, NCH, NCH),
        "Wv": np.ascontiguousarray(np.asarray(L0_attn_Wv, np.float32)),
        "Wo": chunked(L0_attn_Wo, NCH, NCH),
        "gw": chunked(L0_ffn_gate_w, NCH, ICH),
        "uw": chunked(L0_ffn_up_w, NCH, ICH),
        "dw": chunked(L0_ffn_down_w, ICH, NCH),
        "bvecs": bvecs, "scales": scales, "constsb": cbv, "constsf": cfv,
    }

    in_maps = []
    for c in range(NCORES):
        h0, h1 = 2 * c, 2 * c + 1
        offs = np.zeros((1, 24), dtype=np.int32)
        offs[0, 0] = h0 * 128 * TOK
        offs[0, 1] = h1 * 128 * TOK
        for hli, hh in ((0, h0), (1, h1)):
            for r in range(NCORES):
                offs[0, 4 + hli * 8 + r] = r * 2 * D * TOK + D * TOK + hh * 128
        offs[0, 20] = c * TOK
        m = dict(shared)
        m["x_own"] = np.ascontiguousarray(x[c * TOK:(c + 1) * TOK])
        m["bvo"] = np.ascontiguousarray(bv_all[:, [h0, h1]])
        m["offs"] = offs
        in_maps.append(m)

    nc = build()
    res = run_bass_kernel_spmd(nc, in_maps, core_ids=list(range(NCORES)))
    out = np.concatenate([r["out_own"] for r in res.results], axis=0)
    if DEBUG:
        kernel._debug = res.results
    return out.reshape(1, T, D)


# version 11
# speedup vs baseline: 1.6194x; 1.6194x over previous
import contextlib

import numpy as np
import ml_dtypes

import concourse.bass as bass
import concourse.mybir as mybir
from concourse.bass_utils import run_bass_kernel_spmd

F32 = mybir.dt.float32
BF16 = mybir.dt.bfloat16
FP8 = mybir.dt.float8e4
I32 = mybir.dt.int32
AF = mybir.ActivationFunctionType
ALU = mybir.AluOpType
nf32 = np.float32

T, D, H, DH, SBK, II = 8192, 2048, 16, 128, 512, 6144
NCORES = 8
TOK = 1024
NCH = D // 128
ICH = II // 128
CEXP = -24.0
QSCALE = float(1.0 / np.sqrt(DH))

DEBUG = []
STOP = None

ENGS = ("sy", "gp", "te", "ve", "se")


class Sched:

    def __init__(self):
        self.ops = {e: [] for e in ENGS}
        import collections
        self.cnt = collections.defaultdict(int)

    def op(self, eng, fn, waits=(), inc=None, inc_amt=1):
        waits = [w for w in waits if w is not None]
        self.ops[eng].append((tuple(waits), (inc, inc_amt), fn))
        if inc is not None:
            self.cnt[inc] += inc_amt
            return self.cnt[inc]
        return None

    def opsw(self, eng, fn, waits=(), n_incs=1):
        waits = [w for w in waits if w is not None]
        base = self.cnt[eng]
        self.cnt[eng] += n_incs
        self.ops[eng].append((tuple(waits), (None, 0),
                              (lambda fn=fn, base=base: lambda h: fn(h, base))()))
        return self.cnt[eng]

    def opn(self, eng, fn, waits=(), bump=None, amt=0):
        waits = [w for w in waits if w is not None]
        self.ops[eng].append((tuple(waits), (None, 0), fn))
        self.cnt[bump] += amt
        return self.cnt[bump]


def _finish(nc, es, sch, sems, dbg, **kw):
    W_ = kw["W_"]
    W_("sy", lambda h: None, waits=[("sy", sch.cnt["sy"])])
    W_("gp", lambda h: None, waits=[("gp", sch.cnt["gp"])])
    block = es.enter_context(nc.Block())

    def emit(eng_name, h):
        for waits, (inc, amt), fn in sch.ops[eng_name]:
            for s, v in waits:
                h.wait_ge(sems[s], v)
            inst = fn(h)
            if inc is not None:
                assert inst is not None
                inst.then_inc(sems[inc], amt)

    @block.sync
    def _(h):
        emit("sy", h)

    @block.gpsimd
    def _(h):
        emit("gp", h)

    @block.tensor
    def _(h):
        emit("te", h)

    @block.vector
    def _(h):
        emit("ve", h)

    @block.scalar
    def _(h):
        emit("se", h)
    return nc


def ap_of(t, offset, dims):
    return bass.AP(t, offset, [list(d) for d in dims])


def dynreg(h):
    if not hasattr(h, "_dynreg"):
        h._dynreg = h.alloc_register("dynoff")
    return h._dynreg


def build():
    nc = bass.Bass("TRN2", target_bir_lowering=False, detect_race_conditions=False)

    x_own = nc.dram_tensor("x_own", [TOK, D], F32, kind="ExternalInput")
    Wq = nc.dram_tensor("Wq", [NCH, 128, NCH, 128], F32, kind="ExternalInput")
    Wk = nc.dram_tensor("Wk", [NCH, 128, NCH, 128], F32, kind="ExternalInput")
    Wv = nc.dram_tensor("Wv", [D, D], F32, kind="ExternalInput")
    Wo = nc.dram_tensor("Wo", [NCH, 128, NCH, 128], F32, kind="ExternalInput")
    gw = nc.dram_tensor("gw", [ICH, 128, NCH, 128], F32, kind="ExternalInput")
    uw = nc.dram_tensor("uw", [ICH, 128, NCH, 128], F32, kind="ExternalInput")
    dw = nc.dram_tensor("dw", [NCH, 128, ICH, 128], F32, kind="ExternalInput")
    bvecs_d = nc.dram_tensor("bvecs", [128, 48], F32, kind="ExternalInput")
    bvo_d = nc.dram_tensor("bvo", [128, 2], F32, kind="ExternalInput")
    scales_d = nc.dram_tensor("scales", [128, 112], F32, kind="ExternalInput")
    cb_d = nc.dram_tensor("constsb", [128, 2304], BF16, kind="ExternalInput")
    cf_d = nc.dram_tensor("constsf", [128, 128], F32, kind="ExternalInput")
    offs_d = nc.dram_tensor("offs", [1, 24], I32, kind="ExternalInput")
    out_own = nc.dram_tensor("out_own", [TOK, D], F32, kind="ExternalOutput")

    kvb = nc.dram_tensor("kvb", [2, D, TOK], BF16)
    kvg = nc.dram_tensor("kvg", [2 * NCORES, D, TOK], BF16)
    afb = nc.dram_tensor("afb", [256, T], BF16)
    afg = nc.dram_tensor("afg", [D, T], BF16)

    dbg = {}
    for name, shp, dt in [("ws3", [128, 6144], FP8),
                          ("fuse", [128, ICH * SBK], BF16),
                          ("gtf", [128, 512], F32),
                          ("xnT", [128, NCH * TOK], BF16),
                          ("qtT", [128, NCH * TOK], BF16),
                          ("kvg", [2 * NCORES, D, TOK], BF16),
                          ("ffnA", [128, NCH * SBK], F32),
                          ("afg", [D, T], BF16)]:
        if name in DEBUG:
            dbg[name] = nc.dram_tensor("dbg_" + name, shp, dt, kind="ExternalOutput")

    es = contextlib.ExitStack()
    with es:
        S = lambda n, shp, d: es.enter_context(nc.sbuf_tensor(n, shp, d))
        P = lambda n, shp, d: es.enter_context(nc.psum_tensor(n, shp, d))

        xnT = S("xnT", [128, NCH * TOK], BF16)
        qtT = S("qtT", [128, NCH * TOK], BF16)
        ffnA = S("ffnA", [128, NCH * SBK], F32)
        BIG = S("BIG", [128, 65536], FP8)
        ws = [S(f"ws{i}", [128, 6144], FP8) for i in range(4)]
        cb = S("cb", [128, 2304], BF16)
        cf = S("cf", [128, 128], F32)
        bvecs = S("bvecs_s", [128, 48], F32)
        bvo = S("bvo_s", [128, 2], F32)
        scales = S("scales_s", [128, 112], F32)
        stats = S("stats", [128, 24], F32)
        ofs = S("ofs", [1, 24], I32)
        mv = S("mv", [128, 2], F32)
        sd = S("sd", [128, 1], F32)
        rinv = S("rinv", [128, 1], F32)
        qslot = [S(f"qslot{i}", [128, 512], BF16) for i in range(2)]
        ptile = [S(f"ptile{i}", [128, 512], BF16) for i in range(4)]
        lrep = S("lrep", [128, 512], F32)
        linv = S("linv", [128, 512], F32)
        otf = lrep
        ostg = [S(f"ostg{i}", [128, 512], BF16) for i in range(2)]
        kstg = ostg
        gtf = [S("gtf0", [128, 512], F32)] * 2
        gsil = [S("gsil0", [128, 512], F32)] * 2
        dbg_psu = None

        pb = [P(f"pb{i}", [128, 512], F32) for i in range(8)]
        pTv = pb[7].ap().bitcast(BF16)

        BIGb = BIG.ap().bitcast(BF16)
        BIGf = BIG.ap().bitcast(F32)
        ffnB = qtT.ap().bitcast(F32)
        identb = cb.ap()[:, 0:128]
        onesb = cb.ap()[:, 128:256]
        maskb = cb.ap()[:, 256:2304]

        sem_names = list(ENGS) + ["cc", "cs", "xa", "xb", "kst0", "kst1", "vs8",
                                  "ost0", "ost1", "fx0", "fx1", "fst0", "fst1",
                                  "ws0", "ws1", "ws2", "ws3", "wv", "kv0", "kv1",
                                  "ofs", "opm", "dbg"]
        sems = {e: es.enter_context(nc.semaphore(f"sem_{e}")) for e in sem_names}

        sch = Sched()
        W_ = sch.op

        W_("sy", lambda h: h.dma_start(out=cb[:], in_=cb_d.ap()), inc="cs", inc_amt=16)
        W_("sy", lambda h: h.dma_start(out=cf[:], in_=cf_d.ap()), inc="cs", inc_amt=16)
        W_("sy", lambda h: h.dma_start(out=bvecs[:], in_=bvecs_d.ap()), inc="cs", inc_amt=16)
        W_("sy", lambda h: h.dma_start(out=bvo[:], in_=bvo_d.ap()), inc="cs", inc_amt=16)
        ck_consts = W_("sy", lambda h: h.dma_start(out=scales[:], in_=scales_d.ap()),
                       inc="cs", inc_amt=16)

        xt_ap = lambda b: BIGf[:, b * 2048:(b + 1) * 2048]
        xnb_ap = lambda b: BIGb[:, 8192 + b * 2048: 8192 + (b + 1) * 2048]

        ck_xnb = [0] * 8
        ck_tr_tile = [0] * 8
        last_evac = 0
        for t8 in range(8):
            b = t8 % 2
            xsem = "xa" if b == 0 else "xb"
            waits = [("ve", ck_xnb[t8 - 2])] if t8 >= 2 else [("cs", ck_consts)]
            ck_xl = W_("sy", (lambda t8=t8, b=b: lambda h: h.dma_start(
                out=xt_ap(b), in_=x_own.ap()[t8 * 128:(t8 + 1) * 128, :]))(),
                waits=waits, inc=xsem, inc_amt=16)

            def ve_ln(h, base, b=b):
                r = None
                for i in range(4):
                    r = h.bn_stats(stats[:, 6 * i:6 * i + 6], xt_ap(b)[:, 512 * i:512 * i + 512])
                r.then_inc(sems["ve"], 1)
                h.wait_ge(sems["ve"], base + 1)
                h.bn_aggr(mv[:], stats[:]).then_inc(sems["ve"], 1)
                h.wait_ge(sems["ve"], base + 2)
                h.tensor_scalar_add(sd[:], mv[:, 1:2], 1e-5).then_inc(sems["ve"], 1)
            ck_sd = sch.opsw("ve", ve_ln, waits=[(xsem, ck_xl)], n_incs=3)
            ck_sq = W_("se", lambda h: h.activation(out=sd[:], in_=sd[:], func=AF.Sqrt),
                       waits=[("ve", ck_sd)], inc="se")

            def ve_norm(h, base, b=b):
                h.reciprocal(rinv[:], sd[:]).then_inc(sems["ve"], 1)
                h.wait_ge(sems["ve"], base + 1)
                h.tensor_scalar(out=xnb_ap(b), in0=xt_ap(b),
                                scalar1=mv[:, 0:1], scalar2=rinv[:],
                                op0=ALU.subtract, op1=ALU.mult).then_inc(sems["ve"], 1)
            w_n = [("se", ck_sq)]
            if t8 >= 2:
                w_n.append(("te", ck_tr_tile[t8 - 2]))
            ck_xnb[t8] = sch.opsw("ve", ve_norm, waits=w_n, n_incs=2)

            for half in range(2):
                waits = [("ve", ck_xnb[t8])]
                if t8 == 0 and half == 0:
                    waits.append(("cs", ck_consts))
                if last_evac:
                    waits.append(("se", last_evac))

                def te_tr(h, b=b, half=half):
                    r = None
                    for c8 in range(8):
                        ch = half * 8 + c8
                        r = h.transpose(pTv[:, c8 * 128:(c8 + 1) * 128],
                                        xnb_ap(b)[:, ch * 128:(ch + 1) * 128], identb)
                    return r
                ck_tr = W_("te", te_tr, waits=waits, inc="te")
                ck_tr_tile[t8] = ck_tr

                def se_evac(h, t8=t8, half=half):
                    dst = ap_of(xnT, (half * 8) * TOK + t8 * 128,
                                [[NCH * TOK, 128], [TOK, 8], [1, 128]])
                    return h.copy(out=dst,
                                  in_=pTv[:, 0:1024].rearrange("p (a b) -> p a b", a=8))
                last_evac = W_("se", se_evac, waits=[("te", ck_tr)], inc="se")

        ck_xnT = last_evac
        if STOP == "A":
            return _finish(nc, es, sch, sems, dbg, xnT=xnT, qtT=qtT, ffnA=ffnA,
                           kvg=kvg, afg=afg, W_=W_, ck={"xnT": ck_xnT})

        wsb = lambda i: ws[i].ap().bitcast(BF16)

        def w_attn_load(h, Wt, ch, wi):
            src = ap_of(Wt, ch * 128 * D, [[D, 128], [1, D]])
            h.dma_start(out=wsb(wi)[:, 0:2048], in_=src).then_inc(sems[f"ws{wi}"], 16)
            return None

        ck_g = {}
        ck_ev = {}
        gidx = 0
        ck_w = {}
        for pi, (Wt, bcol, kind) in enumerate([(Wq, 0, "q"), (Wk, 16, "k")]):
            for ch in range(NCH):
                ci = pi * NCH + ch
                wi = ci % 2
                waits = [("te", ck_g[ci - 2])] if ci >= 2 else []
                ck_w[ci] = sch.opn("gp", (lambda Wt=Wt, ch=ch, wi=wi:
                                          lambda h: w_attn_load(h, Wt, ch, wi))(),
                                   waits=waits, bump=f"ws{wi}", amt=16)
                for half2 in range(2):
                    par = gidx % 2
                    waits = [(f"ws{wi}", ck_w[ci])]
                    if gidx == 0:
                        waits.append(("se", ck_xnT))
                        waits.append(("cs", ck_consts))
                    if gidx >= 2:
                        waits.append(("ve", ck_ev[gidx - 2]))

                    def te_proj(h, wi=wi, half2=half2, par=par):
                        r = None
                        for dch in range(NCH):
                            r = h.matmul(
                                pb[par][:], wsb(wi)[:, dch * 128:(dch + 1) * 128],
                                xnT[:, dch * TOK + half2 * 512: dch * TOK + half2 * 512 + 512],
                                start=(dch == 0), stop=(dch == NCH - 1))
                        return r
                    ck_g[ci] = W_("te", te_proj, waits=waits, inc="te")

                    if kind == "q":
                        def ve_evq(h, ch=ch, half2=half2, par=par):
                            return h.tensor_scalar(
                                out=qtT[:, ch * TOK + half2 * 512: ch * TOK + half2 * 512 + 512],
                                in0=pb[par][:], scalar1=bvecs[:, ch:ch + 1],
                                scalar2=None, op0=ALU.add)
                        ck_ev[gidx] = W_("ve", ve_evq, waits=[("te", ck_g[ci])], inc="ve")
                    else:
                        kb = gidx % 2
                        waits_e = [("te", ck_g[ci])]
                        if (gidx - 2, "kst") in ck_ev:
                            waits_e.append((f"kst{kb}", ck_ev[(gidx - 2, "kst")]))

                        def ve_evk(h, ch=ch, par=par, kb=kb):
                            return h.tensor_scalar(
                                out=kstg[kb][:], in0=pb[par][:],
                                scalar1=bvecs[:, 16 + ch:16 + ch + 1],
                                scalar2=None, op0=ALU.add)
                        ck_ev[gidx] = W_("ve", ve_evk, waits=waits_e, inc="ve")

                        def sy_kst(h, ch=ch, half2=half2, kb=kb):
                            dst = ap_of(kvb, ch * 128 * TOK + half2 * 512,
                                        [[TOK, 128], [1, 512]])
                            return h.dma_start(out=dst, in_=kstg[kb][:])
                        ck_ev[(gidx, "kst")] = W_("sy", sy_kst,
                                                  waits=[("ve", ck_ev[gidx])],
                                                  inc=f"kst{kb}", inc_amt=16)
                    gidx += 1
        ck_kst_fin = {0: sch.cnt["kst0"], 1: sch.cnt["kst1"]}
        if "qtT" in dbg:
            W_("sy", lambda h: h.dma_start(out=dbg["qtT"].ap(), in_=qtT[:]),
               waits=[("ve", ck_ev[31])], inc="dbg", inc_amt=16)

        wv_ap = lambda dch: BIGb[:, dch * 1024:(dch + 1) * 1024]
        vfull_ap = lambda t8: BIGb[:, 16384 + t8 * 2048: 16384 + (t8 + 1) * 2048]

        ck_wv = {}
        ck_v_h0_last = 0
        vgi = 0
        for vhalf in range(2):
            for dch in range(NCH):
                waits = []
                if vhalf == 0 and dch == 0:
                    waits.append(("se", ck_xnT))
                if vhalf == 1:
                    waits.append(("te", ck_v_h0_last))

                def gp_wv(h, dch=dch, vhalf=vhalf):
                    src = ap_of(Wv, dch * 128 * D + vhalf * 1024,
                                [[D, 128], [1, 1024]])
                    return h.dma_start(out=wv_ap(dch), in_=src)
                ck_wv[(vhalf, dch)] = W_("gp", gp_wv, waits=waits, inc="wv", inc_amt=16)
            for t8 in range(8):
                for g2 in range(2):
                    par = vgi % 2
                    waits = [("wv", ck_wv[(vhalf, NCH - 1)])]
                    if vgi >= 2:
                        waits.append(("ve", ck_ev[("v", vgi - 2)]))
                    else:
                        waits.append(("ve", ck_ev[gidx - 1]))

                    def te_v(h, t8=t8, g2=g2, par=par):
                        r = None
                        for dch in range(NCH):
                            r = h.matmul(
                                pb[par][:],
                                xnT[:, dch * TOK + t8 * 128: dch * TOK + t8 * 128 + 128],
                                wv_ap(dch)[:, g2 * 512:(g2 + 1) * 512],
                                start=(dch == 0), stop=(dch == NCH - 1))
                        return r
                    ck_te_v = W_("te", te_v, waits=waits, inc="te")
                    if vhalf == 0:
                        ck_v_h0_last = ck_te_v

                    def ve_evv(h, t8=t8, g2=g2, vhalf=vhalf, par=par):
                        return h.tensor_scalar(
                            out=vfull_ap(t8)[:, vhalf * 1024 + g2 * 512:
                                             vhalf * 1024 + g2 * 512 + 512],
                            in0=pb[par][:], scalar1=1.0, scalar2=None, op0=ALU.mult)
                    ck_ev[("v", vgi)] = W_("ve", ve_evv, waits=[("te", ck_te_v)], inc="ve")
                    vgi += 1

        ck_vst = 0
        for t8 in range(8):
            def sy_vst(h, t8=t8):
                dst = ap_of(kvb, D * TOK + t8 * 128 * D, [[D, 128], [1, D]])
                return h.dma_start(out=dst, in_=vfull_ap(t8))
            ck_vst = W_("sy", sy_vst, waits=[("ve", ck_ev[("v", 16 + t8 * 2 + 1)])],
                        inc="vs8", inc_amt=16)

        ck_ag1 = W_("gp", lambda h: h.collective_compute(
            "AllGather", ALU.bypass, replica_groups=[list(range(NCORES))],
            ins=[kvb.ap().opt()], outs=[kvg.ap().opt()]),
            waits=[("vs8", ck_vst), ("kst0", ck_kst_fin[0]), ("kst1", ck_kst_fin[1])],
            inc="cc")

        if STOP == "B":
            if "qtT" in dbg:
                pass
            if "kvg" in dbg:
                W_("gp", lambda h: h.dma_start(out=dbg["kvg"].ap(), in_=kvg.ap()),
                   waits=[("cc", ck_ag1)], inc="gp", inc_amt=16)
            return _finish(nc, es, sch, sems, dbg, xnT=xnT, qtT=qtT, ffnA=ffnA,
                           kvg=kvg, afg=afg, W_=W_, ck={})

        fuse_ap = lambda ich: BIGb[:, ich * 512:(ich + 1) * 512]

        def build_ffn(hb, out_slice, first_waits, tag):
            ck_wg = {}
            last_down_te = 0
            for ich in range(ICH):
                for gu, Wt in ((0, gw), (1, uw)):
                    wi = (ich % 2) * 2 + gu
                    waits = list(first_waits) if (ich == 0 and gu == 0) else []
                    if ich >= 2:
                        waits.append(("ve", ck_ev[(tag, "fuse", ich - 2)]))

                    def gp_w(h, Wt=Wt, ich=ich, wi=wi):
                        src = ap_of(Wt, ich * 128 * D, [[D, 128], [1, D]])
                        h.dma_start(out=ws[wi].ap()[:, 0:2048],
                                    in_=src).then_inc(sems[f"ws{wi}"], 16)
                        return None
                    ck_wg[(ich, gu)] = sch.opn("gp", gp_w, waits=waits, bump=f"ws{wi}", amt=16)

                parG, parU = (ich % 2), 2 + (ich % 2)
                wiG, wiU = (ich % 2) * 2, (ich % 2) * 2 + 1
                waits = [(f"ws{wiG}", ck_wg[(ich, 0)]), (f"ws{wiU}", ck_wg[(ich, 1)])]
                if ich >= 2:
                    waits.append(("ve", ck_ev[(tag, "gtf", ich - 2)]))
                    waits.append(("ve", ck_ev[(tag, "fuse", ich - 2)]))

                def te_gu(h, ich=ich, hb=hb, parG=parG, parU=parU):
                    wiG, wiU = (ich % 2) * 2, (ich % 2) * 2 + 1
                    r = None
                    for dch in range(NCH):
                        h.matmul(pb[parG][:], ws[wiG].ap()[:, dch * 128:(dch + 1) * 128],
                                 xnT[:, dch * TOK + hb * 512: dch * TOK + hb * 512 + 512],
                                 start=(dch == 0), stop=(dch == NCH - 1))
                    for dch in range(NCH):
                        r = h.matmul(pb[parU][:], ws[wiU].ap()[:, dch * 128:(dch + 1) * 128],
                                     xnT[:, dch * TOK + hb * 512: dch * TOK + hb * 512 + 512],
                                     start=(dch == 0), stop=(dch == NCH - 1))
                    return r
                ck_teg = W_("te", te_gu, waits=waits, inc="te")



                gb = ich % 2

                def ve_g(h, ich=ich, parG=parG, gb=gb):
                    return h.tensor_scalar(out=gtf[gb][:], in0=pb[parG][:],
                                           scalar1=scales[:, ich:ich + 1], scalar2=None,
                                           op0=ALU.mult)
                wv_g = [("te", ck_teg)]
                if ich >= 1:
                    wv_g.append(("se", ck_ev[(tag, "sil", ich - 1)]))
                ck_ev[(tag, "gtf", ich)] = W_("ve", ve_g, waits=wv_g, inc="ve")

                ws_s = [("ve", ck_ev[(tag, "gtf", ich)])]
                if ich >= 1:
                    ws_s.append(("ve", ck_ev[(tag, "fuse", ich - 1)]))
                ck_sil = W_("se", (lambda gb=gb: lambda h: h.activation(
                    out=gsil[gb][:], in_=gtf[gb][:], func=AF.Silu))(),
                    waits=ws_s, inc="se")
                ck_ev[(tag, "sil", ich)] = ck_sil

                import os as _os
                _fv = _os.environ.get("FUSEVAR", "")

                def ve_fuse(h, ich=ich, parU=parU, gb=gb, _fv=_fv):
                    if _fv == "psu":
                        return h.tensor_scalar(out=fuse_ap(ich), in0=pb[parU][:],
                                               scalar1=scales[:, 48 + ich:48 + ich + 1],
                                               scalar2=None, op0=ALU.mult)
                    if _fv == "gsil":
                        return h.tensor_scalar(out=fuse_ap(ich), in0=gsil[gb][:],
                                               scalar1=1.0, scalar2=None, op0=ALU.mult)
                    return h.scalar_tensor_tensor(out=fuse_ap(ich), in0=pb[parU][:],
                                                  scalar=scales[:, 48 + ich:48 + ich + 1],
                                                  in1=gsil[gb][:], op0=ALU.mult, op1=ALU.mult)
                ck_ev[(tag, "fuse", ich)] = W_("ve", ve_fuse, waits=[("se", ck_sil)], inc="ve")

            for dm in range(NCH):
                wi = dm % 2
                waits = [("ve", ck_ev[(tag, "fuse", ICH - 1)])] if dm < 2 else []
                if dm >= 2:
                    waits.append(("te", ck_ev[(tag, "dte", dm - 2)]))

                def gp_wd(h, dm=dm, wi=wi):
                    src = ap_of(dw, dm * 128 * II, [[II, 128], [1, II]])
                    h.dma_start(out=ws[wi].ap()[:, 0:6144],
                                in_=src).then_inc(sems[f"ws{wi}"], 16)
                    return None
                ck_wd = sch.opn("gp", gp_wd, waits=waits, bump=f"ws{wi}", amt=16)

                par = dm % 2
                waits = [(f"ws{wi}", ck_wd)]
                if dm >= 2:
                    waits.append(("ve", ck_ev[(tag, "down", dm - 2)]))

                def te_d(h, dm=dm, par=par, wi=wi):
                    r = None
                    for ick in range(ICH):
                        r = h.matmul(pb[par][:], ws[wi].ap()[:, ick * 128:(ick + 1) * 128],
                                     fuse_ap(ick), start=(ick == 0), stop=(ick == ICH - 1))
                    return r
                ck_ev[(tag, "dte", dm)] = W_("te", te_d, waits=waits, inc="te")

                def ve_d(h, dm=dm, par=par):
                    return h.tensor_scalar(out=out_slice(dm), in0=pb[par][:],
                                           scalar1=scales[:, 96 + dm:96 + dm + 1],
                                           scalar2=None, op0=ALU.mult)
                ck_ev[(tag, "down", dm)] = W_("ve", ve_d,
                                              waits=[("te", ck_ev[(tag, "dte", dm)])],
                                              inc="ve")
            return ck_ev[(tag, "down", NCH - 1)]

        ck_ffnA = build_ffn(0, lambda dm: ffnA[:, dm * 512:(dm + 1) * 512],
                            [("vs8", ck_vst), ("ve", ck_ev[("v", 31)])], "fA")
        if "ffnA" in dbg:
            W_("sy", lambda h: h.dma_start(out=dbg["ffnA"].ap(), in_=ffnA[:]),
               waits=[("ve", ck_ffnA)], inc="dbg", inc_amt=16)
        ck_dbg_fuse = None
        if "fuse" in dbg:
            ck_dbg_fuse = W_("sy", lambda h: h.dma_start(out=dbg["fuse"].ap(),
                                                         in_=BIGb[:, 0:ICH * 512]),
                             waits=[("ve", ck_ev[("fA", "fuse", ICH - 1)])], inc="dbg", inc_amt=16)


        if STOP == "FA":
            return _finish(nc, es, sch, sems, dbg, xnT=xnT, qtT=qtT, ffnA=ffnA,
                           kvg=kvg, afg=afg, W_=W_, ck={})

        kh_ap = lambda hl: BIGb[:, hl * 16384: hl * 16384 + 8192]
        vh_ap = lambda hl: BIGb[:, hl * 16384 + 8192: hl * 16384 + 16384]

        ck_kv = {}
        ck_ofs = W_("gp", lambda h: h.dma_start(out=ofs[:], in_=offs_d.ap()),
                    inc="ofs", inc_amt=16)
        for hl in range(2):
            waits = [("cc", ck_ag1), ("ofs", ck_ofs)]
            if hl == 0:
                waits.append(("ve", ck_ffnA))
                if "fuse" in dbg:
                    waits.append(("dbg", ck_dbg_fuse))

            def gp_kv(h, hl=hl):
                rk = dynreg(h)
                h.reg_load(rk, ofs[0:1, hl:hl + 1])
                src = bass.AP(kvg, rk,
                              [[TOK, 128], [2 * D * TOK, NCORES], [1, TOK]])
                h.dma_start(out=kh_ap(hl), in_=src).then_inc(sems[f"kv{hl}"], 16)
                for r in range(NCORES):
                    h.reg_load(rk, ofs[0:1, 4 + hl * 8 + r:5 + hl * 8 + r])
                    src = bass.AP(kvg, rk, [[D, 128], [128 * D, 8], [1, 128]])
                    h.dma_start(out=vh_ap(hl)[:, r * 1024:(r + 1) * 1024],
                                in_=src).then_inc(sems[f"kv{hl}"], 16)
                return None
            ck_kv[hl] = sch.opn("gp", gp_kv, waits=waits, bump=f"kv{hl}", amt=16 * 9)

        psL = {0: pb[6], 1: pb[7]}
        ck_qs = {}
        ck_exp = {}
        ck_lmm = {}
        ck_oev = {}
        last_s_te = {}
        for C in range(NCH):
            nst = 4 * (C + 1)
            for hh in range(2):
                waits = [(f"kv{hh}", ck_kv[hh])]
                if C == 0:
                    waits.append(("ve", ck_ffnA))
                else:
                    waits.append(("te", last_s_te[hh]))

                def ve_qs(h, hh=hh, C=C):
                    srcap = ap_of(qtT, hh * 512 + 32 * C,
                                  [[NCH * TOK, 128], [1, 32], [TOK, 16]])
                    return h.tensor_copy(qslot[hh][:], srcap)
                ck_qs[hh] = W_("ve", ve_qs, waits=waits, inc="ve")

            for stg in range(nst + 1):
                if stg < nst:
                    for hh in range(2):
                        st = stg
                        par_s = st % 2
                        waits = [("ve", ck_qs[hh])]
                        if st >= 2:
                            waits.append(("se", ck_exp[(hh, st - 2)]))
                        elif C >= 1:
                            waits.append(("se", ck_exp[(hh, "last")]))
                        elif st < 2:
                            waits.append(("ve", ck_ffnA))

                        def te_s(h, hh=hh, st=st, par_s=par_s):
                            return h.matmul(pb[2 * hh + par_s][:],
                                            kh_ap(hh)[:, st * 128:(st + 1) * 128],
                                            qslot[hh][:], start=True, stop=True)
                        ck_s = W_("te", te_s, waits=waits, inc="te")
                        if st == nst - 1:
                            last_s_te[hh] = ck_s

                        w_e = [("te", ck_s)]
                        if st >= 2:
                            w_e.append(("te", ck_lmm[(hh, st - 2)]))
                        elif C >= 1:
                            w_e.append(("te", ck_lmm[(hh, "last")]))

                        def se_exp(h, hh=hh, pt_b=2 * hh + st % 2, par_s=par_s):
                            return h.activation(out=ptile[pt_b][:],
                                                in_=pb[2 * hh + par_s][:],
                                                func=AF.Exp, bias=nf32(CEXP),
                                                scale=nf32(QSCALE))
                        ck_exp[(hh, st)] = W_("se", se_exp, waits=w_e, inc="se")

                        if st >= 4 * C:
                            def ve_mask(h, st=st, C=C, pt_b=2 * hh + st % 2):
                                mi = st - 4 * C
                                return h.tensor_tensor(out=ptile[pt_b][:],
                                                       in0=ptile[pt_b][:],
                                                       in1=maskb[:, mi * 512:(mi + 1) * 512],
                                                       op=ALU.mult)
                            ck_exp[(hh, st, "m")] = W_("ve", ve_mask,
                                                       waits=[("se", ck_exp[(hh, st)])],
                                                       inc="ve")
                if stg >= 1:
                    for hh in range(2):
                        st = stg - 1
                        pt_b = 2 * hh + st % 2
                        if st >= 4 * C:
                            av_waits = [("ve", ck_exp[(hh, st, "m")])]
                        else:
                            av_waits = [("se", ck_exp[(hh, st)])]
                        if st == 0 and C >= 1:
                            av_waits.append(("ve", ck_oev[hh]))

                        def te_av(h, hh=hh, st=st, pt_b=pt_b,
                                  first=(st == 0), last=(st == nst - 1)):
                            h.matmul(pb[4 + hh][:], vh_ap(hh)[:, st * 128:(st + 1) * 128],
                                     ptile[pt_b][:], start=first, stop=last)
                            return h.matmul(psL[hh][:], onesb, ptile[pt_b][:],
                                            start=first, stop=last)
                        ck_lmm[(hh, st)] = W_("te", te_av, waits=av_waits, inc="te")
            ck_exp[(0, "last")] = ck_exp[(0, nst - 1)]
            ck_exp[(1, "last")] = ck_exp[(1, nst - 1)]
            ck_lmm[(0, "last")] = ck_lmm[(0, nst - 1)]
            ck_lmm[(1, "last")] = ck_lmm[(1, nst - 1)]

            for hh in range(2):
                def ve_oev(h, base, hh=hh):
                    h.tensor_scalar(out=lrep[:], in0=psL[hh][:], scalar1=1.0,
                                    scalar2=None, op0=ALU.mult).then_inc(sems["ve"], 1)
                    h.wait_ge(sems["ve"], base + 1)
                    h.reciprocal(linv[:], lrep[:]).then_inc(sems["ve"], 1)
                    h.wait_ge(sems["ve"], base + 2)
                    h.scalar_tensor_tensor(out=otf[:], in0=pb[4 + hh][:], scalar=1.0,
                                           in1=linv[:], op0=ALU.mult, op1=ALU.mult
                                           ).then_inc(sems["ve"], 1)
                    h.wait_ge(sems["ve"], base + 3)
                    h.tensor_scalar(out=ostg[hh][:], in0=otf[:],
                                    scalar1=bvo[:, hh:hh + 1], scalar2=None,
                                    op0=ALU.add).then_inc(sems["ve"], 1)
                w_o = [("te", ck_lmm[(hh, "last")])]
                if C >= 1:
                    w_o.append((f"ost{hh}", ck_oev[(hh, "st")]))
                ck_oev[hh] = sch.opsw("ve", ve_oev, waits=w_o, n_incs=4)

                def sy_ost(h, hh=hh, C=C):
                    dst = ap_of(afb, hh * 128 * T + C * 512, [[T, 128], [1, 512]])
                    return h.dma_start(out=dst, in_=ostg[hh][:])
                ck_oev[(hh, "st")] = W_("sy", sy_ost, waits=[("ve", ck_oev[hh])],
                                        inc=f"ost{hh}", inc_amt=16)

        last_exp = max(ck_exp[(0, "last")], ck_exp[(1, "last")])
        ost_fin = (sch.cnt["ost0"], sch.cnt["ost1"])

        ck_ffnB = build_ffn(1, lambda dm: ffnB[:, dm * 512:(dm + 1) * 512],
                            [("ve", ck_oev[1]), ("ve", ck_oev[0]),
                             ("te", max(ck_lmm[(0, "last")], ck_lmm[(1, "last")])),
                             ("se", last_exp)], "fB")

        ck_ag2 = W_("gp", lambda h: h.collective_compute(
            "AllGather", ALU.bypass, replica_groups=[list(range(NCORES))],
            ins=[afb.ap().opt()], outs=[afg.ap().opt()]),
            waits=[("ost0", ost_fin[0]), ("ost1", ost_fin[1])], inc="cc")

        if STOP == "FB":
            return _finish(nc, es, sch, sems, dbg, xnT=xnT, qtT=qtT, ffnA=ffnA,
                           kvg=kvg, afg=afg, W_=W_, ck={})

        opm_ap = lambda hd: BIGb[:, hd * 1024:(hd + 1) * 1024]

        def gp_opm(h):
            ro = dynreg(h)
            h.reg_load(ro, ofs[0:1, 20:21])
            return h.dma_start(out=BIGb[:, 0:16384],
                               in_=bass.AP(afg, ro,
                                           [[T, 128], [128 * T, NCH], [1, TOK]]))
        ck_opm = W_("gp", gp_opm,
                    waits=[("cc", ck_ag2), ("ve", ck_ffnB)], inc="opm", inc_amt=16)

        ogi = 0
        ck_og = {}
        ck_oge = {}
        for dm in range(NCH):
            wi = dm % 2
            waits = [("ve", ck_ffnB)] if dm < 2 else [("te", ck_og[dm - 2])]
            ck_wo = sch.opn("gp", (lambda dm=dm, wi=wi:
                                   lambda h: w_attn_load(h, Wo, dm, wi))(),
                            waits=waits, bump=f"ws{wi}", amt=16)
            for half2 in range(2):
                par = ogi % 2
                waits = [(f"ws{wi}", ck_wo), ("opm", ck_opm)]
                if ogi >= 2:
                    waits.append(("ve", ck_oge[ogi - 2]))

                def te_o(h, half2=half2, par=par, wi=wi):
                    r = None
                    for hd in range(NCH):
                        r = h.matmul(pb[par][:], wsb(wi)[:, hd * 128:(hd + 1) * 128],
                                     opm_ap(hd)[:, half2 * 512:(half2 + 1) * 512],
                                     start=(hd == 0), stop=(hd == NCH - 1))
                    return r
                ck_og[dm] = W_("te", te_o, waits=waits, inc="te")

                def ve_op(h, dm=dm, half2=half2, par=par):
                    dst = (ffnA[:, dm * 512:(dm + 1) * 512] if half2 == 0
                           else ffnB[:, dm * 512:(dm + 1) * 512])
                    return h.scalar_tensor_tensor(out=dst, in0=pb[par][:],
                                                  scalar=bvecs[:, 32 + dm:32 + dm + 1],
                                                  in1=dst, op0=ALU.add, op1=ALU.add)
                ck_oge[ogi] = W_("ve", ve_op, waits=[("te", ck_og[dm])], inc="ve")
                ogi += 1
        ck_oproj = ck_oge[ogi - 1]

        fxt_ap = lambda b: BIGf[:, 8192 + b * 2048: 8192 + (b + 1) * 2048]
        fos_ap = lambda b: BIGf[:, 12288 + b * 2048: 12288 + (b + 1) * 2048]

        last_fadd = {0: 0, 1: 0}
        ck_fst = {}
        fti = 0
        for half2 in range(2):
            src_ffn = (lambda ch, t4, half2=half2:
                       (ffnA if half2 == 0 else ffnB)[:, ch * 512 + t4 * 128:
                                                      ch * 512 + t4 * 128 + 128])
            for t4 in range(4):
                b = fti % 2
                waits = [("ve", ck_ffnB)]
                if fti >= 2:
                    waits.append(("ve", ck_fst[("e", fti - 2)]))
                    waits.append((f"fst{b}", ck_fst[fti - 2]))

                def sy_fx(h, half2=half2, t4=t4, b=b):
                    r0 = half2 * 512 + t4 * 128
                    return h.dma_start(out=fxt_ap(b), in_=x_own.ap()[r0:r0 + 128, :])
                ck_fx = W_("sy", sy_fx, waits=waits, inc=f"fx{b}", inc_amt=16)

                for grp in range(4):
                    par = grp % 2
                    waits = [("ve", ck_oproj)]
                    if last_fadd[par]:
                        waits.append(("ve", last_fadd[par]))

                    def te_ftr(h, src_ffn=src_ffn, t4=t4, grp=grp, par=par):
                        r = None
                        for k in range(4):
                            r = h.transpose(pb[par][:, k * 128:(k + 1) * 128],
                                            src_ffn(grp * 4 + k, t4), cf[:])
                        return r
                    ck_tr = W_("te", te_ftr, waits=waits, inc="te")

                    def ve_fadd(h, b=b, grp=grp, par=par):
                        return h.tensor_tensor(out=fos_ap(b)[:, grp * 512:(grp + 1) * 512],
                                               in0=pb[par][:],
                                               in1=fxt_ap(b)[:, grp * 512:(grp + 1) * 512],
                                               op=ALU.add)
                    last_fadd[par] = W_("ve", ve_fadd,
                                        waits=[("te", ck_tr), (f"fx{b}", ck_fx)], inc="ve")
                ck_fst[("e", fti)] = last_fadd[3 % 2]

                def sy_fst(h, half2=half2, t4=t4, b=b):
                    r0 = half2 * 512 + t4 * 128
                    return h.dma_start(out=out_own.ap()[r0:r0 + 128, :], in_=fos_ap(b))
                ck_fst[fti] = W_("sy", sy_fst,
                                 waits=[("ve", last_fadd[0]), ("ve", last_fadd[1])],
                                 inc=f"fst{b}", inc_amt=16)
                fti += 1

        if "xnT" in dbg:
            W_("sy", lambda h: h.dma_start(out=dbg["xnT"].ap(), in_=xnT[:]),
               waits=[("se", ck_xnT)], inc="dbg", inc_amt=16)
        if "kvg" in dbg:
            W_("gp", lambda h: h.dma_start(out=dbg["kvg"].ap(), in_=kvg.ap()),
               waits=[("cc", ck_ag1)], inc="dbg", inc_amt=16)
        if "afg" in dbg:
            W_("gp", lambda h: h.dma_start(out=dbg["afg"].ap(), in_=afg.ap()),
               waits=[("cc", ck_ag2)], inc="dbg", inc_amt=16)

        W_("sy", lambda h: None, waits=[("fst0", sch.cnt["fst0"]),
                                        ("fst1", sch.cnt["fst1"]),
                                        ("sy", sch.cnt["sy"]), ("dbg", sch.cnt["dbg"])])
        W_("gp", lambda h: None, waits=[("gp", sch.cnt["gp"])])

        block = es.enter_context(nc.Block())

        def emit(eng_name, h):
            for waits, (inc, amt), fn in sch.ops[eng_name]:
                for s, v in waits:
                    h.wait_ge(sems[s], v)
                inst = fn(h)
                if inc is not None:
                    assert inst is not None, f"{eng_name} op with inc has no instruction"
                    inst.then_inc(sems[inc], amt)

        @block.sync
        def _(h):
            emit("sy", h)

        @block.gpsimd
        def _(h):
            emit("gp", h)

        @block.tensor
        def _(h):
            emit("te", h)

        @block.vector
        def _(h):
            emit("ve", h)

        @block.scalar
        def _(h):
            emit("se", h)

    return nc


_BF = ml_dtypes.bfloat16


def _consts():
    ident = np.eye(128, dtype=_BF)
    ones = np.ones((128, 128), dtype=_BF)
    masks = np.zeros((128, 4 * 512), dtype=_BF)
    for d in range(4):
        m = (np.arange(512)[None, :] >= (128 * d + np.arange(128))[:, None])
        masks[:, d * 512:(d + 1) * 512] = m.astype(_BF)
    cb = np.concatenate([ident, ones, masks], axis=1)
    cf = np.eye(128, dtype=np.float32)
    return np.ascontiguousarray(cb), np.ascontiguousarray(cf)


def kernel(L0_attn_x, L0_attn_Wq, L0_attn_bq, L0_attn_Wk, L0_attn_bk, L0_attn_Wv,
           L0_attn_bv, L0_attn_Wo, L0_attn_bo, L0_ffn_gate_w, L0_ffn_up_w,
           L0_ffn_down_w, L0_ffn_gate_scale, L0_ffn_up_scale, L0_ffn_down_scale):
    x = np.ascontiguousarray(np.asarray(L0_attn_x, dtype=np.float32).reshape(T, D))
    cbv, cfv = _consts()
    bvecs = np.ascontiguousarray(np.concatenate([
        np.asarray(L0_attn_bq, np.float32).reshape(NCH, 128).T,
        np.asarray(L0_attn_bk, np.float32).reshape(NCH, 128).T,
        np.asarray(L0_attn_bo, np.float32).reshape(NCH, 128).T], axis=1))
    bv_all = np.asarray(L0_attn_bv, np.float32).reshape(NCH, 128).T
    scales = np.ascontiguousarray(np.concatenate([
        np.asarray(L0_ffn_gate_scale, np.float32).reshape(ICH, 128).T,
        np.asarray(L0_ffn_up_scale, np.float32).reshape(ICH, 128).T,
        np.asarray(L0_ffn_down_scale, np.float32).reshape(NCH, 128).T], axis=1))

    def chunked(w, nin, nout):
        a = np.asarray(w, np.float32).reshape(nin, 128, nout, 128)
        return np.ascontiguousarray(a.transpose(2, 1, 0, 3))

    shared = {
        "Wq": chunked(L0_attn_Wq, NCH, NCH),
        "Wk": chunked(L0_attn_Wk, NCH, NCH),
        "Wv": np.ascontiguousarray(np.asarray(L0_attn_Wv, np.float32)),
        "Wo": chunked(L0_attn_Wo, NCH, NCH),
        "gw": chunked(L0_ffn_gate_w, NCH, ICH),
        "uw": chunked(L0_ffn_up_w, NCH, ICH),
        "dw": chunked(L0_ffn_down_w, ICH, NCH),
        "bvecs": bvecs, "scales": scales, "constsb": cbv, "constsf": cfv,
    }

    in_maps = []
    for c in range(NCORES):
        h0, h1 = 2 * c, 2 * c + 1
        offs = np.zeros((1, 24), dtype=np.int32)
        offs[0, 0] = h0 * 128 * TOK
        offs[0, 1] = h1 * 128 * TOK
        for hli, hh in ((0, h0), (1, h1)):
            for r in range(NCORES):
                offs[0, 4 + hli * 8 + r] = r * 2 * D * TOK + D * TOK + hh * 128
        offs[0, 20] = c * TOK
        m = dict(shared)
        m["x_own"] = np.ascontiguousarray(x[c * TOK:(c + 1) * TOK])
        m["bvo"] = np.ascontiguousarray(bv_all[:, [h0, h1]])
        m["offs"] = offs
        in_maps.append(m)

    nc = build()
    res = run_bass_kernel_spmd(nc, in_maps, core_ids=list(range(NCORES)))
    out = np.concatenate([r["out_own"] for r in res.results], axis=0)
    if DEBUG:
        kernel._debug = res.results
    return out.reshape(1, T, D)


# version 12
# speedup vs baseline: 1.7422x; 1.0758x over previous
import contextlib

import numpy as np
import ml_dtypes

import concourse.bass as bass
import concourse.mybir as mybir
from concourse.bass_utils import run_bass_kernel_spmd

F32 = mybir.dt.float32
BF16 = mybir.dt.bfloat16
FP8 = mybir.dt.float8e4
I32 = mybir.dt.int32
AF = mybir.ActivationFunctionType
ALU = mybir.AluOpType
nf32 = np.float32

T, D, H, DH, SBK, II = 8192, 2048, 16, 128, 512, 6144
NCORES = 8
TOK = 1024
NCH = D // 128
ICH = II // 128
CEXP = -24.0
QSCALE = float(1.0 / np.sqrt(DH))

DEBUG = []
STOP = None

ENGS = ("sy", "gp", "te", "ve", "se")


class Sched:

    def __init__(self):
        self.ops = {e: [] for e in ENGS}
        import collections
        self.cnt = collections.defaultdict(int)

    def op(self, eng, fn, waits=(), inc=None, inc_amt=1):
        waits = [w for w in waits if w is not None]
        self.ops[eng].append((tuple(waits), (inc, inc_amt), fn))
        if inc is not None:
            self.cnt[inc] += inc_amt
            return self.cnt[inc]
        return None

    def opsw(self, eng, fn, waits=(), n_incs=1):
        waits = [w for w in waits if w is not None]
        base = self.cnt[eng]
        self.cnt[eng] += n_incs
        self.ops[eng].append((tuple(waits), (None, 0),
                              (lambda fn=fn, base=base: lambda h: fn(h, base))()))
        return self.cnt[eng]

    def opn(self, eng, fn, waits=(), bump=None, amt=0):
        waits = [w for w in waits if w is not None]
        self.ops[eng].append((tuple(waits), (None, 0), fn))
        self.cnt[bump] += amt
        return self.cnt[bump]


def _finish(nc, es, sch, sems, dbg, **kw):
    W_ = kw["W_"]
    W_("sy", lambda h: None, waits=[("sy", sch.cnt["sy"])])
    W_("gp", lambda h: None, waits=[("gp", sch.cnt["gp"])])
    block = es.enter_context(nc.Block())

    def emit(eng_name, h):
        for waits, (inc, amt), fn in sch.ops[eng_name]:
            for s, v in waits:
                h.wait_ge(sems[s], v)
            inst = fn(h)
            if inc is not None:
                assert inst is not None
                inst.then_inc(sems[inc], amt)

    @block.sync
    def _(h):
        emit("sy", h)

    @block.gpsimd
    def _(h):
        emit("gp", h)

    @block.tensor
    def _(h):
        emit("te", h)

    @block.vector
    def _(h):
        emit("ve", h)

    @block.scalar
    def _(h):
        emit("se", h)
    return nc


def ap_of(t, offset, dims):
    return bass.AP(t, offset, [list(d) for d in dims])


def dynreg(h):
    if not hasattr(h, "_dynreg"):
        h._dynreg = h.alloc_register("dynoff")
    return h._dynreg


def build():
    nc = bass.Bass("TRN2", target_bir_lowering=False, detect_race_conditions=False)

    x_own = nc.dram_tensor("x_own", [TOK, D], F32, kind="ExternalInput")
    Wq = nc.dram_tensor("Wq", [NCH, 128, NCH, 128], F32, kind="ExternalInput")
    Wk = nc.dram_tensor("Wk", [NCH, 128, NCH, 128], F32, kind="ExternalInput")
    Wv = nc.dram_tensor("Wv", [D, D], F32, kind="ExternalInput")
    Wo = nc.dram_tensor("Wo", [NCH, 128, NCH, 128], F32, kind="ExternalInput")
    gw = nc.dram_tensor("gw", [ICH, 128, NCH, 128], F32, kind="ExternalInput")
    uw = nc.dram_tensor("uw", [ICH, 128, NCH, 128], F32, kind="ExternalInput")
    dw = nc.dram_tensor("dw", [NCH, 128, ICH, 128], F32, kind="ExternalInput")
    bvecs_d = nc.dram_tensor("bvecs", [128, 48], F32, kind="ExternalInput")
    bvo_d = nc.dram_tensor("bvo", [128, 2], F32, kind="ExternalInput")
    scales_d = nc.dram_tensor("scales", [128, 112], F32, kind="ExternalInput")
    cb_d = nc.dram_tensor("constsb", [128, 2304], BF16, kind="ExternalInput")
    cf_d = nc.dram_tensor("constsf", [128, 128], F32, kind="ExternalInput")
    offs_d = nc.dram_tensor("offs", [1, 24], I32, kind="ExternalInput")
    out_own = nc.dram_tensor("out_own", [TOK, D], F32, kind="ExternalOutput")

    kvb = nc.dram_tensor("kvb", [2, D, TOK], BF16)
    kvg = nc.dram_tensor("kvg", [2 * NCORES, D, TOK], BF16, addr_space="Shared")
    afb = nc.dram_tensor("afb", [256, T], BF16)
    afg = nc.dram_tensor("afg", [D, T], BF16, addr_space="Shared")

    dbg = {}
    for name, shp, dt in [("ws3", [128, 6144], FP8),
                          ("fuse", [128, ICH * SBK], BF16),
                          ("gtf", [128, 512], F32),
                          ("xnT", [128, NCH * TOK], BF16),
                          ("qtT", [128, NCH * TOK], BF16),
                          ("kvg", [2 * NCORES, D, TOK], BF16),
                          ("ffnA", [128, NCH * SBK], F32),
                          ("afg", [D, T], BF16)]:
        if name in DEBUG:
            dbg[name] = nc.dram_tensor("dbg_" + name, shp, dt, kind="ExternalOutput")

    es = contextlib.ExitStack()
    with es:
        S = lambda n, shp, d: es.enter_context(nc.sbuf_tensor(n, shp, d))
        P = lambda n, shp, d: es.enter_context(nc.psum_tensor(n, shp, d))

        xnT = S("xnT", [128, NCH * TOK], BF16)
        qtT = S("qtT", [128, NCH * TOK], BF16)
        ffnA = S("ffnA", [128, NCH * SBK], F32)
        BIG = S("BIG", [128, 65536], FP8)
        ws = [S(f"ws{i}", [128, 6144], FP8) for i in range(4)]
        cb = S("cb", [128, 2304], BF16)
        cf = S("cf", [128, 128], F32)
        bvecs = S("bvecs_s", [128, 48], F32)
        bvo = S("bvo_s", [128, 2], F32)
        scales = S("scales_s", [128, 112], F32)
        stats = S("stats", [128, 24], F32)
        ofs = S("ofs", [1, 24], I32)
        mv = S("mv", [128, 2], F32)
        sd = S("sd", [128, 1], F32)
        rinv = S("rinv", [128, 1], F32)
        qslot = [S(f"qslot{i}", [128, 512], BF16) for i in range(2)]
        ptile = [S(f"ptile{i}", [128, 512], BF16) for i in range(4)]
        lrep = S("lrep", [128, 512], F32)
        linv = S("linv", [128, 512], F32)
        otf = lrep
        ostg = [S(f"ostg{i}", [128, 512], BF16) for i in range(2)]
        kstg = ostg
        gtf = [S("gtf0", [128, 512], F32)] * 2
        gsil = [S("gsil0", [128, 512], F32)] * 2
        dbg_psu = None

        pb = [P(f"pb{i}", [128, 512], F32) for i in range(8)]
        pTv = pb[7].ap().bitcast(BF16)

        BIGb = BIG.ap().bitcast(BF16)
        BIGf = BIG.ap().bitcast(F32)
        ffnB = qtT.ap().bitcast(F32)
        identb = cb.ap()[:, 0:128]
        onesb = cb.ap()[:, 128:256]
        maskb = cb.ap()[:, 256:2304]

        sem_names = list(ENGS) + ["cc", "cs", "xa", "xb", "kst0", "kst1", "vs8",
                                  "ost0", "ost1", "fx0", "fx1", "fst0", "fst1",
                                  "ws0", "ws1", "ws2", "ws3", "wv", "kv0", "kv1",
                                  "ofs", "opm", "dbg"]
        sems = {e: es.enter_context(nc.semaphore(f"sem_{e}")) for e in sem_names}

        sch = Sched()
        W_ = sch.op

        W_("sy", lambda h: h.dma_start(out=cb[:], in_=cb_d.ap()), inc="cs", inc_amt=16)
        W_("sy", lambda h: h.dma_start(out=cf[:], in_=cf_d.ap()), inc="cs", inc_amt=16)
        W_("sy", lambda h: h.dma_start(out=bvecs[:], in_=bvecs_d.ap()), inc="cs", inc_amt=16)
        W_("sy", lambda h: h.dma_start(out=bvo[:], in_=bvo_d.ap()), inc="cs", inc_amt=16)
        ck_consts = W_("sy", lambda h: h.dma_start(out=scales[:], in_=scales_d.ap()),
                       inc="cs", inc_amt=16)

        xt_ap = lambda b: BIGf[:, b * 2048:(b + 1) * 2048]
        xnb_ap = lambda b: BIGb[:, 8192 + b * 2048: 8192 + (b + 1) * 2048]

        ck_xnb = [0] * 8
        ck_tr_tile = [0] * 8
        last_evac = 0
        for t8 in range(8):
            b = t8 % 2
            xsem = "xa" if b == 0 else "xb"
            waits = [("ve", ck_xnb[t8 - 2])] if t8 >= 2 else [("cs", ck_consts)]
            ck_xl = W_("sy", (lambda t8=t8, b=b: lambda h: h.dma_start(
                out=xt_ap(b), in_=x_own.ap()[t8 * 128:(t8 + 1) * 128, :]))(),
                waits=waits, inc=xsem, inc_amt=16)

            def ve_ln(h, base, b=b):
                r = None
                for i in range(4):
                    r = h.bn_stats(stats[:, 6 * i:6 * i + 6], xt_ap(b)[:, 512 * i:512 * i + 512])
                r.then_inc(sems["ve"], 1)
                h.wait_ge(sems["ve"], base + 1)
                h.bn_aggr(mv[:], stats[:]).then_inc(sems["ve"], 1)
                h.wait_ge(sems["ve"], base + 2)
                h.tensor_scalar_add(sd[:], mv[:, 1:2], 1e-5).then_inc(sems["ve"], 1)
            ck_sd = sch.opsw("ve", ve_ln, waits=[(xsem, ck_xl)], n_incs=3)
            ck_sq = W_("se", lambda h: h.activation(out=sd[:], in_=sd[:], func=AF.Sqrt),
                       waits=[("ve", ck_sd)], inc="se")

            def ve_norm(h, base, b=b):
                h.reciprocal(rinv[:], sd[:]).then_inc(sems["ve"], 1)
                h.wait_ge(sems["ve"], base + 1)
                h.tensor_scalar(out=xnb_ap(b), in0=xt_ap(b),
                                scalar1=mv[:, 0:1], scalar2=rinv[:],
                                op0=ALU.subtract, op1=ALU.mult).then_inc(sems["ve"], 1)
            w_n = [("se", ck_sq)]
            if t8 >= 2:
                w_n.append(("te", ck_tr_tile[t8 - 2]))
            ck_xnb[t8] = sch.opsw("ve", ve_norm, waits=w_n, n_incs=2)

            for half in range(2):
                waits = [("ve", ck_xnb[t8])]
                if t8 == 0 and half == 0:
                    waits.append(("cs", ck_consts))
                if last_evac:
                    waits.append(("se", last_evac))

                def te_tr(h, b=b, half=half):
                    r = None
                    for c8 in range(8):
                        ch = half * 8 + c8
                        r = h.transpose(pTv[:, c8 * 128:(c8 + 1) * 128],
                                        xnb_ap(b)[:, ch * 128:(ch + 1) * 128], identb)
                    return r
                ck_tr = W_("te", te_tr, waits=waits, inc="te")
                ck_tr_tile[t8] = ck_tr

                def se_evac(h, t8=t8, half=half):
                    dst = ap_of(xnT, (half * 8) * TOK + t8 * 128,
                                [[NCH * TOK, 128], [TOK, 8], [1, 128]])
                    return h.copy(out=dst,
                                  in_=pTv[:, 0:1024].rearrange("p (a b) -> p a b", a=8))
                last_evac = W_("se", se_evac, waits=[("te", ck_tr)], inc="se")

        ck_xnT = last_evac
        if STOP == "A":
            return _finish(nc, es, sch, sems, dbg, xnT=xnT, qtT=qtT, ffnA=ffnA,
                           kvg=kvg, afg=afg, W_=W_, ck={"xnT": ck_xnT})

        wsb = lambda i: ws[i].ap().bitcast(BF16)

        def w_attn_load(h, Wt, ch, wi):
            src = ap_of(Wt, ch * 128 * D, [[D, 128], [1, D]])
            h.dma_start(out=wsb(wi)[:, 0:2048], in_=src).then_inc(sems[f"ws{wi}"], 16)
            return None

        ck_g = {}
        ck_ev = {}
        gidx = 0
        ck_w = {}
        for pi, (Wt, bcol, kind) in enumerate([(Wq, 0, "q"), (Wk, 16, "k")]):
            for ch in range(NCH):
                ci = pi * NCH + ch
                wi = ci % 2
                waits = [("te", ck_g[ci - 2])] if ci >= 2 else []
                ck_w[ci] = sch.opn("gp", (lambda Wt=Wt, ch=ch, wi=wi:
                                          lambda h: w_attn_load(h, Wt, ch, wi))(),
                                   waits=waits, bump=f"ws{wi}", amt=16)
                for half2 in range(2):
                    par = gidx % 2
                    waits = [(f"ws{wi}", ck_w[ci])]
                    if gidx == 0:
                        waits.append(("se", ck_xnT))
                        waits.append(("cs", ck_consts))
                    if gidx >= 2:
                        waits.append(("ve", ck_ev[gidx - 2]))

                    def te_proj(h, wi=wi, half2=half2, par=par):
                        r = None
                        for dch in range(NCH):
                            r = h.matmul(
                                pb[par][:], wsb(wi)[:, dch * 128:(dch + 1) * 128],
                                xnT[:, dch * TOK + half2 * 512: dch * TOK + half2 * 512 + 512],
                                start=(dch == 0), stop=(dch == NCH - 1))
                        return r
                    ck_g[ci] = W_("te", te_proj, waits=waits, inc="te")

                    if kind == "q":
                        def ve_evq(h, ch=ch, half2=half2, par=par):
                            return h.tensor_scalar(
                                out=qtT[:, ch * TOK + half2 * 512: ch * TOK + half2 * 512 + 512],
                                in0=pb[par][:], scalar1=bvecs[:, ch:ch + 1],
                                scalar2=None, op0=ALU.add)
                        ck_ev[gidx] = W_("ve", ve_evq, waits=[("te", ck_g[ci])], inc="ve")
                    else:
                        kb = gidx % 2
                        waits_e = [("te", ck_g[ci])]
                        if (gidx - 2, "kst") in ck_ev:
                            waits_e.append((f"kst{kb}", ck_ev[(gidx - 2, "kst")]))

                        def ve_evk(h, ch=ch, par=par, kb=kb):
                            return h.tensor_scalar(
                                out=kstg[kb][:], in0=pb[par][:],
                                scalar1=bvecs[:, 16 + ch:16 + ch + 1],
                                scalar2=None, op0=ALU.add)
                        ck_ev[gidx] = W_("ve", ve_evk, waits=waits_e, inc="ve")

                        def sy_kst(h, ch=ch, half2=half2, kb=kb):
                            dst = ap_of(kvb, ch * 128 * TOK + half2 * 512,
                                        [[TOK, 128], [1, 512]])
                            return h.dma_start(out=dst, in_=kstg[kb][:])
                        ck_ev[(gidx, "kst")] = W_("sy", sy_kst,
                                                  waits=[("ve", ck_ev[gidx])],
                                                  inc=f"kst{kb}", inc_amt=16)
                    gidx += 1
        ck_kst_fin = {0: sch.cnt["kst0"], 1: sch.cnt["kst1"]}
        if "qtT" in dbg:
            W_("sy", lambda h: h.dma_start(out=dbg["qtT"].ap(), in_=qtT[:]),
               waits=[("ve", ck_ev[31])], inc="dbg", inc_amt=16)

        wv_ap = lambda dch: BIGb[:, dch * 1024:(dch + 1) * 1024]
        vfull_ap = lambda t8: BIGb[:, 16384 + t8 * 2048: 16384 + (t8 + 1) * 2048]

        ck_wv = {}
        ck_v_h0_last = 0
        vgi = 0
        for vhalf in range(2):
            for dch in range(NCH):
                waits = []
                if vhalf == 0 and dch == 0:
                    waits.append(("se", ck_xnT))
                if vhalf == 1:
                    waits.append(("te", ck_v_h0_last))

                def gp_wv(h, dch=dch, vhalf=vhalf):
                    src = ap_of(Wv, dch * 128 * D + vhalf * 1024,
                                [[D, 128], [1, 1024]])
                    return h.dma_start(out=wv_ap(dch), in_=src)
                ck_wv[(vhalf, dch)] = W_("gp", gp_wv, waits=waits, inc="wv", inc_amt=16)
            for t8 in range(8):
                for g2 in range(2):
                    par = vgi % 2
                    waits = [("wv", ck_wv[(vhalf, NCH - 1)])]
                    if vgi >= 2:
                        waits.append(("ve", ck_ev[("v", vgi - 2)]))
                    else:
                        waits.append(("ve", ck_ev[gidx - 1]))

                    def te_v(h, t8=t8, g2=g2, par=par):
                        r = None
                        for dch in range(NCH):
                            r = h.matmul(
                                pb[par][:],
                                xnT[:, dch * TOK + t8 * 128: dch * TOK + t8 * 128 + 128],
                                wv_ap(dch)[:, g2 * 512:(g2 + 1) * 512],
                                start=(dch == 0), stop=(dch == NCH - 1))
                        return r
                    ck_te_v = W_("te", te_v, waits=waits, inc="te")
                    if vhalf == 0:
                        ck_v_h0_last = ck_te_v

                    def ve_evv(h, t8=t8, g2=g2, vhalf=vhalf, par=par):
                        return h.tensor_scalar(
                            out=vfull_ap(t8)[:, vhalf * 1024 + g2 * 512:
                                             vhalf * 1024 + g2 * 512 + 512],
                            in0=pb[par][:], scalar1=1.0, scalar2=None, op0=ALU.mult)
                    ck_ev[("v", vgi)] = W_("ve", ve_evv, waits=[("te", ck_te_v)], inc="ve")
                    vgi += 1

        ck_vst = 0
        for t8 in range(8):
            def sy_vst(h, t8=t8):
                dst = ap_of(kvb, D * TOK + t8 * 128 * D, [[D, 128], [1, D]])
                return h.dma_start(out=dst, in_=vfull_ap(t8))
            ck_vst = W_("sy", sy_vst, waits=[("ve", ck_ev[("v", 16 + t8 * 2 + 1)])],
                        inc="vs8", inc_amt=16)

        ck_ag1 = W_("gp", lambda h: h.collective_compute(
            "AllGather", ALU.bypass, replica_groups=[list(range(NCORES))],
            ins=[kvb.ap().opt()], outs=[kvg.ap().opt()]),
            waits=[("vs8", ck_vst), ("kst0", ck_kst_fin[0]), ("kst1", ck_kst_fin[1])],
            inc="cc")

        if STOP == "B":
            if "qtT" in dbg:
                pass
            if "kvg" in dbg:
                W_("gp", lambda h: h.dma_start(out=dbg["kvg"].ap(), in_=kvg.ap()),
                   waits=[("cc", ck_ag1)], inc="gp", inc_amt=16)
            return _finish(nc, es, sch, sems, dbg, xnT=xnT, qtT=qtT, ffnA=ffnA,
                           kvg=kvg, afg=afg, W_=W_, ck={})

        fuse_ap = lambda ich: BIGb[:, ich * 512:(ich + 1) * 512]

        def build_ffn(hb, out_slice, first_waits, tag):
            ck_wg = {}
            last_down_te = 0
            for ich in range(ICH):
                for gu, Wt in ((0, gw), (1, uw)):
                    wi = (ich % 2) * 2 + gu
                    waits = list(first_waits) if (ich == 0 and gu == 0) else []
                    if ich >= 2:
                        waits.append(("ve", ck_ev[(tag, "fuse", ich - 2)]))

                    def gp_w(h, Wt=Wt, ich=ich, wi=wi):
                        src = ap_of(Wt, ich * 128 * D, [[D, 128], [1, D]])
                        h.dma_start(out=ws[wi].ap()[:, 0:2048],
                                    in_=src).then_inc(sems[f"ws{wi}"], 16)
                        return None
                    ck_wg[(ich, gu)] = sch.opn("gp", gp_w, waits=waits, bump=f"ws{wi}", amt=16)

                parG, parU = (ich % 2), 2 + (ich % 2)
                wiG, wiU = (ich % 2) * 2, (ich % 2) * 2 + 1
                waits = [(f"ws{wiG}", ck_wg[(ich, 0)]), (f"ws{wiU}", ck_wg[(ich, 1)])]
                if ich >= 2:
                    waits.append(("ve", ck_ev[(tag, "gtf", ich - 2)]))
                    waits.append(("ve", ck_ev[(tag, "fuse", ich - 2)]))

                def te_gu(h, ich=ich, hb=hb, parG=parG, parU=parU):
                    wiG, wiU = (ich % 2) * 2, (ich % 2) * 2 + 1
                    r = None
                    for dch in range(NCH):
                        h.matmul(pb[parG][:], ws[wiG].ap()[:, dch * 128:(dch + 1) * 128],
                                 xnT[:, dch * TOK + hb * 512: dch * TOK + hb * 512 + 512],
                                 start=(dch == 0), stop=(dch == NCH - 1))
                    for dch in range(NCH):
                        r = h.matmul(pb[parU][:], ws[wiU].ap()[:, dch * 128:(dch + 1) * 128],
                                     xnT[:, dch * TOK + hb * 512: dch * TOK + hb * 512 + 512],
                                     start=(dch == 0), stop=(dch == NCH - 1))
                    return r
                ck_teg = W_("te", te_gu, waits=waits, inc="te")



                gb = ich % 2

                def ve_g(h, ich=ich, parG=parG, gb=gb):
                    return h.tensor_scalar(out=gtf[gb][:], in0=pb[parG][:],
                                           scalar1=scales[:, ich:ich + 1], scalar2=None,
                                           op0=ALU.mult)
                wv_g = [("te", ck_teg)]
                if ich >= 1:
                    wv_g.append(("se", ck_ev[(tag, "sil", ich - 1)]))
                ck_ev[(tag, "gtf", ich)] = W_("ve", ve_g, waits=wv_g, inc="ve")

                ws_s = [("ve", ck_ev[(tag, "gtf", ich)])]
                if ich >= 1:
                    ws_s.append(("ve", ck_ev[(tag, "fuse", ich - 1)]))
                ck_sil = W_("se", (lambda gb=gb: lambda h: h.activation(
                    out=gsil[gb][:], in_=gtf[gb][:], func=AF.Silu))(),
                    waits=ws_s, inc="se")
                ck_ev[(tag, "sil", ich)] = ck_sil

                import os as _os
                _fv = _os.environ.get("FUSEVAR", "")

                def ve_fuse(h, ich=ich, parU=parU, gb=gb, _fv=_fv):
                    if _fv == "psu":
                        return h.tensor_scalar(out=fuse_ap(ich), in0=pb[parU][:],
                                               scalar1=scales[:, 48 + ich:48 + ich + 1],
                                               scalar2=None, op0=ALU.mult)
                    if _fv == "gsil":
                        return h.tensor_scalar(out=fuse_ap(ich), in0=gsil[gb][:],
                                               scalar1=1.0, scalar2=None, op0=ALU.mult)
                    return h.scalar_tensor_tensor(out=fuse_ap(ich), in0=pb[parU][:],
                                                  scalar=scales[:, 48 + ich:48 + ich + 1],
                                                  in1=gsil[gb][:], op0=ALU.mult, op1=ALU.mult)
                ck_ev[(tag, "fuse", ich)] = W_("ve", ve_fuse, waits=[("se", ck_sil)], inc="ve")

            for dm in range(NCH):
                wi = dm % 2
                waits = [("ve", ck_ev[(tag, "fuse", ICH - 1)])] if dm < 2 else []
                if dm >= 2:
                    waits.append(("te", ck_ev[(tag, "dte", dm - 2)]))

                def gp_wd(h, dm=dm, wi=wi):
                    src = ap_of(dw, dm * 128 * II, [[II, 128], [1, II]])
                    h.dma_start(out=ws[wi].ap()[:, 0:6144],
                                in_=src).then_inc(sems[f"ws{wi}"], 16)
                    return None
                ck_wd = sch.opn("gp", gp_wd, waits=waits, bump=f"ws{wi}", amt=16)

                par = dm % 2
                waits = [(f"ws{wi}", ck_wd)]
                if dm >= 2:
                    waits.append(("ve", ck_ev[(tag, "down", dm - 2)]))

                def te_d(h, dm=dm, par=par, wi=wi):
                    r = None
                    for ick in range(ICH):
                        r = h.matmul(pb[par][:], ws[wi].ap()[:, ick * 128:(ick + 1) * 128],
                                     fuse_ap(ick), start=(ick == 0), stop=(ick == ICH - 1))
                    return r
                ck_ev[(tag, "dte", dm)] = W_("te", te_d, waits=waits, inc="te")

                def ve_d(h, dm=dm, par=par):
                    return h.tensor_scalar(out=out_slice(dm), in0=pb[par][:],
                                           scalar1=scales[:, 96 + dm:96 + dm + 1],
                                           scalar2=None, op0=ALU.mult)
                ck_ev[(tag, "down", dm)] = W_("ve", ve_d,
                                              waits=[("te", ck_ev[(tag, "dte", dm)])],
                                              inc="ve")
            return ck_ev[(tag, "down", NCH - 1)]

        ck_ffnA = build_ffn(0, lambda dm: ffnA[:, dm * 512:(dm + 1) * 512],
                            [("vs8", ck_vst), ("ve", ck_ev[("v", 31)])], "fA")
        if "ffnA" in dbg:
            W_("sy", lambda h: h.dma_start(out=dbg["ffnA"].ap(), in_=ffnA[:]),
               waits=[("ve", ck_ffnA)], inc="dbg", inc_amt=16)
        ck_dbg_fuse = None
        if "fuse" in dbg:
            ck_dbg_fuse = W_("sy", lambda h: h.dma_start(out=dbg["fuse"].ap(),
                                                         in_=BIGb[:, 0:ICH * 512]),
                             waits=[("ve", ck_ev[("fA", "fuse", ICH - 1)])], inc="dbg", inc_amt=16)


        if STOP == "FA":
            return _finish(nc, es, sch, sems, dbg, xnT=xnT, qtT=qtT, ffnA=ffnA,
                           kvg=kvg, afg=afg, W_=W_, ck={})

        kh_ap = lambda hl: BIGb[:, hl * 16384: hl * 16384 + 8192]
        vh_ap = lambda hl: BIGb[:, hl * 16384 + 8192: hl * 16384 + 16384]

        ck_kv = {}
        ck_ofs = W_("gp", lambda h: h.dma_start(out=ofs[:], in_=offs_d.ap()),
                    inc="ofs", inc_amt=16)
        for hl in range(2):
            waits = [("cc", ck_ag1), ("ofs", ck_ofs)]
            if hl == 0:
                waits.append(("ve", ck_ffnA))
                if "fuse" in dbg:
                    waits.append(("dbg", ck_dbg_fuse))

            def gp_kv(h, hl=hl):
                rk = dynreg(h)
                h.reg_load(rk, ofs[0:1, hl:hl + 1])
                src = bass.AP(kvg, rk,
                              [[TOK, 128], [2 * D * TOK, NCORES], [1, TOK]])
                h.dma_start(out=kh_ap(hl), in_=src).then_inc(sems[f"kv{hl}"], 16)
                for r in range(NCORES):
                    h.reg_load(rk, ofs[0:1, 4 + hl * 8 + r:5 + hl * 8 + r])
                    src = bass.AP(kvg, rk, [[D, 128], [128 * D, 8], [1, 128]])
                    h.dma_start(out=vh_ap(hl)[:, r * 1024:(r + 1) * 1024],
                                in_=src).then_inc(sems[f"kv{hl}"], 16)
                return None
            ck_kv[hl] = sch.opn("gp", gp_kv, waits=waits, bump=f"kv{hl}", amt=16 * 9)

        psL = {0: pb[6], 1: pb[7]}
        ck_qs = {}
        ck_exp = {}
        ck_lmm = {}
        ck_oev = {}
        last_s_te = {}
        for C in range(NCH):
            nst = 4 * (C + 1)
            for hh in range(2):
                waits = [(f"kv{hh}", ck_kv[hh])]
                if C == 0:
                    waits.append(("ve", ck_ffnA))
                else:
                    waits.append(("te", last_s_te[hh]))

                def ve_qs(h, hh=hh, C=C):
                    srcap = ap_of(qtT, hh * 512 + 32 * C,
                                  [[NCH * TOK, 128], [1, 32], [TOK, 16]])
                    return h.tensor_copy(qslot[hh][:], srcap)
                ck_qs[hh] = W_("ve", ve_qs, waits=waits, inc="ve")

            for stg in range(nst + 1):
                if stg < nst:
                    for hh in range(2):
                        st = stg
                        par_s = st % 2
                        waits = [("ve", ck_qs[hh])]
                        if st >= 2:
                            waits.append(("se", ck_exp[(hh, st - 2)]))
                        elif C >= 1:
                            waits.append(("se", ck_exp[(hh, "last")]))
                        elif st < 2:
                            waits.append(("ve", ck_ffnA))

                        def te_s(h, hh=hh, st=st, par_s=par_s):
                            return h.matmul(pb[2 * hh + par_s][:],
                                            kh_ap(hh)[:, st * 128:(st + 1) * 128],
                                            qslot[hh][:], start=True, stop=True)
                        ck_s = W_("te", te_s, waits=waits, inc="te")
                        if st == nst - 1:
                            last_s_te[hh] = ck_s

                        w_e = [("te", ck_s)]
                        if st >= 2:
                            w_e.append(("te", ck_lmm[(hh, st - 2)]))
                        elif C >= 1:
                            w_e.append(("te", ck_lmm[(hh, "last")]))

                        def se_exp(h, hh=hh, pt_b=2 * hh + st % 2, par_s=par_s):
                            return h.activation(out=ptile[pt_b][:],
                                                in_=pb[2 * hh + par_s][:],
                                                func=AF.Exp, bias=nf32(CEXP),
                                                scale=nf32(QSCALE))
                        ck_exp[(hh, st)] = W_("se", se_exp, waits=w_e, inc="se")

                        if st >= 4 * C:
                            def ve_mask(h, st=st, C=C, pt_b=2 * hh + st % 2):
                                mi = st - 4 * C
                                return h.tensor_tensor(out=ptile[pt_b][:],
                                                       in0=ptile[pt_b][:],
                                                       in1=maskb[:, mi * 512:(mi + 1) * 512],
                                                       op=ALU.mult)
                            ck_exp[(hh, st, "m")] = W_("ve", ve_mask,
                                                       waits=[("se", ck_exp[(hh, st)])],
                                                       inc="ve")
                if stg >= 1:
                    for hh in range(2):
                        st = stg - 1
                        pt_b = 2 * hh + st % 2
                        if st >= 4 * C:
                            av_waits = [("ve", ck_exp[(hh, st, "m")])]
                        else:
                            av_waits = [("se", ck_exp[(hh, st)])]
                        if st == 0 and C >= 1:
                            av_waits.append(("ve", ck_oev[hh]))

                        def te_av(h, hh=hh, st=st, pt_b=pt_b,
                                  first=(st == 0), last=(st == nst - 1)):
                            h.matmul(pb[4 + hh][:], vh_ap(hh)[:, st * 128:(st + 1) * 128],
                                     ptile[pt_b][:], start=first, stop=last)
                            return h.matmul(psL[hh][:], onesb, ptile[pt_b][:],
                                            start=first, stop=last)
                        ck_lmm[(hh, st)] = W_("te", te_av, waits=av_waits, inc="te")
            ck_exp[(0, "last")] = ck_exp[(0, nst - 1)]
            ck_exp[(1, "last")] = ck_exp[(1, nst - 1)]
            ck_lmm[(0, "last")] = ck_lmm[(0, nst - 1)]
            ck_lmm[(1, "last")] = ck_lmm[(1, nst - 1)]

            for hh in range(2):
                def ve_oev(h, base, hh=hh):
                    h.tensor_scalar(out=lrep[:], in0=psL[hh][:], scalar1=1.0,
                                    scalar2=None, op0=ALU.mult).then_inc(sems["ve"], 1)
                    h.wait_ge(sems["ve"], base + 1)
                    h.reciprocal(linv[:], lrep[:]).then_inc(sems["ve"], 1)
                    h.wait_ge(sems["ve"], base + 2)
                    h.scalar_tensor_tensor(out=otf[:], in0=pb[4 + hh][:], scalar=1.0,
                                           in1=linv[:], op0=ALU.mult, op1=ALU.mult
                                           ).then_inc(sems["ve"], 1)
                    h.wait_ge(sems["ve"], base + 3)
                    h.tensor_scalar(out=ostg[hh][:], in0=otf[:],
                                    scalar1=bvo[:, hh:hh + 1], scalar2=None,
                                    op0=ALU.add).then_inc(sems["ve"], 1)
                w_o = [("te", ck_lmm[(hh, "last")])]
                if C >= 1:
                    w_o.append((f"ost{hh}", ck_oev[(hh, "st")]))
                ck_oev[hh] = sch.opsw("ve", ve_oev, waits=w_o, n_incs=4)

                def sy_ost(h, hh=hh, C=C):
                    dst = ap_of(afb, hh * 128 * T + C * 512, [[T, 128], [1, 512]])
                    return h.dma_start(out=dst, in_=ostg[hh][:])
                ck_oev[(hh, "st")] = W_("sy", sy_ost, waits=[("ve", ck_oev[hh])],
                                        inc=f"ost{hh}", inc_amt=16)

        last_exp = max(ck_exp[(0, "last")], ck_exp[(1, "last")])
        ost_fin = (sch.cnt["ost0"], sch.cnt["ost1"])

        ck_ffnB = build_ffn(1, lambda dm: ffnB[:, dm * 512:(dm + 1) * 512],
                            [("ve", ck_oev[1]), ("ve", ck_oev[0]),
                             ("te", max(ck_lmm[(0, "last")], ck_lmm[(1, "last")])),
                             ("se", last_exp)], "fB")

        ck_ag2 = W_("gp", lambda h: h.collective_compute(
            "AllGather", ALU.bypass, replica_groups=[list(range(NCORES))],
            ins=[afb.ap().opt()], outs=[afg.ap().opt()]),
            waits=[("ost0", ost_fin[0]), ("ost1", ost_fin[1])], inc="cc")

        if STOP == "FB":
            return _finish(nc, es, sch, sems, dbg, xnT=xnT, qtT=qtT, ffnA=ffnA,
                           kvg=kvg, afg=afg, W_=W_, ck={})

        opm_ap = lambda hd: BIGb[:, hd * 1024:(hd + 1) * 1024]

        def gp_opm(h):
            ro = dynreg(h)
            h.reg_load(ro, ofs[0:1, 20:21])
            return h.dma_start(out=BIGb[:, 0:16384],
                               in_=bass.AP(afg, ro,
                                           [[T, 128], [128 * T, NCH], [1, TOK]]))
        ck_opm = W_("gp", gp_opm,
                    waits=[("cc", ck_ag2), ("ve", ck_ffnB)], inc="opm", inc_amt=16)

        ogi = 0
        ck_og = {}
        ck_oge = {}
        for dm in range(NCH):
            wi = dm % 2
            waits = [("ve", ck_ffnB)] if dm < 2 else [("te", ck_og[dm - 2])]
            ck_wo = sch.opn("gp", (lambda dm=dm, wi=wi:
                                   lambda h: w_attn_load(h, Wo, dm, wi))(),
                            waits=waits, bump=f"ws{wi}", amt=16)
            for half2 in range(2):
                par = ogi % 2
                waits = [(f"ws{wi}", ck_wo), ("opm", ck_opm)]
                if ogi >= 2:
                    waits.append(("ve", ck_oge[ogi - 2]))

                def te_o(h, half2=half2, par=par, wi=wi):
                    r = None
                    for hd in range(NCH):
                        r = h.matmul(pb[par][:], wsb(wi)[:, hd * 128:(hd + 1) * 128],
                                     opm_ap(hd)[:, half2 * 512:(half2 + 1) * 512],
                                     start=(hd == 0), stop=(hd == NCH - 1))
                    return r
                ck_og[dm] = W_("te", te_o, waits=waits, inc="te")

                def ve_op(h, dm=dm, half2=half2, par=par):
                    dst = (ffnA[:, dm * 512:(dm + 1) * 512] if half2 == 0
                           else ffnB[:, dm * 512:(dm + 1) * 512])
                    return h.scalar_tensor_tensor(out=dst, in0=pb[par][:],
                                                  scalar=bvecs[:, 32 + dm:32 + dm + 1],
                                                  in1=dst, op0=ALU.add, op1=ALU.add)
                ck_oge[ogi] = W_("ve", ve_op, waits=[("te", ck_og[dm])], inc="ve")
                ogi += 1
        ck_oproj = ck_oge[ogi - 1]

        fxt_ap = lambda b: BIGf[:, 8192 + b * 2048: 8192 + (b + 1) * 2048]
        fos_ap = lambda b: BIGf[:, 12288 + b * 2048: 12288 + (b + 1) * 2048]

        last_fadd = {0: 0, 1: 0}
        ck_fst = {}
        fti = 0
        for half2 in range(2):
            src_ffn = (lambda ch, t4, half2=half2:
                       (ffnA if half2 == 0 else ffnB)[:, ch * 512 + t4 * 128:
                                                      ch * 512 + t4 * 128 + 128])
            for t4 in range(4):
                b = fti % 2
                waits = [("ve", ck_ffnB)]
                if fti >= 2:
                    waits.append(("ve", ck_fst[("e", fti - 2)]))
                    waits.append((f"fst{b}", ck_fst[fti - 2]))

                def sy_fx(h, half2=half2, t4=t4, b=b):
                    r0 = half2 * 512 + t4 * 128
                    return h.dma_start(out=fxt_ap(b), in_=x_own.ap()[r0:r0 + 128, :])
                ck_fx = W_("sy", sy_fx, waits=waits, inc=f"fx{b}", inc_amt=16)

                for grp in range(4):
                    par = grp % 2
                    waits = [("ve", ck_oproj)]
                    if last_fadd[par]:
                        waits.append(("ve", last_fadd[par]))

                    def te_ftr(h, src_ffn=src_ffn, t4=t4, grp=grp, par=par):
                        r = None
                        for k in range(4):
                            r = h.transpose(pb[par][:, k * 128:(k + 1) * 128],
                                            src_ffn(grp * 4 + k, t4), cf[:])
                        return r
                    ck_tr = W_("te", te_ftr, waits=waits, inc="te")

                    def ve_fadd(h, b=b, grp=grp, par=par):
                        return h.tensor_tensor(out=fos_ap(b)[:, grp * 512:(grp + 1) * 512],
                                               in0=pb[par][:],
                                               in1=fxt_ap(b)[:, grp * 512:(grp + 1) * 512],
                                               op=ALU.add)
                    last_fadd[par] = W_("ve", ve_fadd,
                                        waits=[("te", ck_tr), (f"fx{b}", ck_fx)], inc="ve")
                ck_fst[("e", fti)] = last_fadd[3 % 2]

                def sy_fst(h, half2=half2, t4=t4, b=b):
                    r0 = half2 * 512 + t4 * 128
                    return h.dma_start(out=out_own.ap()[r0:r0 + 128, :], in_=fos_ap(b))
                ck_fst[fti] = W_("sy", sy_fst,
                                 waits=[("ve", last_fadd[0]), ("ve", last_fadd[1])],
                                 inc=f"fst{b}", inc_amt=16)
                fti += 1

        if "xnT" in dbg:
            W_("sy", lambda h: h.dma_start(out=dbg["xnT"].ap(), in_=xnT[:]),
               waits=[("se", ck_xnT)], inc="dbg", inc_amt=16)
        if "kvg" in dbg:
            W_("gp", lambda h: h.dma_start(out=dbg["kvg"].ap(), in_=kvg.ap()),
               waits=[("cc", ck_ag1)], inc="dbg", inc_amt=16)
        if "afg" in dbg:
            W_("gp", lambda h: h.dma_start(out=dbg["afg"].ap(), in_=afg.ap()),
               waits=[("cc", ck_ag2)], inc="dbg", inc_amt=16)

        W_("sy", lambda h: None, waits=[("fst0", sch.cnt["fst0"]),
                                        ("fst1", sch.cnt["fst1"]),
                                        ("sy", sch.cnt["sy"]), ("dbg", sch.cnt["dbg"])])
        W_("gp", lambda h: None, waits=[("gp", sch.cnt["gp"])])

        block = es.enter_context(nc.Block())

        def emit(eng_name, h):
            for waits, (inc, amt), fn in sch.ops[eng_name]:
                for s, v in waits:
                    h.wait_ge(sems[s], v)
                inst = fn(h)
                if inc is not None:
                    assert inst is not None, f"{eng_name} op with inc has no instruction"
                    inst.then_inc(sems[inc], amt)

        @block.sync
        def _(h):
            emit("sy", h)

        @block.gpsimd
        def _(h):
            emit("gp", h)

        @block.tensor
        def _(h):
            emit("te", h)

        @block.vector
        def _(h):
            emit("ve", h)

        @block.scalar
        def _(h):
            emit("se", h)

    return nc


_BF = ml_dtypes.bfloat16


def _consts():
    ident = np.eye(128, dtype=_BF)
    ones = np.ones((128, 128), dtype=_BF)
    masks = np.zeros((128, 4 * 512), dtype=_BF)
    for d in range(4):
        m = (np.arange(512)[None, :] >= (128 * d + np.arange(128))[:, None])
        masks[:, d * 512:(d + 1) * 512] = m.astype(_BF)
    cb = np.concatenate([ident, ones, masks], axis=1)
    cf = np.eye(128, dtype=np.float32)
    return np.ascontiguousarray(cb), np.ascontiguousarray(cf)


def kernel(L0_attn_x, L0_attn_Wq, L0_attn_bq, L0_attn_Wk, L0_attn_bk, L0_attn_Wv,
           L0_attn_bv, L0_attn_Wo, L0_attn_bo, L0_ffn_gate_w, L0_ffn_up_w,
           L0_ffn_down_w, L0_ffn_gate_scale, L0_ffn_up_scale, L0_ffn_down_scale):
    x = np.ascontiguousarray(np.asarray(L0_attn_x, dtype=np.float32).reshape(T, D))
    cbv, cfv = _consts()
    bvecs = np.ascontiguousarray(np.concatenate([
        np.asarray(L0_attn_bq, np.float32).reshape(NCH, 128).T,
        np.asarray(L0_attn_bk, np.float32).reshape(NCH, 128).T,
        np.asarray(L0_attn_bo, np.float32).reshape(NCH, 128).T], axis=1))
    bv_all = np.asarray(L0_attn_bv, np.float32).reshape(NCH, 128).T
    scales = np.ascontiguousarray(np.concatenate([
        np.asarray(L0_ffn_gate_scale, np.float32).reshape(ICH, 128).T,
        np.asarray(L0_ffn_up_scale, np.float32).reshape(ICH, 128).T,
        np.asarray(L0_ffn_down_scale, np.float32).reshape(NCH, 128).T], axis=1))

    def chunked(w, nin, nout):
        a = np.asarray(w, np.float32).reshape(nin, 128, nout, 128)
        return np.ascontiguousarray(a.transpose(2, 1, 0, 3))

    shared = {
        "Wq": chunked(L0_attn_Wq, NCH, NCH),
        "Wk": chunked(L0_attn_Wk, NCH, NCH),
        "Wv": np.ascontiguousarray(np.asarray(L0_attn_Wv, np.float32)),
        "Wo": chunked(L0_attn_Wo, NCH, NCH),
        "gw": chunked(L0_ffn_gate_w, NCH, ICH),
        "uw": chunked(L0_ffn_up_w, NCH, ICH),
        "dw": chunked(L0_ffn_down_w, ICH, NCH),
        "bvecs": bvecs, "scales": scales, "constsb": cbv, "constsf": cfv,
    }

    in_maps = []
    for c in range(NCORES):
        h0, h1 = 2 * c, 2 * c + 1
        offs = np.zeros((1, 24), dtype=np.int32)
        offs[0, 0] = h0 * 128 * TOK
        offs[0, 1] = h1 * 128 * TOK
        for hli, hh in ((0, h0), (1, h1)):
            for r in range(NCORES):
                offs[0, 4 + hli * 8 + r] = r * 2 * D * TOK + D * TOK + hh * 128
        offs[0, 20] = c * TOK
        m = dict(shared)
        m["x_own"] = np.ascontiguousarray(x[c * TOK:(c + 1) * TOK])
        m["bvo"] = np.ascontiguousarray(bv_all[:, [h0, h1]])
        m["offs"] = offs
        in_maps.append(m)

    nc = build()
    res = run_bass_kernel_spmd(nc, in_maps, core_ids=list(range(NCORES)))
    out = np.concatenate([r["out_own"] for r in res.results], axis=0)
    if DEBUG:
        kernel._debug = res.results
    return out.reshape(1, T, D)


# version 13
# speedup vs baseline: 1.8001x; 1.0332x over previous
import contextlib

import numpy as np
import ml_dtypes

import concourse.bass as bass
import concourse.mybir as mybir
from concourse.bass_utils import run_bass_kernel_spmd

F32 = mybir.dt.float32
BF16 = mybir.dt.bfloat16
FP8 = mybir.dt.float8e4
I32 = mybir.dt.int32
AF = mybir.ActivationFunctionType
ALU = mybir.AluOpType
nf32 = np.float32

T, D, H, DH, SBK, II = 8192, 2048, 16, 128, 512, 6144
NCORES = 8
TOK = 1024
NCH = D // 128
ICH = II // 128
CEXP = -24.0
QSCALE = float(1.0 / np.sqrt(DH))

DEBUG = []
STOP = None

ENGS = ("sy", "gp", "te", "ve", "se")


class Sched:

    def __init__(self):
        self.ops = {e: [] for e in ENGS}
        import collections
        self.cnt = collections.defaultdict(int)

    def op(self, eng, fn, waits=(), inc=None, inc_amt=1):
        waits = [w for w in waits if w is not None]
        self.ops[eng].append((tuple(waits), (inc, inc_amt), fn))
        if inc is not None:
            self.cnt[inc] += inc_amt
            return self.cnt[inc]
        return None

    def opsw(self, eng, fn, waits=(), n_incs=1):
        waits = [w for w in waits if w is not None]
        base = self.cnt[eng]
        self.cnt[eng] += n_incs
        self.ops[eng].append((tuple(waits), (None, 0),
                              (lambda fn=fn, base=base: lambda h: fn(h, base))()))
        return self.cnt[eng]

    def opn(self, eng, fn, waits=(), bump=None, amt=0):
        waits = [w for w in waits if w is not None]
        self.ops[eng].append((tuple(waits), (None, 0), fn))
        self.cnt[bump] += amt
        return self.cnt[bump]


def _finish(nc, es, sch, sems, dbg, **kw):
    W_ = kw["W_"]
    W_("sy", lambda h: None, waits=[("sy", sch.cnt["sy"])])
    W_("gp", lambda h: None, waits=[("gp", sch.cnt["gp"])])
    block = es.enter_context(nc.Block())

    def emit(eng_name, h):
        for waits, (inc, amt), fn in sch.ops[eng_name]:
            for s, v in waits:
                h.wait_ge(sems[s], v)
            inst = fn(h)
            if inc is not None:
                assert inst is not None
                inst.then_inc(sems[inc], amt)

    @block.sync
    def _(h):
        emit("sy", h)

    @block.gpsimd
    def _(h):
        emit("gp", h)

    @block.tensor
    def _(h):
        emit("te", h)

    @block.vector
    def _(h):
        emit("ve", h)

    @block.scalar
    def _(h):
        emit("se", h)
    return nc


def ap_of(t, offset, dims):
    return bass.AP(t, offset, [list(d) for d in dims])


def dynreg(h):
    if not hasattr(h, "_dynreg"):
        h._dynreg = h.alloc_register("dynoff")
    return h._dynreg


def build():
    nc = bass.Bass("TRN2", target_bir_lowering=False, detect_race_conditions=False)

    x_own = nc.dram_tensor("x_own", [TOK, D], F32, kind="ExternalInput")
    Wq = nc.dram_tensor("Wq", [NCH, 128, NCH, 128], F32, kind="ExternalInput")
    Wk = nc.dram_tensor("Wk", [NCH, 128, NCH, 128], F32, kind="ExternalInput")
    Wv = nc.dram_tensor("Wv", [D, D], F32, kind="ExternalInput")
    Wo = nc.dram_tensor("Wo", [NCH, 128, NCH, 128], F32, kind="ExternalInput")
    gw = nc.dram_tensor("gw", [ICH, 128, NCH, 128], F32, kind="ExternalInput")
    uw = nc.dram_tensor("uw", [ICH, 128, NCH, 128], F32, kind="ExternalInput")
    dw = nc.dram_tensor("dw", [NCH, 128, ICH, 128], F32, kind="ExternalInput")
    bvecs_d = nc.dram_tensor("bvecs", [128, 48], F32, kind="ExternalInput")
    bvo_d = nc.dram_tensor("bvo", [128, 2], F32, kind="ExternalInput")
    scales_d = nc.dram_tensor("scales", [128, 112], F32, kind="ExternalInput")
    cb_d = nc.dram_tensor("constsb", [128, 2304], BF16, kind="ExternalInput")
    cf_d = nc.dram_tensor("constsf", [128, 128], F32, kind="ExternalInput")
    offs_d = nc.dram_tensor("offs", [1, 24], I32, kind="ExternalInput")
    out_own = nc.dram_tensor("out_own", [TOK, D], F32, kind="ExternalOutput")

    kvb = nc.dram_tensor("kvb", [2, D, TOK], BF16)
    kvg = nc.dram_tensor("kvg", [2 * NCORES, D, TOK], BF16, addr_space="Shared")
    afb = nc.dram_tensor("afb", [256, T], BF16)
    afg = nc.dram_tensor("afg", [D, T], BF16, addr_space="Shared")

    dbg = {}
    for name, shp, dt in [("ws3", [128, 6144], FP8),
                          ("fuse", [128, ICH * SBK], BF16),
                          ("gtf", [128, 512], F32),
                          ("xnT", [128, NCH * TOK], BF16),
                          ("qtT", [128, NCH * TOK], BF16),
                          ("kvg", [2 * NCORES, D, TOK], BF16),
                          ("ffnA", [128, NCH * SBK], F32),
                          ("afg", [D, T], BF16)]:
        if name in DEBUG:
            dbg[name] = nc.dram_tensor("dbg_" + name, shp, dt, kind="ExternalOutput")

    es = contextlib.ExitStack()
    with es:
        S = lambda n, shp, d: es.enter_context(nc.sbuf_tensor(n, shp, d))
        P = lambda n, shp, d: es.enter_context(nc.psum_tensor(n, shp, d))

        xnT = S("xnT", [128, NCH * TOK], BF16)
        qtT = S("qtT", [128, NCH * TOK], BF16)
        ffnA = S("ffnA", [128, NCH * SBK], F32)
        BIG = S("BIG", [128, 65536], FP8)
        ws = [S(f"ws{i}", [128, 6144], FP8) for i in range(4)]
        cb = S("cb", [128, 2304], BF16)
        cf = S("cf", [128, 128], F32)
        bvecs = S("bvecs_s", [128, 48], F32)
        bvo = S("bvo_s", [128, 2], F32)
        scales = S("scales_s", [128, 112], F32)
        stats = S("stats", [128, 24], F32)
        ofs = S("ofs", [1, 24], I32)
        mv = S("mv", [128, 2], F32)
        sd = S("sd", [128, 1], F32)
        rinv = S("rinv", [128, 1], F32)
        qslot = [S(f"qslot{i}", [128, 512], BF16) for i in range(2)]
        ptile = [S(f"ptile{i}", [128, 512], BF16) for i in range(4)]
        lrep = S("lrep", [128, 512], F32)
        linv = S("linv", [128, 512], F32)
        otf = lrep
        ostg = [S(f"ostg{i}", [128, 512], BF16) for i in range(2)]
        kstg = ostg
        gtf = [S("gtf0", [128, 512], F32)] * 2
        gsil = [S("gsil0", [128, 512], F32)] * 2
        dbg_psu = None

        pb = [P(f"pb{i}", [128, 512], F32) for i in range(8)]
        pTv = pb[7].ap().bitcast(BF16)

        BIGb = BIG.ap().bitcast(BF16)
        BIGf = BIG.ap().bitcast(F32)
        ffnB = qtT.ap().bitcast(F32)
        identb = cb.ap()[:, 0:128]
        onesb = cb.ap()[:, 128:256]
        maskb = cb.ap()[:, 256:2304]

        sem_names = list(ENGS) + ["cc", "cs", "xa", "xb", "kst0", "kst1", "vs8",
                                  "ost0", "ost1", "fx0", "fx1", "fst0", "fst1",
                                  "ws0", "ws1", "ws2", "ws3", "wv", "kv0", "kv1",
                                  "ofs", "opm", "dbg",
                                  "dwA0", "dwA1", "dwB0", "dwB1"]
        sems = {e: es.enter_context(nc.semaphore(f"sem_{e}")) for e in sem_names}

        sch = Sched()
        W_ = sch.op

        W_("sy", lambda h: h.dma_start(out=cb[:], in_=cb_d.ap()), inc="cs", inc_amt=16)
        W_("sy", lambda h: h.dma_start(out=cf[:], in_=cf_d.ap()), inc="cs", inc_amt=16)
        W_("sy", lambda h: h.dma_start(out=bvecs[:], in_=bvecs_d.ap()), inc="cs", inc_amt=16)
        W_("sy", lambda h: h.dma_start(out=bvo[:], in_=bvo_d.ap()), inc="cs", inc_amt=16)
        ck_consts = W_("sy", lambda h: h.dma_start(out=scales[:], in_=scales_d.ap()),
                       inc="cs", inc_amt=16)

        xt_ap = lambda b: BIGf[:, b * 2048:(b + 1) * 2048]
        xnb_ap = lambda b: BIGb[:, 8192 + b * 2048: 8192 + (b + 1) * 2048]

        ck_xnb = [0] * 8
        ck_tr_tile = [0] * 8
        last_evac = 0
        for t8 in range(8):
            b = t8 % 2
            xsem = "xa" if b == 0 else "xb"
            waits = [("ve", ck_xnb[t8 - 2])] if t8 >= 2 else [("cs", ck_consts)]
            ck_xl = W_("sy", (lambda t8=t8, b=b: lambda h: h.dma_start(
                out=xt_ap(b), in_=x_own.ap()[t8 * 128:(t8 + 1) * 128, :]))(),
                waits=waits, inc=xsem, inc_amt=16)

            def ve_ln(h, base, b=b):
                r = None
                for i in range(4):
                    r = h.bn_stats(stats[:, 6 * i:6 * i + 6], xt_ap(b)[:, 512 * i:512 * i + 512])
                r.then_inc(sems["ve"], 1)
                h.wait_ge(sems["ve"], base + 1)
                h.bn_aggr(mv[:], stats[:]).then_inc(sems["ve"], 1)
                h.wait_ge(sems["ve"], base + 2)
                h.tensor_scalar_add(sd[:], mv[:, 1:2], 1e-5).then_inc(sems["ve"], 1)
            ck_sd = sch.opsw("ve", ve_ln, waits=[(xsem, ck_xl)], n_incs=3)
            ck_sq = W_("se", lambda h: h.activation(out=sd[:], in_=sd[:], func=AF.Sqrt),
                       waits=[("ve", ck_sd)], inc="se")

            def ve_norm(h, base, b=b):
                h.reciprocal(rinv[:], sd[:]).then_inc(sems["ve"], 1)
                h.wait_ge(sems["ve"], base + 1)
                h.tensor_scalar(out=xnb_ap(b), in0=xt_ap(b),
                                scalar1=mv[:, 0:1], scalar2=rinv[:],
                                op0=ALU.subtract, op1=ALU.mult).then_inc(sems["ve"], 1)
            w_n = [("se", ck_sq)]
            if t8 >= 2:
                w_n.append(("te", ck_tr_tile[t8 - 2]))
            ck_xnb[t8] = sch.opsw("ve", ve_norm, waits=w_n, n_incs=2)

            for half in range(2):
                waits = [("ve", ck_xnb[t8])]
                if t8 == 0 and half == 0:
                    waits.append(("cs", ck_consts))
                if last_evac:
                    waits.append(("se", last_evac))

                def te_tr(h, b=b, half=half):
                    r = None
                    for c8 in range(8):
                        ch = half * 8 + c8
                        r = h.transpose(pTv[:, c8 * 128:(c8 + 1) * 128],
                                        xnb_ap(b)[:, ch * 128:(ch + 1) * 128], identb)
                    return r
                ck_tr = W_("te", te_tr, waits=waits, inc="te")
                ck_tr_tile[t8] = ck_tr

                def se_evac(h, t8=t8, half=half):
                    dst = ap_of(xnT, (half * 8) * TOK + t8 * 128,
                                [[NCH * TOK, 128], [TOK, 8], [1, 128]])
                    return h.copy(out=dst,
                                  in_=pTv[:, 0:1024].rearrange("p (a b) -> p a b", a=8))
                last_evac = W_("se", se_evac, waits=[("te", ck_tr)], inc="se")

        ck_xnT = last_evac
        if STOP == "A":
            return _finish(nc, es, sch, sems, dbg, xnT=xnT, qtT=qtT, ffnA=ffnA,
                           kvg=kvg, afg=afg, W_=W_, ck={"xnT": ck_xnT})

        wsb = lambda i: ws[i].ap().bitcast(BF16)

        def w_attn_load(h, Wt, ch, wi):
            src = ap_of(Wt, ch * 128 * D, [[D, 128], [1, D]])
            h.dma_start(out=wsb(wi)[:, 0:2048], in_=src).then_inc(sems[f"ws{wi}"], 16)
            return None

        ck_g = {}
        ck_ev = {}
        gidx = 0
        ck_w = {}
        for pi, (Wt, bcol, kind) in enumerate([(Wq, 0, "q"), (Wk, 16, "k")]):
            for ch in range(NCH):
                ci = pi * NCH + ch
                wi = ci % 2
                waits = [("te", ck_g[ci - 2])] if ci >= 2 else []
                ck_w[ci] = sch.opn("gp", (lambda Wt=Wt, ch=ch, wi=wi:
                                          lambda h: w_attn_load(h, Wt, ch, wi))(),
                                   waits=waits, bump=f"ws{wi}", amt=16)
                for half2 in range(2):
                    par = gidx % 2
                    waits = [(f"ws{wi}", ck_w[ci])]
                    if gidx == 0:
                        waits.append(("se", ck_xnT))
                        waits.append(("cs", ck_consts))
                    if gidx >= 2:
                        waits.append(("ve", ck_ev[gidx - 2]))

                    def te_proj(h, wi=wi, half2=half2, par=par):
                        r = None
                        for dch in range(NCH):
                            r = h.matmul(
                                pb[par][:], wsb(wi)[:, dch * 128:(dch + 1) * 128],
                                xnT[:, dch * TOK + half2 * 512: dch * TOK + half2 * 512 + 512],
                                start=(dch == 0), stop=(dch == NCH - 1))
                        return r
                    ck_g[ci] = W_("te", te_proj, waits=waits, inc="te")

                    if kind == "q":
                        def ve_evq(h, ch=ch, half2=half2, par=par):
                            return h.tensor_scalar(
                                out=qtT[:, ch * TOK + half2 * 512: ch * TOK + half2 * 512 + 512],
                                in0=pb[par][:], scalar1=bvecs[:, ch:ch + 1],
                                scalar2=None, op0=ALU.add)
                        ck_ev[gidx] = W_("ve", ve_evq, waits=[("te", ck_g[ci])], inc="ve")
                    else:
                        kb = gidx % 2
                        waits_e = [("te", ck_g[ci])]
                        if (gidx - 2, "kst") in ck_ev:
                            waits_e.append((f"kst{kb}", ck_ev[(gidx - 2, "kst")]))

                        def ve_evk(h, ch=ch, par=par, kb=kb):
                            return h.tensor_scalar(
                                out=kstg[kb][:], in0=pb[par][:],
                                scalar1=bvecs[:, 16 + ch:16 + ch + 1],
                                scalar2=None, op0=ALU.add)
                        ck_ev[gidx] = W_("ve", ve_evk, waits=waits_e, inc="ve")

                        def sy_kst(h, ch=ch, half2=half2, kb=kb):
                            dst = ap_of(kvb, ch * 128 * TOK + half2 * 512,
                                        [[TOK, 128], [1, 512]])
                            return h.dma_start(out=dst, in_=kstg[kb][:])
                        ck_ev[(gidx, "kst")] = W_("sy", sy_kst,
                                                  waits=[("ve", ck_ev[gidx])],
                                                  inc=f"kst{kb}", inc_amt=16)
                    gidx += 1
        ck_kst_fin = {0: sch.cnt["kst0"], 1: sch.cnt["kst1"]}
        if "qtT" in dbg:
            W_("sy", lambda h: h.dma_start(out=dbg["qtT"].ap(), in_=qtT[:]),
               waits=[("ve", ck_ev[31])], inc="dbg", inc_amt=16)

        wv_ap = lambda dch: BIGb[:, dch * 1024:(dch + 1) * 1024]
        vfull_ap = lambda t8: BIGb[:, 16384 + t8 * 2048: 16384 + (t8 + 1) * 2048]

        ck_wv = {}
        ck_v_h0_last = 0
        vgi = 0
        for vhalf in range(2):
            for dch in range(NCH):
                waits = []
                if vhalf == 0 and dch == 0:
                    waits.append(("se", ck_xnT))
                if vhalf == 1:
                    waits.append(("te", ck_v_h0_last))

                def gp_wv(h, dch=dch, vhalf=vhalf):
                    src = ap_of(Wv, dch * 128 * D + vhalf * 1024,
                                [[D, 128], [1, 1024]])
                    return h.dma_start(out=wv_ap(dch), in_=src)
                ck_wv[(vhalf, dch)] = W_("gp", gp_wv, waits=waits, inc="wv", inc_amt=16)
            for t8 in range(8):
                for g2 in range(2):
                    par = vgi % 2
                    waits = [("wv", ck_wv[(vhalf, NCH - 1)])]
                    if vgi >= 2:
                        waits.append(("ve", ck_ev[("v", vgi - 2)]))
                    else:
                        waits.append(("ve", ck_ev[gidx - 1]))

                    def te_v(h, t8=t8, g2=g2, par=par):
                        r = None
                        for dch in range(NCH):
                            r = h.matmul(
                                pb[par][:],
                                xnT[:, dch * TOK + t8 * 128: dch * TOK + t8 * 128 + 128],
                                wv_ap(dch)[:, g2 * 512:(g2 + 1) * 512],
                                start=(dch == 0), stop=(dch == NCH - 1))
                        return r
                    ck_te_v = W_("te", te_v, waits=waits, inc="te")
                    if vhalf == 0:
                        ck_v_h0_last = ck_te_v

                    def ve_evv(h, t8=t8, g2=g2, vhalf=vhalf, par=par):
                        return h.tensor_scalar(
                            out=vfull_ap(t8)[:, vhalf * 1024 + g2 * 512:
                                             vhalf * 1024 + g2 * 512 + 512],
                            in0=pb[par][:], scalar1=1.0, scalar2=None, op0=ALU.mult)
                    ck_ev[("v", vgi)] = W_("ve", ve_evv, waits=[("te", ck_te_v)], inc="ve")
                    vgi += 1

        ck_vst = 0
        for t8 in range(8):
            def sy_vst(h, t8=t8):
                dst = ap_of(kvb, D * TOK + t8 * 128 * D, [[D, 128], [1, D]])
                return h.dma_start(out=dst, in_=vfull_ap(t8))
            ck_vst = W_("sy", sy_vst, waits=[("ve", ck_ev[("v", 16 + t8 * 2 + 1)])],
                        inc="vs8", inc_amt=16)

        def ag1_hook():
            return W_("gp", lambda h: h.collective_compute(
                "AllGather", ALU.bypass, replica_groups=[list(range(NCORES))],
                ins=[kvb.ap().opt()], outs=[kvg.ap().opt()]),
                waits=[("vs8", ck_vst), ("kst0", ck_kst_fin[0]),
                       ("kst1", ck_kst_fin[1])], inc="cc")

        if STOP == "B":
            ck_ag1 = ag1_hook()
            if "qtT" in dbg:
                pass
            if "kvg" in dbg:
                W_("gp", lambda h: h.dma_start(out=dbg["kvg"].ap(), in_=kvg.ap()),
                   waits=[("cc", ck_ag1)], inc="gp", inc_amt=16)
            return _finish(nc, es, sch, sems, dbg, xnT=xnT, qtT=qtT, ffnA=ffnA,
                           kvg=kvg, afg=afg, W_=W_, ck={})

        fuse_ap = lambda ich: BIGb[:, ich * 512:(ich + 1) * 512]

        def build_ffn(hb, out_slice, tag, load_waits, te0_waits, fuse32_waits,
                      dwp_waits, dm_hook):
            tg = tag[1]
            ck_wg = {}
            ck_dwp = {}
            for ich in range(ICH):
                for gu, Wt in ((0, gw), (1, uw)):
                    wi = (ich % 2) * 2 + gu
                    waits = list(load_waits.get((ich, gu), ())) if ich < 2 else []
                    if ich >= 2:
                        waits.append(("ve", ck_ev[(tag, "fuse", ich - 2)]))

                    def gp_w(h, Wt=Wt, ich=ich, wi=wi):
                        src = ap_of(Wt, ich * 128 * D, [[D, 128], [1, D]])
                        h.dma_start(out=ws[wi].ap()[:, 0:2048],
                                    in_=src).then_inc(sems[f"ws{wi}"], 16)
                        return None
                    ck_wg[(ich, gu)] = sch.opn("gp", gp_w, waits=waits, bump=f"ws{wi}", amt=16)

                if ich == 12:
                    for j in range(2):
                        def gp_dwp(h, j=j, tg=tg):
                            src = ap_of(dw, j * 128 * II, [[II, 128], [1, II]])
                            h.dma_start(out=BIG.ap()[:, 49152 + j * 6144:
                                                     49152 + (j + 1) * 6144],
                                        in_=src).then_inc(sems[f"dw{tg}{j}"], 16)
                            return None
                        ck_dwp[j] = sch.opn("gp", gp_dwp,
                                            waits=(list(dwp_waits) if j == 0 else []),
                                            bump=f"dw{tg}{j}", amt=16)

                parG, parU = (ich % 2), 2 + (ich % 2)
                wiG, wiU = (ich % 2) * 2, (ich % 2) * 2 + 1
                waits = [(f"ws{wiG}", ck_wg[(ich, 0)]), (f"ws{wiU}", ck_wg[(ich, 1)])]
                if ich == 0:
                    waits.extend(te0_waits)
                if ich >= 2:
                    waits.append(("ve", ck_ev[(tag, "gtf", ich - 2)]))
                    waits.append(("ve", ck_ev[(tag, "fuse", ich - 2)]))

                def te_gu(h, ich=ich, hb=hb, parG=parG, parU=parU):
                    wiG, wiU = (ich % 2) * 2, (ich % 2) * 2 + 1
                    r = None
                    for dch in range(NCH):
                        h.matmul(pb[parG][:], ws[wiG].ap()[:, dch * 128:(dch + 1) * 128],
                                 xnT[:, dch * TOK + hb * 512: dch * TOK + hb * 512 + 512],
                                 start=(dch == 0), stop=(dch == NCH - 1))
                    for dch in range(NCH):
                        r = h.matmul(pb[parU][:], ws[wiU].ap()[:, dch * 128:(dch + 1) * 128],
                                     xnT[:, dch * TOK + hb * 512: dch * TOK + hb * 512 + 512],
                                     start=(dch == 0), stop=(dch == NCH - 1))
                    return r
                ck_teg = W_("te", te_gu, waits=waits, inc="te")
                ck_ev[(tag, "teg", ich)] = ck_teg



                gb = ich % 2

                def ve_g(h, ich=ich, parG=parG, gb=gb):
                    return h.tensor_scalar(out=gtf[gb][:], in0=pb[parG][:],
                                           scalar1=scales[:, ich:ich + 1], scalar2=None,
                                           op0=ALU.mult)
                wv_g = [("te", ck_teg)]
                if ich >= 1:
                    wv_g.append(("se", ck_ev[(tag, "sil", ich - 1)]))
                ck_ev[(tag, "gtf", ich)] = W_("ve", ve_g, waits=wv_g, inc="ve")

                ws_s = [("ve", ck_ev[(tag, "gtf", ich)])]
                if ich >= 1:
                    ws_s.append(("ve", ck_ev[(tag, "fuse", ich - 1)]))
                ck_sil = W_("se", (lambda gb=gb: lambda h: h.activation(
                    out=gsil[gb][:], in_=gtf[gb][:], func=AF.Silu))(),
                    waits=ws_s, inc="se")
                ck_ev[(tag, "sil", ich)] = ck_sil

                import os as _os
                _fv = _os.environ.get("FUSEVAR", "")

                def ve_fuse(h, ich=ich, parU=parU, gb=gb, _fv=_fv):
                    if _fv == "psu":
                        return h.tensor_scalar(out=fuse_ap(ich), in0=pb[parU][:],
                                               scalar1=scales[:, 48 + ich:48 + ich + 1],
                                               scalar2=None, op0=ALU.mult)
                    if _fv == "gsil":
                        return h.tensor_scalar(out=fuse_ap(ich), in0=gsil[gb][:],
                                               scalar1=1.0, scalar2=None, op0=ALU.mult)
                    return h.scalar_tensor_tensor(out=fuse_ap(ich), in0=pb[parU][:],
                                                  scalar=scales[:, 48 + ich:48 + ich + 1],
                                                  in1=gsil[gb][:], op0=ALU.mult, op1=ALU.mult)
                fwaits = [("se", ck_sil)]
                if ich == 32:
                    fwaits.extend(fuse32_waits)
                ck_ev[(tag, "fuse", ich)] = W_("ve", ve_fuse, waits=fwaits, inc="ve")

            ck_ag = None
            for dm in range(NCH):
                wi = dm % 2
                if dm == 2:
                    ck_ag = dm_hook()
                if dm >= 2:
                    if dm == 2:
                        waits = [("te", ck_ev[(tag, "teg", 46)])]
                    elif dm == 3:
                        waits = [("te", ck_ev[(tag, "teg", 47)])]
                    else:
                        waits = [("te", ck_ev[(tag, "dte", dm - 2)])]

                    def gp_wd(h, dm=dm, wi=wi):
                        src = ap_of(dw, dm * 128 * II, [[II, 128], [1, II]])
                        h.dma_start(out=ws[wi].ap()[:, 0:6144],
                                    in_=src).then_inc(sems[f"ws{wi}"], 16)
                        return None
                    ck_wd = sch.opn("gp", gp_wd, waits=waits, bump=f"ws{wi}", amt=16)

                par = dm % 2
                if dm < 2:
                    waits = [(f"dw{tg}{dm}", ck_dwp[dm]),
                             ("ve", ck_ev[(tag, "fuse", ICH - 1)])]
                else:
                    waits = [(f"ws{wi}", ck_wd)]
                    waits.append(("ve", ck_ev[(tag, "down", dm - 2)]))

                def te_d(h, dm=dm, par=par, wi=wi):
                    r = None
                    for ick in range(ICH):
                        if dm < 2:
                            stat = BIG.ap()[:, 49152 + dm * 6144 + ick * 128:
                                            49152 + dm * 6144 + (ick + 1) * 128]
                        else:
                            stat = ws[wi].ap()[:, ick * 128:(ick + 1) * 128]
                        r = h.matmul(pb[par][:], stat, fuse_ap(ick),
                                     start=(ick == 0), stop=(ick == ICH - 1))
                    return r
                ck_ev[(tag, "dte", dm)] = W_("te", te_d, waits=waits, inc="te")

                def ve_d(h, dm=dm, par=par):
                    return h.tensor_scalar(out=out_slice(dm), in0=pb[par][:],
                                           scalar1=scales[:, 96 + dm:96 + dm + 1],
                                           scalar2=None, op0=ALU.mult)
                ck_ev[(tag, "down", dm)] = W_("ve", ve_d,
                                              waits=[("te", ck_ev[(tag, "dte", dm)])],
                                              inc="ve")
            return ck_ev[(tag, "down", NCH - 1)]

        ck_ffnA = build_ffn(
            0, lambda dm: ffnA[:, dm * 512:(dm + 1) * 512], "fA",
            load_waits={(0, 0): [("te", ck_g[30])], (0, 1): [("te", ck_g[31])],
                        (1, 0): [], (1, 1): []},
            te0_waits=[("ve", ck_ev[("v", 31)])],
            fuse32_waits=[("vs8", ck_vst)],
            dwp_waits=[("vs8", ck_vst)],
            dm_hook=ag1_hook)
        ck_ag1 = sch.cnt["cc"]
        if "ffnA" in dbg:
            W_("sy", lambda h: h.dma_start(out=dbg["ffnA"].ap(), in_=ffnA[:]),
               waits=[("ve", ck_ffnA)], inc="dbg", inc_amt=16)
        ck_dbg_fuse = None
        if "fuse" in dbg:
            ck_dbg_fuse = W_("sy", lambda h: h.dma_start(out=dbg["fuse"].ap(),
                                                         in_=BIGb[:, 0:ICH * 512]),
                             waits=[("ve", ck_ev[("fA", "fuse", ICH - 1)])], inc="dbg", inc_amt=16)


        if STOP == "FA":
            return _finish(nc, es, sch, sems, dbg, xnT=xnT, qtT=qtT, ffnA=ffnA,
                           kvg=kvg, afg=afg, W_=W_, ck={})

        kh_ap = lambda hl: BIGb[:, hl * 16384: hl * 16384 + 8192]
        vh_ap = lambda hl: BIGb[:, hl * 16384 + 8192: hl * 16384 + 16384]

        ck_kv = {}
        ck_ofs = W_("gp", lambda h: h.dma_start(out=ofs[:], in_=offs_d.ap()),
                    inc="ofs", inc_amt=16)
        for hl in range(2):
            waits = [("cc", ck_ag1), ("ofs", ck_ofs)]
            if hl == 0:
                waits.append(("te", ck_ev[("fA", "dte", 15)]))
                if "fuse" in dbg:
                    waits.append(("dbg", ck_dbg_fuse))

            def gp_kv(h, hl=hl):
                rk = dynreg(h)
                h.reg_load(rk, ofs[0:1, hl:hl + 1])
                src = bass.AP(kvg, rk,
                              [[TOK, 128], [2 * D * TOK, NCORES], [1, TOK]])
                h.dma_start(out=kh_ap(hl), in_=src).then_inc(sems[f"kv{hl}"], 16)
                for r in range(NCORES):
                    h.reg_load(rk, ofs[0:1, 4 + hl * 8 + r:5 + hl * 8 + r])
                    src = bass.AP(kvg, rk, [[D, 128], [128 * D, 8], [1, 128]])
                    h.dma_start(out=vh_ap(hl)[:, r * 1024:(r + 1) * 1024],
                                in_=src).then_inc(sems[f"kv{hl}"], 16)
                return None
            ck_kv[hl] = sch.opn("gp", gp_kv, waits=waits, bump=f"kv{hl}", amt=16 * 9)

        psL = {0: pb[6], 1: pb[7]}
        ck_qs = {}
        ck_exp = {}
        ck_lmm = {}
        ck_oev = {}
        last_s_te = {}
        for C in range(NCH):
            nst = 4 * (C + 1)
            for hh in range(2):
                waits = [(f"kv{hh}", ck_kv[hh])]
                if C == 0:
                    waits.append(("ve", ck_ffnA))
                else:
                    waits.append(("te", last_s_te[hh]))

                def ve_qs(h, hh=hh, C=C):
                    srcap = ap_of(qtT, hh * 512 + 32 * C,
                                  [[NCH * TOK, 128], [1, 32], [TOK, 16]])
                    return h.tensor_copy(qslot[hh][:], srcap)
                ck_qs[hh] = W_("ve", ve_qs, waits=waits, inc="ve")

            for stg in range(nst + 1):
                if stg < nst:
                    for hh in range(2):
                        st = stg
                        par_s = st % 2
                        waits = [("ve", ck_qs[hh])]
                        if st >= 2:
                            waits.append(("se", ck_exp[(hh, st - 2)]))
                        elif C >= 1:
                            waits.append(("se", ck_exp[(hh, "last")]))
                        elif st < 2:
                            waits.append(("ve", ck_ffnA))

                        def te_s(h, hh=hh, st=st, par_s=par_s):
                            return h.matmul(pb[2 * hh + par_s][:],
                                            kh_ap(hh)[:, st * 128:(st + 1) * 128],
                                            qslot[hh][:], start=True, stop=True)
                        ck_s = W_("te", te_s, waits=waits, inc="te")
                        if st == nst - 1:
                            last_s_te[hh] = ck_s

                        w_e = [("te", ck_s)]
                        if st >= 2:
                            w_e.append(("te", ck_lmm[(hh, st - 2)]))
                        elif C >= 1:
                            w_e.append(("te", ck_lmm[(hh, "last")]))

                        def se_exp(h, hh=hh, pt_b=2 * hh + st % 2, par_s=par_s):
                            return h.activation(out=ptile[pt_b][:],
                                                in_=pb[2 * hh + par_s][:],
                                                func=AF.Exp, bias=nf32(CEXP),
                                                scale=nf32(QSCALE))
                        ck_exp[(hh, st)] = W_("se", se_exp, waits=w_e, inc="se")

                        if st >= 4 * C:
                            def ve_mask(h, st=st, C=C, pt_b=2 * hh + st % 2):
                                mi = st - 4 * C
                                return h.tensor_tensor(out=ptile[pt_b][:],
                                                       in0=ptile[pt_b][:],
                                                       in1=maskb[:, mi * 512:(mi + 1) * 512],
                                                       op=ALU.mult)
                            ck_exp[(hh, st, "m")] = W_("ve", ve_mask,
                                                       waits=[("se", ck_exp[(hh, st)])],
                                                       inc="ve")
                if stg >= 1:
                    for hh in range(2):
                        st = stg - 1
                        pt_b = 2 * hh + st % 2
                        if st >= 4 * C:
                            av_waits = [("ve", ck_exp[(hh, st, "m")])]
                        else:
                            av_waits = [("se", ck_exp[(hh, st)])]
                        if st == 0 and C >= 1:
                            av_waits.append(("ve", ck_oev[hh]))

                        def te_av(h, hh=hh, st=st, pt_b=pt_b,
                                  first=(st == 0), last=(st == nst - 1)):
                            h.matmul(pb[4 + hh][:], vh_ap(hh)[:, st * 128:(st + 1) * 128],
                                     ptile[pt_b][:], start=first, stop=last)
                            return h.matmul(psL[hh][:], onesb, ptile[pt_b][:],
                                            start=first, stop=last)
                        ck_lmm[(hh, st)] = W_("te", te_av, waits=av_waits, inc="te")
            ck_exp[(0, "last")] = ck_exp[(0, nst - 1)]
            ck_exp[(1, "last")] = ck_exp[(1, nst - 1)]
            ck_lmm[(0, "last")] = ck_lmm[(0, nst - 1)]
            ck_lmm[(1, "last")] = ck_lmm[(1, nst - 1)]

            for hh in range(2):
                def ve_oev(h, base, hh=hh):
                    h.tensor_scalar(out=lrep[:], in0=psL[hh][:], scalar1=1.0,
                                    scalar2=None, op0=ALU.mult).then_inc(sems["ve"], 1)
                    h.wait_ge(sems["ve"], base + 1)
                    h.reciprocal(linv[:], lrep[:]).then_inc(sems["ve"], 1)
                    h.wait_ge(sems["ve"], base + 2)
                    h.scalar_tensor_tensor(out=otf[:], in0=pb[4 + hh][:], scalar=1.0,
                                           in1=linv[:], op0=ALU.mult, op1=ALU.mult
                                           ).then_inc(sems["ve"], 1)
                    h.wait_ge(sems["ve"], base + 3)
                    h.tensor_scalar(out=ostg[hh][:], in0=otf[:],
                                    scalar1=bvo[:, hh:hh + 1], scalar2=None,
                                    op0=ALU.add).then_inc(sems["ve"], 1)
                w_o = [("te", ck_lmm[(hh, "last")])]
                if C >= 1:
                    w_o.append((f"ost{hh}", ck_oev[(hh, "st")]))
                ck_oev[hh] = sch.opsw("ve", ve_oev, waits=w_o, n_incs=4)

                def sy_ost(h, hh=hh, C=C):
                    dst = ap_of(afb, hh * 128 * T + C * 512, [[T, 128], [1, 512]])
                    return h.dma_start(out=dst, in_=ostg[hh][:])
                ck_oev[(hh, "st")] = W_("sy", sy_ost, waits=[("ve", ck_oev[hh])],
                                        inc=f"ost{hh}", inc_amt=16)

        last_exp = max(ck_exp[(0, "last")], ck_exp[(1, "last")])
        ost_fin = (sch.cnt["ost0"], sch.cnt["ost1"])

        def ag2_hook():
            return W_("gp", lambda h: h.collective_compute(
                "AllGather", ALU.bypass, replica_groups=[list(range(NCORES))],
                ins=[afb.ap().opt()], outs=[afg.ap().opt()]),
                waits=[("ost0", ost_fin[0]), ("ost1", ost_fin[1])], inc="cc")

        ck_ffnB = build_ffn(
            1, lambda dm: ffnB[:, dm * 512:(dm + 1) * 512], "fB",
            load_waits={(0, 0): [("te", ck_ev[("fA", "dte", 14)])],
                        (0, 1): [("te", ck_ev[("fA", "dte", 15)])],
                        (1, 0): [("te", ck_ev[("fA", "teg", 47)])],
                        (1, 1): []},
            te0_waits=[("ve", ck_oev[1]), ("ve", ck_oev[0]),
                       ("te", max(ck_lmm[(0, "last")], ck_lmm[(1, "last")])),
                       ("se", last_exp)],
            fuse32_waits=(),
            dwp_waits=[("te", ck_lmm[(1, "last")])],
            dm_hook=ag2_hook)
        ck_ag2 = sch.cnt["cc"]

        if STOP == "FB":
            return _finish(nc, es, sch, sems, dbg, xnT=xnT, qtT=qtT, ffnA=ffnA,
                           kvg=kvg, afg=afg, W_=W_, ck={})

        opm_ap = lambda hd: BIGb[:, hd * 1024:(hd + 1) * 1024]

        def gp_opm(h):
            ro = dynreg(h)
            h.reg_load(ro, ofs[0:1, 20:21])
            return h.dma_start(out=BIGb[:, 0:16384],
                               in_=bass.AP(afg, ro,
                                           [[T, 128], [128 * T, NCH], [1, TOK]]))
        ck_opm = W_("gp", gp_opm,
                    waits=[("cc", ck_ag2), ("te", ck_ev[("fB", "dte", 15)])],
                    inc="opm", inc_amt=16)

        ogi = 0
        ck_og = {}
        ck_oge = {}
        for dm in range(NCH):
            wi = dm % 2
            if dm < 2:
                waits = [("te", ck_ev[("fB", "dte", 14 + dm)])]
            else:
                waits = [("te", ck_og[dm - 2])]
            ck_wo = sch.opn("gp", (lambda dm=dm, wi=wi:
                                   lambda h: w_attn_load(h, Wo, dm, wi))(),
                            waits=waits, bump=f"ws{wi}", amt=16)
            for half2 in range(2):
                par = ogi % 2
                waits = [(f"ws{wi}", ck_wo), ("opm", ck_opm)]
                if ogi < 2:
                    waits.append(("ve", ck_ffnB))
                if ogi >= 2:
                    waits.append(("ve", ck_oge[ogi - 2]))

                def te_o(h, half2=half2, par=par, wi=wi):
                    r = None
                    for hd in range(NCH):
                        r = h.matmul(pb[par][:], wsb(wi)[:, hd * 128:(hd + 1) * 128],
                                     opm_ap(hd)[:, half2 * 512:(half2 + 1) * 512],
                                     start=(hd == 0), stop=(hd == NCH - 1))
                    return r
                ck_og[dm] = W_("te", te_o, waits=waits, inc="te")

                def ve_op(h, dm=dm, half2=half2, par=par):
                    dst = (ffnA[:, dm * 512:(dm + 1) * 512] if half2 == 0
                           else ffnB[:, dm * 512:(dm + 1) * 512])
                    return h.scalar_tensor_tensor(out=dst, in0=pb[par][:],
                                                  scalar=bvecs[:, 32 + dm:32 + dm + 1],
                                                  in1=dst, op0=ALU.add, op1=ALU.add)
                ck_oge[ogi] = W_("ve", ve_op, waits=[("te", ck_og[dm])], inc="ve")
                ogi += 1
        ck_oproj = ck_oge[ogi - 1]

        fxt_ap = lambda b: BIGf[:, 8192 + b * 2048: 8192 + (b + 1) * 2048]
        fos_ap = lambda b: BIGf[:, 12288 + b * 2048: 12288 + (b + 1) * 2048]

        last_fadd = {0: 0, 1: 0}
        ck_fst = {}
        fti = 0
        for half2 in range(2):
            src_ffn = (lambda ch, t4, half2=half2:
                       (ffnA if half2 == 0 else ffnB)[:, ch * 512 + t4 * 128:
                                                      ch * 512 + t4 * 128 + 128])
            for t4 in range(4):
                b = fti % 2
                waits = [("ve", ck_ffnB)]
                if fti >= 2:
                    waits.append(("ve", ck_fst[("e", fti - 2)]))
                    waits.append((f"fst{b}", ck_fst[fti - 2]))

                def sy_fx(h, half2=half2, t4=t4, b=b):
                    r0 = half2 * 512 + t4 * 128
                    return h.dma_start(out=fxt_ap(b), in_=x_own.ap()[r0:r0 + 128, :])
                ck_fx = W_("sy", sy_fx, waits=waits, inc=f"fx{b}", inc_amt=16)

                for grp in range(4):
                    par = grp % 2
                    waits = [("ve", ck_oproj)]
                    if last_fadd[par]:
                        waits.append(("ve", last_fadd[par]))

                    def te_ftr(h, src_ffn=src_ffn, t4=t4, grp=grp, par=par):
                        r = None
                        for k in range(4):
                            r = h.transpose(pb[par][:, k * 128:(k + 1) * 128],
                                            src_ffn(grp * 4 + k, t4), cf[:])
                        return r
                    ck_tr = W_("te", te_ftr, waits=waits, inc="te")

                    def ve_fadd(h, b=b, grp=grp, par=par):
                        return h.tensor_tensor(out=fos_ap(b)[:, grp * 512:(grp + 1) * 512],
                                               in0=pb[par][:],
                                               in1=fxt_ap(b)[:, grp * 512:(grp + 1) * 512],
                                               op=ALU.add)
                    last_fadd[par] = W_("ve", ve_fadd,
                                        waits=[("te", ck_tr), (f"fx{b}", ck_fx)], inc="ve")
                ck_fst[("e", fti)] = last_fadd[3 % 2]

                def sy_fst(h, half2=half2, t4=t4, b=b):
                    r0 = half2 * 512 + t4 * 128
                    return h.dma_start(out=out_own.ap()[r0:r0 + 128, :], in_=fos_ap(b))
                ck_fst[fti] = W_("sy", sy_fst,
                                 waits=[("ve", last_fadd[0]), ("ve", last_fadd[1])],
                                 inc=f"fst{b}", inc_amt=16)
                fti += 1

        if "xnT" in dbg:
            W_("sy", lambda h: h.dma_start(out=dbg["xnT"].ap(), in_=xnT[:]),
               waits=[("se", ck_xnT)], inc="dbg", inc_amt=16)
        if "kvg" in dbg:
            W_("gp", lambda h: h.dma_start(out=dbg["kvg"].ap(), in_=kvg.ap()),
               waits=[("cc", ck_ag1)], inc="dbg", inc_amt=16)
        if "afg" in dbg:
            W_("gp", lambda h: h.dma_start(out=dbg["afg"].ap(), in_=afg.ap()),
               waits=[("cc", ck_ag2)], inc="dbg", inc_amt=16)

        W_("sy", lambda h: None, waits=[("fst0", sch.cnt["fst0"]),
                                        ("fst1", sch.cnt["fst1"]),
                                        ("sy", sch.cnt["sy"]), ("dbg", sch.cnt["dbg"])])
        W_("gp", lambda h: None, waits=[("gp", sch.cnt["gp"])])

        block = es.enter_context(nc.Block())

        def emit(eng_name, h):
            for waits, (inc, amt), fn in sch.ops[eng_name]:
                for s, v in waits:
                    h.wait_ge(sems[s], v)
                inst = fn(h)
                if inc is not None:
                    assert inst is not None, f"{eng_name} op with inc has no instruction"
                    inst.then_inc(sems[inc], amt)

        @block.sync
        def _(h):
            emit("sy", h)

        @block.gpsimd
        def _(h):
            emit("gp", h)

        @block.tensor
        def _(h):
            emit("te", h)

        @block.vector
        def _(h):
            emit("ve", h)

        @block.scalar
        def _(h):
            emit("se", h)

    return nc


_BF = ml_dtypes.bfloat16


def _consts():
    ident = np.eye(128, dtype=_BF)
    ones = np.ones((128, 128), dtype=_BF)
    masks = np.zeros((128, 4 * 512), dtype=_BF)
    for d in range(4):
        m = (np.arange(512)[None, :] >= (128 * d + np.arange(128))[:, None])
        masks[:, d * 512:(d + 1) * 512] = m.astype(_BF)
    cb = np.concatenate([ident, ones, masks], axis=1)
    cf = np.eye(128, dtype=np.float32)
    return np.ascontiguousarray(cb), np.ascontiguousarray(cf)


def kernel(L0_attn_x, L0_attn_Wq, L0_attn_bq, L0_attn_Wk, L0_attn_bk, L0_attn_Wv,
           L0_attn_bv, L0_attn_Wo, L0_attn_bo, L0_ffn_gate_w, L0_ffn_up_w,
           L0_ffn_down_w, L0_ffn_gate_scale, L0_ffn_up_scale, L0_ffn_down_scale):
    x = np.ascontiguousarray(np.asarray(L0_attn_x, dtype=np.float32).reshape(T, D))
    cbv, cfv = _consts()
    bvecs = np.ascontiguousarray(np.concatenate([
        np.asarray(L0_attn_bq, np.float32).reshape(NCH, 128).T,
        np.asarray(L0_attn_bk, np.float32).reshape(NCH, 128).T,
        np.asarray(L0_attn_bo, np.float32).reshape(NCH, 128).T], axis=1))
    bv_all = np.asarray(L0_attn_bv, np.float32).reshape(NCH, 128).T
    scales = np.ascontiguousarray(np.concatenate([
        np.asarray(L0_ffn_gate_scale, np.float32).reshape(ICH, 128).T,
        np.asarray(L0_ffn_up_scale, np.float32).reshape(ICH, 128).T,
        np.asarray(L0_ffn_down_scale, np.float32).reshape(NCH, 128).T], axis=1))

    def chunked(w, nin, nout):
        a = np.asarray(w, np.float32).reshape(nin, 128, nout, 128)
        return np.ascontiguousarray(a.transpose(2, 1, 0, 3))

    shared = {
        "Wq": chunked(L0_attn_Wq, NCH, NCH),
        "Wk": chunked(L0_attn_Wk, NCH, NCH),
        "Wv": np.ascontiguousarray(np.asarray(L0_attn_Wv, np.float32)),
        "Wo": chunked(L0_attn_Wo, NCH, NCH),
        "gw": chunked(L0_ffn_gate_w, NCH, ICH),
        "uw": chunked(L0_ffn_up_w, NCH, ICH),
        "dw": chunked(L0_ffn_down_w, ICH, NCH),
        "bvecs": bvecs, "scales": scales, "constsb": cbv, "constsf": cfv,
    }

    in_maps = []
    for c in range(NCORES):
        h0, h1 = 2 * c, 2 * c + 1
        offs = np.zeros((1, 24), dtype=np.int32)
        offs[0, 0] = h0 * 128 * TOK
        offs[0, 1] = h1 * 128 * TOK
        for hli, hh in ((0, h0), (1, h1)):
            for r in range(NCORES):
                offs[0, 4 + hli * 8 + r] = r * 2 * D * TOK + D * TOK + hh * 128
        offs[0, 20] = c * TOK
        m = dict(shared)
        m["x_own"] = np.ascontiguousarray(x[c * TOK:(c + 1) * TOK])
        m["bvo"] = np.ascontiguousarray(bv_all[:, [h0, h1]])
        m["offs"] = offs
        in_maps.append(m)

    nc = build()
    res = run_bass_kernel_spmd(nc, in_maps, core_ids=list(range(NCORES)))
    out = np.concatenate([r["out_own"] for r in res.results], axis=0)
    if DEBUG:
        kernel._debug = res.results
    return out.reshape(1, T, D)


# version 17
# speedup vs baseline: 1.9207x; 1.0670x over previous
import contextlib

import numpy as np
import ml_dtypes

import concourse.bass as bass
import concourse.mybir as mybir
from concourse.bass_utils import run_bass_kernel_spmd

F32 = mybir.dt.float32
BF16 = mybir.dt.bfloat16
FP8 = mybir.dt.float8e4
I32 = mybir.dt.int32
AF = mybir.ActivationFunctionType
ALU = mybir.AluOpType
nf32 = np.float32

T, D, H, DH, SBK, II = 8192, 2048, 16, 128, 512, 6144
NCORES = 8
TOK = 1024
NCH = D // 128
ICH = II // 128
CEXP = -24.0
QSCALE = float(1.0 / np.sqrt(DH))

DEBUG = []
STOP = None

ENGS = ("sy", "gp", "te", "ve", "se")


class Sched:

    def __init__(self):
        self.ops = {e: [] for e in ENGS}
        import collections
        self.cnt = collections.defaultdict(int)

    def op(self, eng, fn, waits=(), inc=None, inc_amt=1):
        waits = [w for w in waits if w is not None]
        self.ops[eng].append((tuple(waits), (inc, inc_amt), fn))
        if inc is not None:
            self.cnt[inc] += inc_amt
            return self.cnt[inc]
        return None

    def opsw(self, eng, fn, waits=(), n_incs=1):
        waits = [w for w in waits if w is not None]
        base = self.cnt[eng]
        self.cnt[eng] += n_incs
        self.ops[eng].append((tuple(waits), (None, 0),
                              (lambda fn=fn, base=base: lambda h: fn(h, base))()))
        return self.cnt[eng]

    def opn(self, eng, fn, waits=(), bump=None, amt=0):
        waits = [w for w in waits if w is not None]
        self.ops[eng].append((tuple(waits), (None, 0), fn))
        self.cnt[bump] += amt
        return self.cnt[bump]


def _finish(nc, es, sch, sems, dbg, **kw):
    W_ = kw["W_"]
    W_("sy", lambda h: None, waits=[("sy", sch.cnt["sy"])])
    W_("gp", lambda h: None, waits=[("gp", sch.cnt["gp"])])
    block = es.enter_context(nc.Block())

    def emit(eng_name, h):
        for waits, (inc, amt), fn in sch.ops[eng_name]:
            for s, v in waits:
                h.wait_ge(sems[s], v)
            inst = fn(h)
            if inc is not None:
                assert inst is not None
                inst.then_inc(sems[inc], amt)

    @block.sync
    def _(h):
        emit("sy", h)

    @block.gpsimd
    def _(h):
        emit("gp", h)

    @block.tensor
    def _(h):
        emit("te", h)

    @block.vector
    def _(h):
        emit("ve", h)

    @block.scalar
    def _(h):
        emit("se", h)
    return nc


def ap_of(t, offset, dims):
    return bass.AP(t, offset, [list(d) for d in dims])


def dynreg(h):
    if not hasattr(h, "_dynreg"):
        h._dynreg = h.alloc_register("dynoff")
    return h._dynreg


def build():
    nc = bass.Bass("TRN2", target_bir_lowering=False, detect_race_conditions=False)

    x_own = nc.dram_tensor("x_own", [TOK, D], F32, kind="ExternalInput")
    Wq = nc.dram_tensor("Wq", [NCH, 128, NCH, 128], F32, kind="ExternalInput")
    Wk = nc.dram_tensor("Wk", [NCH, 128, NCH, 128], F32, kind="ExternalInput")
    Wv = nc.dram_tensor("Wv", [D, D], F32, kind="ExternalInput")
    Wo = nc.dram_tensor("Wo", [NCH, 128, NCH, 128], F32, kind="ExternalInput")
    gw = nc.dram_tensor("gw", [ICH, 128, NCH, 128], F32, kind="ExternalInput")
    uw = nc.dram_tensor("uw", [ICH, 128, NCH, 128], F32, kind="ExternalInput")
    dw = nc.dram_tensor("dw", [NCH, 128, ICH, 128], F32, kind="ExternalInput")
    bvecs_d = nc.dram_tensor("bvecs", [128, 48], F32, kind="ExternalInput")
    bvo_d = nc.dram_tensor("bvo", [128, 2], F32, kind="ExternalInput")
    scales_d = nc.dram_tensor("scales", [128, 112], F32, kind="ExternalInput")
    cb_d = nc.dram_tensor("constsb", [128, 2304], BF16, kind="ExternalInput")
    cf_d = nc.dram_tensor("constsf", [128, 128], F32, kind="ExternalInput")
    offs_d = nc.dram_tensor("offs", [1, 24], I32, kind="ExternalInput")
    out_own = nc.dram_tensor("out_own", [TOK, D], F32, kind="ExternalOutput")

    kvb = nc.dram_tensor("kvb", [2, D, TOK], BF16)
    kvg = nc.dram_tensor("kvg", [2 * NCORES, D, TOK], BF16, addr_space="Shared")
    afb = nc.dram_tensor("afb", [256, T], BF16)
    afg = nc.dram_tensor("afg", [D, T], BF16, addr_space="Shared")

    qgw = nc.dram_tensor("qgw", [768, D], F32, kind="ExternalInput")
    quw = nc.dram_tensor("quw", [768, D], F32, kind="ExternalInput")
    qdw = nc.dram_tensor("qdw", [256, II], F32, kind="ExternalInput")
    qwo = nc.dram_tensor("qwo", [256, D], F32, kind="ExternalInput")
    SH = 768 * D
    A3 = 3 * SH
    q8b_a = nc.dram_tensor("q8b_a", [1, A3], FP8)
    q8g_a = nc.dram_tensor("q8g_a", [NCORES, A3], FP8, addr_space="Shared")
    q8b_w = nc.dram_tensor("q8b_w", [1, 256 * D], BF16)
    q8g_w = nc.dram_tensor("q8g_w", [NCORES, 256 * D], BF16, addr_space="Shared")

    dbg = {}
    for name, shp, dt in [("ws3", [128, 6144], FP8),
                          ("fuse", [128, ICH * SBK], BF16),
                          ("gtf", [128, 512], F32),
                          ("xnT", [128, NCH * TOK], BF16),
                          ("qtT", [128, NCH * TOK], BF16),
                          ("kvg", [2 * NCORES, D, TOK], BF16),
                          ("ffnA", [128, NCH * SBK], F32),
                          ("afg", [D, T], BF16),
                          ("q8ga", [NCORES, 1152, 4096], FP8)]:
        if name in DEBUG:
            dbg[name] = nc.dram_tensor("dbg_" + name, shp, dt, kind="ExternalOutput")

    es = contextlib.ExitStack()
    with es:
        S = lambda n, shp, d: es.enter_context(nc.sbuf_tensor(n, shp, d))
        P = lambda n, shp, d: es.enter_context(nc.psum_tensor(n, shp, d))

        xnT = S("xnT", [128, NCH * TOK], BF16)
        qtT = S("qtT", [128, NCH * TOK], BF16)
        ffnA = S("ffnA", [128, NCH * SBK], F32)
        BIG = S("BIG", [128, 65536], FP8)
        ws = [S(f"ws{i}", [128, 6144], FP8) for i in range(4)]
        cb = S("cb", [128, 2304], BF16)
        cf = S("cf", [128, 128], F32)
        bvecs = S("bvecs_s", [128, 48], F32)
        bvo = S("bvo_s", [128, 2], F32)
        scales = S("scales_s", [128, 112], F32)
        stats = S("stats", [128, 24], F32)
        ofs = S("ofs", [1, 24], I32)
        mv = S("mv", [128, 2], F32)
        sd = S("sd", [128, 1], F32)
        rinv = S("rinv", [128, 1], F32)
        qslot = [S(f"qslot{i}", [128, 512], BF16) for i in range(2)]
        ptile = [S(f"ptile{i}", [128, 512], BF16) for i in range(4)]
        lrep = S("lrep", [128, 512], F32)
        linv = S("linv", [128, 512], F32)
        otf = lrep
        ostg = [S(f"ostg{i}", [128, 512], BF16) for i in range(2)]
        kstg = ostg
        gtf = [S("gtf0", [128, 512], F32)] * 2
        gsil = [S("gsil0", [128, 512], F32)] * 2
        dbg_psu = None

        pb = [P(f"pb{i}", [128, 512], F32) for i in range(8)]
        pTv = pb[7].ap().bitcast(BF16)

        BIGb = BIG.ap().bitcast(BF16)
        BIGf = BIG.ap().bitcast(F32)
        ffnB = qtT.ap().bitcast(F32)
        identb = cb.ap()[:, 0:128]
        onesb = cb.ap()[:, 128:256]
        maskb = cb.ap()[:, 256:2304]

        sem_names = list(ENGS) + ["cc", "cs", "xa", "xb", "kst0", "kst1", "vs8",
                                  "ost0", "ost1", "fx0", "fx1", "fst0", "fst1",
                                  "ws0", "ws1", "ws2", "ws3", "wv", "kv0", "kv1",
                                  "ofs", "opm", "dbg",
                                  "dwA0", "dwA1", "dwB0", "dwB1", "q8"]
        sems = {e: es.enter_context(nc.semaphore(f"sem_{e}")) for e in sem_names}

        sch = Sched()
        W_ = sch.op

        W_("sy", lambda h: h.dma_start(out=cb[:], in_=cb_d.ap()), inc="cs", inc_amt=16)
        W_("sy", lambda h: h.dma_start(out=cf[:], in_=cf_d.ap()), inc="cs", inc_amt=16)
        W_("sy", lambda h: h.dma_start(out=bvecs[:], in_=bvecs_d.ap()), inc="cs", inc_amt=16)
        W_("sy", lambda h: h.dma_start(out=bvo[:], in_=bvo_d.ap()), inc="cs", inc_amt=16)
        ck_consts = W_("sy", lambda h: h.dma_start(out=scales[:], in_=scales_d.ap()),
                       inc="cs", inc_amt=16)
        ck_ofs = W_("gp", lambda h: h.dma_start(out=ofs[:], in_=offs_d.ap()),
                    inc="ofs", inc_amt=16)

        quant_jobs = [(qgw, ap_of(q8b_a, 0, [[D, 768], [1, D]])),
                      (quw, ap_of(q8b_a, SH, [[D, 768], [1, D]])),
                      (qdw, ap_of(q8b_a, 2 * SH, [[II, 256], [1, II]])),
                      (qwo, ap_of(q8b_w, 0, [[D, 256], [1, D]]))]

        def gp_quant(h, job):
            srcT, dst = job
            h.dma_start(out=dst, in_=srcT.ap()).then_inc(sems["q8"], 16)
            return None

        xt_ap = lambda b: BIGf[:, b * 2048:(b + 1) * 2048]
        xnb_ap = lambda b: BIGb[:, 8192 + b * 2048: 8192 + (b + 1) * 2048]

        ck_xnb = [0] * 8
        ck_tr_tile = [0] * 8
        last_evac = 0
        for t8 in range(8):
            b = t8 % 2
            xsem = "xa" if b == 0 else "xb"
            waits = [("ve", ck_xnb[t8 - 2])] if t8 >= 2 else [("cs", ck_consts)]
            ck_xl = W_("sy", (lambda t8=t8, b=b: lambda h: h.dma_start(
                out=xt_ap(b), in_=x_own.ap()[t8 * 128:(t8 + 1) * 128, :]))(),
                waits=waits, inc=xsem, inc_amt=16)

            def ve_ln(h, base, b=b):
                r = None
                for i in range(4):
                    r = h.bn_stats(stats[:, 6 * i:6 * i + 6], xt_ap(b)[:, 512 * i:512 * i + 512])
                r.then_inc(sems["ve"], 1)
                h.wait_ge(sems["ve"], base + 1)
                h.bn_aggr(mv[:], stats[:]).then_inc(sems["ve"], 1)
                h.wait_ge(sems["ve"], base + 2)
                h.tensor_scalar_add(sd[:], mv[:, 1:2], 1e-5).then_inc(sems["ve"], 1)
            ck_sd = sch.opsw("ve", ve_ln, waits=[(xsem, ck_xl)], n_incs=3)
            ck_sq = W_("se", lambda h: h.activation(out=sd[:], in_=sd[:], func=AF.Sqrt),
                       waits=[("ve", ck_sd)], inc="se")

            def ve_norm(h, base, b=b):
                h.reciprocal(rinv[:], sd[:]).then_inc(sems["ve"], 1)
                h.wait_ge(sems["ve"], base + 1)
                h.tensor_scalar(out=xnb_ap(b), in0=xt_ap(b),
                                scalar1=mv[:, 0:1], scalar2=rinv[:],
                                op0=ALU.subtract, op1=ALU.mult).then_inc(sems["ve"], 1)
            w_n = [("se", ck_sq)]
            if t8 >= 2:
                w_n.append(("te", ck_tr_tile[t8 - 2]))
            ck_xnb[t8] = sch.opsw("ve", ve_norm, waits=w_n, n_incs=2)

            for half in range(2):
                waits = [("ve", ck_xnb[t8])]
                if t8 == 0 and half == 0:
                    waits.append(("cs", ck_consts))
                if last_evac:
                    waits.append(("se", last_evac))

                def te_tr(h, b=b, half=half):
                    r = None
                    for c8 in range(8):
                        ch = half * 8 + c8
                        r = h.transpose(pTv[:, c8 * 128:(c8 + 1) * 128],
                                        xnb_ap(b)[:, ch * 128:(ch + 1) * 128], identb)
                    return r
                ck_tr = W_("te", te_tr, waits=waits, inc="te")
                ck_tr_tile[t8] = ck_tr

                def se_evac(h, t8=t8, half=half):
                    dst = ap_of(xnT, (half * 8) * TOK + t8 * 128,
                                [[NCH * TOK, 128], [TOK, 8], [1, 128]])
                    return h.copy(out=dst,
                                  in_=pTv[:, 0:1024].rearrange("p (a b) -> p a b", a=8))
                last_evac = W_("se", se_evac, waits=[("te", ck_tr)], inc="se")

        ck_xnT = last_evac
        if STOP == "A":
            return _finish(nc, es, sch, sems, dbg, xnT=xnT, qtT=qtT, ffnA=ffnA,
                           kvg=kvg, afg=afg, W_=W_, ck={"xnT": ck_xnT})

        wsb = lambda i: ws[i].ap().bitcast(BF16)

        def w_attn_load(h, Wt, ch, wi):
            src = ap_of(Wt, ch * 128 * D, [[D, 128], [1, D]])
            h.dma_start(out=wsb(wi)[:, 0:2048], in_=src).then_inc(sems[f"ws{wi}"], 16)
            return None

        ck_g = {}
        ck_ev = {}
        gidx = 0
        ck_w = {}
        for pi, (Wt, bcol, kind) in enumerate([(Wq, 0, "q"), (Wk, 16, "k")]):
            for ch in range(NCH):
                ci = pi * NCH + ch
                wi = ci % 2
                waits = [("te", ck_g[ci - 2])] if ci >= 2 else []
                ck_w[ci] = sch.opn("gp", (lambda Wt=Wt, ch=ch, wi=wi:
                                          lambda h: w_attn_load(h, Wt, ch, wi))(),
                                   waits=waits, bump=f"ws{wi}", amt=16)
                if 2 <= ci <= 5:
                    sch.opn("gp", (lambda job=quant_jobs[ci - 2]:
                                   lambda h: gp_quant(h, job))(),
                            waits=[("ofs", ck_ofs)], bump="q8", amt=16)
                for half2 in range(2):
                    par = gidx % 2
                    waits = [(f"ws{wi}", ck_w[ci])]
                    if gidx == 0:
                        waits.append(("se", ck_xnT))
                        waits.append(("cs", ck_consts))
                    if gidx >= 2:
                        waits.append(("ve", ck_ev[gidx - 2]))

                    def te_proj(h, wi=wi, half2=half2, par=par):
                        r = None
                        for dch in range(NCH):
                            r = h.matmul(
                                pb[par][:], wsb(wi)[:, dch * 128:(dch + 1) * 128],
                                xnT[:, dch * TOK + half2 * 512: dch * TOK + half2 * 512 + 512],
                                start=(dch == 0), stop=(dch == NCH - 1))
                        return r
                    ck_g[ci] = W_("te", te_proj, waits=waits, inc="te")

                    if kind == "q":
                        def ve_evq(h, ch=ch, half2=half2, par=par):
                            return h.tensor_scalar(
                                out=qtT[:, ch * TOK + half2 * 512: ch * TOK + half2 * 512 + 512],
                                in0=pb[par][:], scalar1=bvecs[:, ch:ch + 1],
                                scalar2=None, op0=ALU.add)
                        ck_ev[gidx] = W_("ve", ve_evq, waits=[("te", ck_g[ci])], inc="ve")
                    else:
                        kb = gidx % 2
                        waits_e = [("te", ck_g[ci])]
                        if (gidx - 2, "kst") in ck_ev:
                            waits_e.append((f"kst{kb}", ck_ev[(gidx - 2, "kst")]))

                        def ve_evk(h, ch=ch, par=par, kb=kb):
                            return h.tensor_scalar(
                                out=kstg[kb][:], in0=pb[par][:],
                                scalar1=bvecs[:, 16 + ch:16 + ch + 1],
                                scalar2=None, op0=ALU.add)
                        ck_ev[gidx] = W_("ve", ve_evk, waits=waits_e, inc="ve")

                        def sy_kst(h, ch=ch, half2=half2, kb=kb):
                            dst = ap_of(kvb, ch * 128 * TOK + half2 * 512,
                                        [[TOK, 128], [1, 512]])
                            return h.dma_start(out=dst, in_=kstg[kb][:])
                        ck_ev[(gidx, "kst")] = W_("sy", sy_kst,
                                                  waits=[("ve", ck_ev[gidx])],
                                                  inc=f"kst{kb}", inc_amt=16)
                    gidx += 1
        ck_kst_fin = {0: sch.cnt["kst0"], 1: sch.cnt["kst1"]}
        if "qtT" in dbg:
            W_("sy", lambda h: h.dma_start(out=dbg["qtT"].ap(), in_=qtT[:]),
               waits=[("ve", ck_ev[31])], inc="dbg", inc_amt=16)

        wv_ap = lambda dch: BIGb[:, dch * 1024:(dch + 1) * 1024]
        vfull_ap = lambda t8: BIGb[:, 16384 + t8 * 2048: 16384 + (t8 + 1) * 2048]

        ck_wv = {}
        ck_v_h0_last = 0
        vgi = 0
        for vhalf in range(2):
            if vhalf == 1:
                W_("gp", lambda h: h.collective_compute(
                    "AllGather", ALU.bypass, replica_groups=[list(range(NCORES))],
                    ins=[q8b_a.ap().opt()], outs=[q8g_a.ap().opt()]),
                    waits=[("q8", 4 * 16)], inc="cc")
                ck_qag = W_("gp", lambda h: h.collective_compute(
                    "AllGather", ALU.bypass, replica_groups=[list(range(NCORES))],
                    ins=[q8b_w.ap().opt()], outs=[q8g_w.ap().opt()]),
                    inc="cc")
                if "q8ga" in dbg:
                    W_("gp", lambda h: h.dma_start(
                        out=dbg["q8ga"].ap(),
                        in_=bass.AP(q8g_a, 0, [[A3, NCORES], [4096, 1152],
                                               [1, 4096]])),
                       waits=[("cc", ck_qag)], inc="dbg", inc_amt=16)
            for dch in range(NCH):
                waits = []
                if vhalf == 0 and dch == 0:
                    waits.append(("se", ck_xnT))
                if vhalf == 1:
                    waits.append(("te", ck_v_h0_last))

                def gp_wv(h, dch=dch, vhalf=vhalf):
                    src = ap_of(Wv, dch * 128 * D + vhalf * 1024,
                                [[D, 128], [1, 1024]])
                    return h.dma_start(out=wv_ap(dch), in_=src)
                ck_wv[(vhalf, dch)] = W_("gp", gp_wv, waits=waits, inc="wv", inc_amt=16)
            for t8 in range(8):
                for g2 in range(2):
                    par = vgi % 2
                    waits = [("wv", ck_wv[(vhalf, NCH - 1)])]
                    if vgi >= 2:
                        waits.append(("ve", ck_ev[("v", vgi - 2)]))
                    else:
                        waits.append(("ve", ck_ev[gidx - 1]))

                    def te_v(h, t8=t8, g2=g2, par=par):
                        r = None
                        for dch in range(NCH):
                            r = h.matmul(
                                pb[par][:],
                                xnT[:, dch * TOK + t8 * 128: dch * TOK + t8 * 128 + 128],
                                wv_ap(dch)[:, g2 * 512:(g2 + 1) * 512],
                                start=(dch == 0), stop=(dch == NCH - 1))
                        return r
                    ck_te_v = W_("te", te_v, waits=waits, inc="te")
                    if vhalf == 0:
                        ck_v_h0_last = ck_te_v

                    def ve_evv(h, t8=t8, g2=g2, vhalf=vhalf, par=par):
                        return h.tensor_scalar(
                            out=vfull_ap(t8)[:, vhalf * 1024 + g2 * 512:
                                             vhalf * 1024 + g2 * 512 + 512],
                            in0=pb[par][:], scalar1=1.0, scalar2=None, op0=ALU.mult)
                    ck_ev[("v", vgi)] = W_("ve", ve_evv, waits=[("te", ck_te_v)], inc="ve")
                    vgi += 1

        ck_vst = 0
        for t8 in range(8):
            def sy_vst(h, t8=t8):
                dst = ap_of(kvb, D * TOK + t8 * 128 * D, [[D, 128], [1, D]])
                return h.dma_start(out=dst, in_=vfull_ap(t8))
            ck_vst = W_("sy", sy_vst, waits=[("ve", ck_ev[("v", 16 + t8 * 2 + 1)])],
                        inc="vs8", inc_amt=16)

        def ag1_hook():
            return W_("gp", lambda h: h.collective_compute(
                "AllGather", ALU.bypass, replica_groups=[list(range(NCORES))],
                ins=[kvb.ap().opt()], outs=[kvg.ap().opt()]),
                waits=[("vs8", ck_vst), ("kst0", ck_kst_fin[0]),
                       ("kst1", ck_kst_fin[1])], inc="cc")

        if STOP == "B":
            ck_ag1 = ag1_hook()
            if "qtT" in dbg:
                pass
            if "kvg" in dbg:
                W_("gp", lambda h: h.dma_start(out=dbg["kvg"].ap(), in_=kvg.ap()),
                   waits=[("cc", ck_ag1)], inc="gp", inc_amt=16)
            return _finish(nc, es, sch, sems, dbg, xnT=xnT, qtT=qtT, ffnA=ffnA,
                           kvg=kvg, afg=afg, W_=W_, ck={})

        fuse_ap = lambda ich: BIGb[:, ich * 512:(ich + 1) * 512]

        def build_ffn(hb, out_slice, tag, load_waits, te0_waits, fuse32_waits,
                      dwp_waits, dm_hook):
            tg = tag[1]
            ck_wg = {}
            ck_dwp = {}
            for ich in range(ICH):
                for gu in range(2):
                    wi = (ich % 2) * 2 + gu
                    waits = list(load_waits.get((ich, gu), ())) if ich < 2 else []
                    if ich >= 2:
                        waits.append(("ve", ck_ev[(tag, "fuse", ich - 2)]))

                    def gp_w(h, ich=ich, wi=wi, gu=gu):
                        base = (ich // 6) * A3 + (ich % 6) * 128 * D + gu * SH
                        src = ap_of(q8g_a, base, [[D, 128], [1, D]])
                        h.dma_start(out=ws[wi].ap()[:, 0:2048],
                                    in_=src).then_inc(sems[f"ws{wi}"], 16)
                        return None
                    ck_wg[(ich, gu)] = sch.opn("gp", gp_w, waits=waits, bump=f"ws{wi}", amt=16)

                if ich == 12:
                    for j in range(2):
                        def gp_dwp(h, j=j, tg=tg):
                            src = ap_of(q8g_a, (j // 2) * A3 + 2 * SH
                                        + (j % 2) * 128 * II, [[II, 128], [1, II]])
                            h.dma_start(out=BIG.ap()[:, 49152 + j * 6144:
                                                     49152 + (j + 1) * 6144],
                                        in_=src).then_inc(sems[f"dw{tg}{j}"], 16)
                            return None
                        ck_dwp[j] = sch.opn("gp", gp_dwp,
                                            waits=(list(dwp_waits) if j == 0 else []),
                                            bump=f"dw{tg}{j}", amt=16)

                parG, parU = (ich % 2), 2 + (ich % 2)
                wiG, wiU = (ich % 2) * 2, (ich % 2) * 2 + 1
                waits = [(f"ws{wiG}", ck_wg[(ich, 0)]), (f"ws{wiU}", ck_wg[(ich, 1)])]
                if ich == 0:
                    waits.extend(te0_waits)
                if ich >= 2:
                    waits.append(("ve", ck_ev[(tag, "gtf", ich - 2)]))
                    waits.append(("ve", ck_ev[(tag, "fuse", ich - 2)]))

                def te_gu(h, ich=ich, hb=hb, parG=parG, parU=parU):
                    wiG, wiU = (ich % 2) * 2, (ich % 2) * 2 + 1
                    r = None
                    for dch in range(NCH):
                        h.matmul(pb[parG][:], ws[wiG].ap()[:, dch * 128:(dch + 1) * 128],
                                 xnT[:, dch * TOK + hb * 512: dch * TOK + hb * 512 + 512],
                                 start=(dch == 0), stop=(dch == NCH - 1))
                    for dch in range(NCH):
                        r = h.matmul(pb[parU][:], ws[wiU].ap()[:, dch * 128:(dch + 1) * 128],
                                     xnT[:, dch * TOK + hb * 512: dch * TOK + hb * 512 + 512],
                                     start=(dch == 0), stop=(dch == NCH - 1))
                    return r
                ck_teg = W_("te", te_gu, waits=waits, inc="te")
                ck_ev[(tag, "teg", ich)] = ck_teg



                gb = ich % 2

                def ve_g(h, ich=ich, parG=parG, gb=gb):
                    return h.tensor_scalar(out=gtf[gb][:], in0=pb[parG][:],
                                           scalar1=scales[:, ich:ich + 1], scalar2=None,
                                           op0=ALU.mult)
                wv_g = [("te", ck_teg)]
                if ich >= 1:
                    wv_g.append(("se", ck_ev[(tag, "sil", ich - 1)]))
                ck_ev[(tag, "gtf", ich)] = W_("ve", ve_g, waits=wv_g, inc="ve")

                ws_s = [("ve", ck_ev[(tag, "gtf", ich)])]
                if ich >= 1:
                    ws_s.append(("ve", ck_ev[(tag, "fuse", ich - 1)]))
                ck_sil = W_("se", (lambda gb=gb: lambda h: h.activation(
                    out=gsil[gb][:], in_=gtf[gb][:], func=AF.Silu))(),
                    waits=ws_s, inc="se")
                ck_ev[(tag, "sil", ich)] = ck_sil

                import os as _os
                _fv = _os.environ.get("FUSEVAR", "")

                def ve_fuse(h, ich=ich, parU=parU, gb=gb, _fv=_fv):
                    if _fv == "psu":
                        return h.tensor_scalar(out=fuse_ap(ich), in0=pb[parU][:],
                                               scalar1=scales[:, 48 + ich:48 + ich + 1],
                                               scalar2=None, op0=ALU.mult)
                    if _fv == "gsil":
                        return h.tensor_scalar(out=fuse_ap(ich), in0=gsil[gb][:],
                                               scalar1=1.0, scalar2=None, op0=ALU.mult)
                    return h.scalar_tensor_tensor(out=fuse_ap(ich), in0=pb[parU][:],
                                                  scalar=scales[:, 48 + ich:48 + ich + 1],
                                                  in1=gsil[gb][:], op0=ALU.mult, op1=ALU.mult)
                fwaits = [("se", ck_sil)]
                if ich == 32:
                    fwaits.extend(fuse32_waits)
                ck_ev[(tag, "fuse", ich)] = W_("ve", ve_fuse, waits=fwaits, inc="ve")

            ck_ag = None
            for dm in range(NCH):
                wi = dm % 2
                if dm == 2:
                    ck_ag = dm_hook()
                if dm >= 2:
                    if dm == 2:
                        waits = [("te", ck_ev[(tag, "teg", 46)])]
                    elif dm == 3:
                        waits = [("te", ck_ev[(tag, "teg", 47)])]
                    else:
                        waits = [("te", ck_ev[(tag, "dte", dm - 2)])]

                    def gp_wd(h, dm=dm, wi=wi):
                        src = ap_of(q8g_a, (dm // 2) * A3 + 2 * SH
                                    + (dm % 2) * 128 * II, [[II, 128], [1, II]])
                        h.dma_start(out=ws[wi].ap()[:, 0:6144],
                                    in_=src).then_inc(sems[f"ws{wi}"], 16)
                        return None
                    ck_wd = sch.opn("gp", gp_wd, waits=waits, bump=f"ws{wi}", amt=16)

                par = dm % 2
                if dm < 2:
                    waits = [(f"dw{tg}{dm}", ck_dwp[dm]),
                             ("ve", ck_ev[(tag, "fuse", ICH - 1)])]
                else:
                    waits = [(f"ws{wi}", ck_wd)]
                    waits.append(("ve", ck_ev[(tag, "down", dm - 2)]))

                def te_d(h, dm=dm, par=par, wi=wi):
                    r = None
                    for ick in range(ICH):
                        if dm < 2:
                            stat = BIG.ap()[:, 49152 + dm * 6144 + ick * 128:
                                            49152 + dm * 6144 + (ick + 1) * 128]
                        else:
                            stat = ws[wi].ap()[:, ick * 128:(ick + 1) * 128]
                        r = h.matmul(pb[par][:], stat, fuse_ap(ick),
                                     start=(ick == 0), stop=(ick == ICH - 1))
                    return r
                ck_ev[(tag, "dte", dm)] = W_("te", te_d, waits=waits, inc="te")

                def ve_d(h, dm=dm, par=par):
                    return h.tensor_scalar(out=out_slice(dm), in0=pb[par][:],
                                           scalar1=scales[:, 96 + dm:96 + dm + 1],
                                           scalar2=None, op0=ALU.mult)
                ck_ev[(tag, "down", dm)] = W_("ve", ve_d,
                                              waits=[("te", ck_ev[(tag, "dte", dm)])],
                                              inc="ve")
            return ck_ev[(tag, "down", NCH - 1)]

        ck_ffnA = build_ffn(
            0, lambda dm: ffnA[:, dm * 512:(dm + 1) * 512], "fA",
            load_waits={(0, 0): [("vs8", ck_vst)], (0, 1): [("vs8", ck_vst)],
                        (1, 0): [("vs8", ck_vst)], (1, 1): [("vs8", ck_vst)]},
            te0_waits=[("ve", ck_ev[("v", 31)])],
            fuse32_waits=[("vs8", ck_vst)],
            dwp_waits=[("vs8", ck_vst)],
            dm_hook=ag1_hook)
        ck_ag1 = sch.cnt["cc"]
        if "ffnA" in dbg:
            W_("sy", lambda h: h.dma_start(out=dbg["ffnA"].ap(), in_=ffnA[:]),
               waits=[("ve", ck_ffnA)], inc="dbg", inc_amt=16)
        ck_dbg_fuse = None
        if "fuse" in dbg:
            ck_dbg_fuse = W_("sy", lambda h: h.dma_start(out=dbg["fuse"].ap(),
                                                         in_=BIGb[:, 0:ICH * 512]),
                             waits=[("ve", ck_ev[("fA", "fuse", ICH - 1)])], inc="dbg", inc_amt=16)


        if STOP == "FA":
            return _finish(nc, es, sch, sems, dbg, xnT=xnT, qtT=qtT, ffnA=ffnA,
                           kvg=kvg, afg=afg, W_=W_, ck={})

        kh_ap = lambda hl: BIGb[:, hl * 16384: hl * 16384 + 8192]
        vh_ap = lambda hl: BIGb[:, hl * 16384 + 8192: hl * 16384 + 16384]

        ck_kv = {}
        for hl in range(2):
            waits = [("cc", ck_ag1), ("ofs", ck_ofs)]
            if hl == 0:
                waits.append(("te", ck_ev[("fA", "dte", 15)]))
                if "fuse" in dbg:
                    waits.append(("dbg", ck_dbg_fuse))

            def gp_kv(h, hl=hl):
                rk = dynreg(h)
                h.reg_load(rk, ofs[0:1, hl:hl + 1])
                src = bass.AP(kvg, rk,
                              [[TOK, 128], [2 * D * TOK, NCORES], [1, TOK]])
                h.dma_start(out=kh_ap(hl), in_=src).then_inc(sems[f"kv{hl}"], 16)
                for r in range(NCORES):
                    h.reg_load(rk, ofs[0:1, 4 + hl * 8 + r:5 + hl * 8 + r])
                    src = bass.AP(kvg, rk, [[D, 128], [128 * D, 8], [1, 128]])
                    h.dma_start(out=vh_ap(hl)[:, r * 1024:(r + 1) * 1024],
                                in_=src).then_inc(sems[f"kv{hl}"], 16)
                return None
            ck_kv[hl] = sch.opn("gp", gp_kv, waits=waits, bump=f"kv{hl}", amt=16 * 9)

        psL = {0: pb[6], 1: pb[7]}
        ck_qs = {}
        ck_exp = {}
        ck_lmm = {}
        ck_oev = {}
        last_s_te = {}
        for C in range(NCH):
            nst = 4 * (C + 1)
            for hh in range(2):
                waits = [(f"kv{hh}", ck_kv[hh])]
                if C == 0:
                    waits.append(("ve", ck_ffnA))
                else:
                    waits.append(("te", last_s_te[hh]))

                def ve_qs(h, hh=hh, C=C):
                    srcap = ap_of(qtT, hh * 512 + 32 * C,
                                  [[NCH * TOK, 128], [1, 32], [TOK, 16]])
                    return h.tensor_copy(qslot[hh][:], srcap)
                ck_qs[hh] = W_("ve", ve_qs, waits=waits, inc="ve")

            for stg in range(nst + 1):
                if stg < nst:
                    for hh in range(2):
                        st = stg
                        par_s = st % 2
                        waits = [("ve", ck_qs[hh])]
                        if st >= 2:
                            waits.append(("se", ck_exp[(hh, st - 2)]))
                        elif C >= 1:
                            waits.append(("se", ck_exp[(hh, "last")]))
                        elif st < 2:
                            waits.append(("ve", ck_ffnA))

                        def te_s(h, hh=hh, st=st, par_s=par_s):
                            return h.matmul(pb[2 * hh + par_s][:],
                                            kh_ap(hh)[:, st * 128:(st + 1) * 128],
                                            qslot[hh][:], start=True, stop=True)
                        ck_s = W_("te", te_s, waits=waits, inc="te")
                        if st == nst - 1:
                            last_s_te[hh] = ck_s

                        w_e = [("te", ck_s)]
                        if st >= 2:
                            w_e.append(("te", ck_lmm[(hh, st - 2)]))
                        elif C >= 1:
                            w_e.append(("te", ck_lmm[(hh, "last")]))

                        def se_exp(h, hh=hh, pt_b=2 * hh + st % 2, par_s=par_s):
                            return h.activation(out=ptile[pt_b][:],
                                                in_=pb[2 * hh + par_s][:],
                                                func=AF.Exp, bias=nf32(CEXP),
                                                scale=nf32(QSCALE))
                        ck_exp[(hh, st)] = W_("se", se_exp, waits=w_e, inc="se")

                        if st >= 4 * C:
                            def ve_mask(h, st=st, C=C, pt_b=2 * hh + st % 2):
                                mi = st - 4 * C
                                return h.tensor_tensor(out=ptile[pt_b][:],
                                                       in0=ptile[pt_b][:],
                                                       in1=maskb[:, mi * 512:(mi + 1) * 512],
                                                       op=ALU.mult)
                            ck_exp[(hh, st, "m")] = W_("ve", ve_mask,
                                                       waits=[("se", ck_exp[(hh, st)])],
                                                       inc="ve")
                if stg >= 1:
                    for hh in range(2):
                        st = stg - 1
                        pt_b = 2 * hh + st % 2
                        if st >= 4 * C:
                            av_waits = [("ve", ck_exp[(hh, st, "m")])]
                        else:
                            av_waits = [("se", ck_exp[(hh, st)])]
                        if st == 0 and C >= 1:
                            av_waits.append(("ve", ck_oev[hh]))

                        def te_av(h, hh=hh, st=st, pt_b=pt_b,
                                  first=(st == 0), last=(st == nst - 1)):
                            h.matmul(pb[4 + hh][:], vh_ap(hh)[:, st * 128:(st + 1) * 128],
                                     ptile[pt_b][:], start=first, stop=last)
                            return h.matmul(psL[hh][:], onesb, ptile[pt_b][:],
                                            start=first, stop=last)
                        ck_lmm[(hh, st)] = W_("te", te_av, waits=av_waits, inc="te")
            ck_exp[(0, "last")] = ck_exp[(0, nst - 1)]
            ck_exp[(1, "last")] = ck_exp[(1, nst - 1)]
            ck_lmm[(0, "last")] = ck_lmm[(0, nst - 1)]
            ck_lmm[(1, "last")] = ck_lmm[(1, nst - 1)]

            for hh in range(2):
                def ve_oev(h, base, hh=hh):
                    h.tensor_scalar(out=lrep[:], in0=psL[hh][:], scalar1=1.0,
                                    scalar2=None, op0=ALU.mult).then_inc(sems["ve"], 1)
                    h.wait_ge(sems["ve"], base + 1)
                    h.reciprocal(linv[:], lrep[:]).then_inc(sems["ve"], 1)
                    h.wait_ge(sems["ve"], base + 2)
                    h.scalar_tensor_tensor(out=otf[:], in0=pb[4 + hh][:], scalar=1.0,
                                           in1=linv[:], op0=ALU.mult, op1=ALU.mult
                                           ).then_inc(sems["ve"], 1)
                    h.wait_ge(sems["ve"], base + 3)
                    h.tensor_scalar(out=ostg[hh][:], in0=otf[:],
                                    scalar1=bvo[:, hh:hh + 1], scalar2=None,
                                    op0=ALU.add).then_inc(sems["ve"], 1)
                w_o = [("te", ck_lmm[(hh, "last")])]
                if C >= 1:
                    w_o.append((f"ost{hh}", ck_oev[(hh, "st")]))
                ck_oev[hh] = sch.opsw("ve", ve_oev, waits=w_o, n_incs=4)

                def sy_ost(h, hh=hh, C=C):
                    dst = ap_of(afb, hh * 128 * T + C * 512, [[T, 128], [1, 512]])
                    return h.dma_start(out=dst, in_=ostg[hh][:])
                ck_oev[(hh, "st")] = W_("sy", sy_ost, waits=[("ve", ck_oev[hh])],
                                        inc=f"ost{hh}", inc_amt=16)

        last_exp = max(ck_exp[(0, "last")], ck_exp[(1, "last")])
        ost_fin = (sch.cnt["ost0"], sch.cnt["ost1"])

        def ag2_hook():
            return W_("gp", lambda h: h.collective_compute(
                "AllGather", ALU.bypass, replica_groups=[list(range(NCORES))],
                ins=[afb.ap().opt()], outs=[afg.ap().opt()]),
                waits=[("ost0", ost_fin[0]), ("ost1", ost_fin[1])], inc="cc")

        ck_ffnB = build_ffn(
            1, lambda dm: ffnB[:, dm * 512:(dm + 1) * 512], "fB",
            load_waits={(0, 0): [("te", ck_ev[("fA", "dte", 14)])],
                        (0, 1): [("te", ck_ev[("fA", "dte", 15)])],
                        (1, 0): [("te", ck_ev[("fA", "teg", 47)])],
                        (1, 1): []},
            te0_waits=[("ve", ck_oev[1]), ("ve", ck_oev[0]),
                       ("te", max(ck_lmm[(0, "last")], ck_lmm[(1, "last")])),
                       ("se", last_exp)],
            fuse32_waits=(),
            dwp_waits=[("te", ck_lmm[(1, "last")])],
            dm_hook=ag2_hook)
        ck_ag2 = sch.cnt["cc"]

        if STOP == "FB":
            return _finish(nc, es, sch, sems, dbg, xnT=xnT, qtT=qtT, ffnA=ffnA,
                           kvg=kvg, afg=afg, W_=W_, ck={})

        opm_ap = lambda hd: BIGb[:, hd * 1024:(hd + 1) * 1024]

        def gp_opm(h):
            ro = dynreg(h)
            h.reg_load(ro, ofs[0:1, 20:21])
            return h.dma_start(out=BIGb[:, 0:16384],
                               in_=bass.AP(afg, ro,
                                           [[T, 128], [128 * T, NCH], [1, TOK]]))
        ck_opm = W_("gp", gp_opm,
                    waits=[("cc", ck_ag2), ("te", ck_ev[("fB", "dte", 15)])],
                    inc="opm", inc_amt=16)

        ogi = 0
        ck_og = {}
        ck_oge = {}
        for dm in range(NCH):
            wi = dm % 2
            if dm < 2:
                waits = [("te", ck_ev[("fB", "dte", 14 + dm)])]
            else:
                waits = [("te", ck_og[dm - 2])]
            def gp_wo(h, dm=dm, wi=wi):
                base = (dm // 2) * (256 * D) + (dm % 2) * 128 * D
                h.dma_start(out=wsb(wi)[:, 0:2048],
                            in_=ap_of(q8g_w, base, [[D, 128], [1, D]])
                            ).then_inc(sems[f"ws{wi}"], 16)
                return None
            ck_wo = sch.opn("gp", gp_wo, waits=waits, bump=f"ws{wi}", amt=16)
            for half2 in range(2):
                par = ogi % 2
                waits = [(f"ws{wi}", ck_wo), ("opm", ck_opm)]
                if ogi < 2:
                    waits.append(("ve", ck_ffnB))
                if ogi >= 2:
                    waits.append(("ve", ck_oge[ogi - 2]))

                def te_o(h, half2=half2, par=par, wi=wi):
                    r = None
                    for hd in range(NCH):
                        r = h.matmul(pb[par][:], wsb(wi)[:, hd * 128:(hd + 1) * 128],
                                     opm_ap(hd)[:, half2 * 512:(half2 + 1) * 512],
                                     start=(hd == 0), stop=(hd == NCH - 1))
                    return r
                ck_og[dm] = W_("te", te_o, waits=waits, inc="te")

                def ve_op(h, dm=dm, half2=half2, par=par):
                    dst = (ffnA[:, dm * 512:(dm + 1) * 512] if half2 == 0
                           else ffnB[:, dm * 512:(dm + 1) * 512])
                    return h.scalar_tensor_tensor(out=dst, in0=pb[par][:],
                                                  scalar=bvecs[:, 32 + dm:32 + dm + 1],
                                                  in1=dst, op0=ALU.add, op1=ALU.add)
                ck_oge[ogi] = W_("ve", ve_op, waits=[("te", ck_og[dm])], inc="ve")
                ogi += 1
        ck_oproj = ck_oge[ogi - 1]

        fxt_ap = lambda b: BIGf[:, 8192 + b * 2048: 8192 + (b + 1) * 2048]
        fos_ap = lambda b: BIGf[:, 12288 + b * 2048: 12288 + (b + 1) * 2048]

        last_fadd = {0: 0, 1: 0}
        ck_fst = {}
        fti = 0
        for half2 in range(2):
            src_ffn = (lambda ch, t4, half2=half2:
                       (ffnA if half2 == 0 else ffnB)[:, ch * 512 + t4 * 128:
                                                      ch * 512 + t4 * 128 + 128])
            for t4 in range(4):
                b = fti % 2
                waits = [("ve", ck_ffnB)]
                if fti >= 2:
                    waits.append(("ve", ck_fst[("e", fti - 2)]))
                    waits.append((f"fst{b}", ck_fst[fti - 2]))

                def sy_fx(h, half2=half2, t4=t4, b=b):
                    r0 = half2 * 512 + t4 * 128
                    return h.dma_start(out=fxt_ap(b), in_=x_own.ap()[r0:r0 + 128, :])
                ck_fx = W_("sy", sy_fx, waits=waits, inc=f"fx{b}", inc_amt=16)

                for grp in range(4):
                    par = grp % 2
                    waits = [("ve", ck_oproj)]
                    if last_fadd[par]:
                        waits.append(("ve", last_fadd[par]))

                    def te_ftr(h, src_ffn=src_ffn, t4=t4, grp=grp, par=par):
                        r = None
                        for k in range(4):
                            r = h.transpose(pb[par][:, k * 128:(k + 1) * 128],
                                            src_ffn(grp * 4 + k, t4), cf[:])
                        return r
                    ck_tr = W_("te", te_ftr, waits=waits, inc="te")

                    def ve_fadd(h, b=b, grp=grp, par=par):
                        return h.tensor_tensor(out=fos_ap(b)[:, grp * 512:(grp + 1) * 512],
                                               in0=pb[par][:],
                                               in1=fxt_ap(b)[:, grp * 512:(grp + 1) * 512],
                                               op=ALU.add)
                    last_fadd[par] = W_("ve", ve_fadd,
                                        waits=[("te", ck_tr), (f"fx{b}", ck_fx)], inc="ve")
                ck_fst[("e", fti)] = last_fadd[3 % 2]

                def sy_fst(h, half2=half2, t4=t4, b=b):
                    r0 = half2 * 512 + t4 * 128
                    return h.dma_start(out=out_own.ap()[r0:r0 + 128, :], in_=fos_ap(b))
                ck_fst[fti] = W_("sy", sy_fst,
                                 waits=[("ve", last_fadd[0]), ("ve", last_fadd[1])],
                                 inc=f"fst{b}", inc_amt=16)
                fti += 1

        if "xnT" in dbg:
            W_("sy", lambda h: h.dma_start(out=dbg["xnT"].ap(), in_=xnT[:]),
               waits=[("se", ck_xnT)], inc="dbg", inc_amt=16)
        if "kvg" in dbg:
            W_("gp", lambda h: h.dma_start(out=dbg["kvg"].ap(), in_=kvg.ap()),
               waits=[("cc", ck_ag1)], inc="dbg", inc_amt=16)
        if "afg" in dbg:
            W_("gp", lambda h: h.dma_start(out=dbg["afg"].ap(), in_=afg.ap()),
               waits=[("cc", ck_ag2)], inc="dbg", inc_amt=16)

        W_("sy", lambda h: None, waits=[("fst0", sch.cnt["fst0"]),
                                        ("fst1", sch.cnt["fst1"]),
                                        ("sy", sch.cnt["sy"]), ("dbg", sch.cnt["dbg"])])
        W_("gp", lambda h: None, waits=[("gp", sch.cnt["gp"])])

        block = es.enter_context(nc.Block())

        def emit(eng_name, h):
            for waits, (inc, amt), fn in sch.ops[eng_name]:
                for s, v in waits:
                    h.wait_ge(sems[s], v)
                inst = fn(h)
                if inc is not None:
                    assert inst is not None, f"{eng_name} op with inc has no instruction"
                    inst.then_inc(sems[inc], amt)

        @block.sync
        def _(h):
            emit("sy", h)

        @block.gpsimd
        def _(h):
            emit("gp", h)

        @block.tensor
        def _(h):
            emit("te", h)

        @block.vector
        def _(h):
            emit("ve", h)

        @block.scalar
        def _(h):
            emit("se", h)

    return nc


_BF = ml_dtypes.bfloat16


def _consts():
    ident = np.eye(128, dtype=_BF)
    ones = np.ones((128, 128), dtype=_BF)
    masks = np.zeros((128, 4 * 512), dtype=_BF)
    for d in range(4):
        m = (np.arange(512)[None, :] >= (128 * d + np.arange(128))[:, None])
        masks[:, d * 512:(d + 1) * 512] = m.astype(_BF)
    cb = np.concatenate([ident, ones, masks], axis=1)
    cf = np.eye(128, dtype=np.float32)
    return np.ascontiguousarray(cb), np.ascontiguousarray(cf)


def kernel(L0_attn_x, L0_attn_Wq, L0_attn_bq, L0_attn_Wk, L0_attn_bk, L0_attn_Wv,
           L0_attn_bv, L0_attn_Wo, L0_attn_bo, L0_ffn_gate_w, L0_ffn_up_w,
           L0_ffn_down_w, L0_ffn_gate_scale, L0_ffn_up_scale, L0_ffn_down_scale):
    x = np.ascontiguousarray(np.asarray(L0_attn_x, dtype=np.float32).reshape(T, D))
    cbv, cfv = _consts()
    bvecs = np.ascontiguousarray(np.concatenate([
        np.asarray(L0_attn_bq, np.float32).reshape(NCH, 128).T,
        np.asarray(L0_attn_bk, np.float32).reshape(NCH, 128).T,
        np.asarray(L0_attn_bo, np.float32).reshape(NCH, 128).T], axis=1))
    bv_all = np.asarray(L0_attn_bv, np.float32).reshape(NCH, 128).T
    scales = np.ascontiguousarray(np.concatenate([
        np.asarray(L0_ffn_gate_scale, np.float32).reshape(ICH, 128).T,
        np.asarray(L0_ffn_up_scale, np.float32).reshape(ICH, 128).T,
        np.asarray(L0_ffn_down_scale, np.float32).reshape(NCH, 128).T], axis=1))

    def chunked(w, nin, nout):
        a = np.asarray(w, np.float32).reshape(nin, 128, nout, 128)
        return np.ascontiguousarray(a.transpose(2, 1, 0, 3))

    shared = {
        "Wq": chunked(L0_attn_Wq, NCH, NCH),
        "Wk": chunked(L0_attn_Wk, NCH, NCH),
        "Wv": np.ascontiguousarray(np.asarray(L0_attn_Wv, np.float32)),
        "Wo": chunked(L0_attn_Wo, NCH, NCH),
        "gw": chunked(L0_ffn_gate_w, NCH, ICH),
        "uw": chunked(L0_ffn_up_w, NCH, ICH),
        "dw": chunked(L0_ffn_down_w, ICH, NCH),
        "bvecs": bvecs, "scales": scales, "constsb": cbv, "constsf": cfv,
    }

    in_maps = []
    for c in range(NCORES):
        h0, h1 = 2 * c, 2 * c + 1
        offs = np.zeros((1, 24), dtype=np.int32)
        offs[0, 0] = h0 * 128 * TOK
        offs[0, 1] = h1 * 128 * TOK
        for hli, hh in ((0, h0), (1, h1)):
            for r in range(NCORES):
                offs[0, 4 + hli * 8 + r] = r * 2 * D * TOK + D * TOK + hh * 128
        offs[0, 20] = c * TOK
        offs[0, 21] = c * 768 * D
        offs[0, 22] = c * 256 * II
        offs[0, 23] = c * 256 * D
        m = dict(shared)
        m["x_own"] = np.ascontiguousarray(x[c * TOK:(c + 1) * TOK])
        m["bvo"] = np.ascontiguousarray(bv_all[:, [h0, h1]])
        m["offs"] = offs
        m["qgw"] = np.ascontiguousarray(
            shared["gw"].reshape(ICH * 128, D)[c * 768:(c + 1) * 768])
        m["quw"] = np.ascontiguousarray(
            shared["uw"].reshape(ICH * 128, D)[c * 768:(c + 1) * 768])
        m["qdw"] = np.ascontiguousarray(
            shared["dw"].reshape(NCH * 128, II)[c * 256:(c + 1) * 256])
        m["qwo"] = np.ascontiguousarray(
            shared["Wo"].reshape(NCH * 128, D)[c * 256:(c + 1) * 256])
        in_maps.append(m)

    nc = build()
    res = run_bass_kernel_spmd(nc, in_maps, core_ids=list(range(NCORES)))
    out = np.concatenate([r["out_own"] for r in res.results], axis=0)
    if DEBUG:
        kernel._debug = res.results
    return out.reshape(1, T, D)
